# version 01
# baseline (speedup 1.0000x reference)
import numpy as np
from contextlib import ExitStack

import concourse.bass as bass
import concourse.mybir as mybir
from concourse.bass_utils import run_bass_kernel_spmd

F32 = mybir.dt.float32
BF16 = mybir.dt.bfloat16
AF = mybir.ActivationFunctionType
ALU = mybir.AluOpType
AX = mybir.AxisListType

NCORES = 8
B = 2
N = 4096
D = 2048
TPC = 1024
NCH = D // 128


class Prog:
    ENGINES = ("tensor", "vector", "scalar", "gpsimd", "sync")

    SERIAL = False

    def __init__(self, nc, es):
        self.nc = nc
        self.es = es
        self.sems = {e: es.enter_context(nc.semaphore("sem_" + e))
                     for e in self.ENGINES}
        self.cnt = {e: 0 for e in self.ENGINES}
        self.stream = {e: [] for e in self.ENGINES}
        self.hwm = {}
        self.extra_sems = []

    def do(self, e, fn, deps=(), dma=False, inc=True):
        if Prog.SERIAL:
            deps = list(deps) + [(e2, self.cnt[e2]) for e2 in self.ENGINES
                                 if self.cnt[e2] > 0]
        waits = []
        for tok in deps:
            if tok is None:
                continue
            se, v = tok
            if isinstance(se, str):
                key = (e, se)
                if self.hwm.get(key, 0) >= v:
                    continue
                self.hwm[key] = v
                waits.append((self.sems[se], v))
            else:
                key = (e, id(se))
                if self.hwm.get(key, 0) >= v:
                    continue
                self.hwm[key] = v
                waits.append((se, v))
        step = 16 if dma else 1
        tok = None
        incs = []
        if inc:
            self.cnt[e] += step
            tok = (e, self.cnt[e])
            incs = [(self.sems[e], step)]
        self.stream[e].append((waits, fn, incs))
        return tok

    def fence_toks(self):
        return [(e, self.cnt[e]) for e in self.ENGINES if self.cnt[e] > 0]

    def raw_sem(self, name):
        s = self.es.enter_context(self.nc.semaphore(name))
        self.extra_sems.append(s)
        return s

    def emit(self):
        nc = self.nc
        with nc.Block() as block:
            def mk(e):
                def body(eng):
                    for waits, fn, incs in self.stream[e]:
                        for sem, v in waits:
                            eng.wait_ge(sem, v)
                        inst = fn(eng)
                        for sem, n in incs:
                            inst.then_inc(sem, n)
                return body
            block.tensor(mk("tensor"))
            block.vector(mk("vector"))
            block.scalar(mk("scalar"))
            block.gpsimd(mk("gpsimd"))
            block.sync(mk("sync"))


class KB:

    def __init__(self, name):
        self.nc = bass.Bass("TRN2", target_bir_lowering=False)
        self.es = ExitStack()
        self.p = Prog(self.nc, self.es)
        self.sbuf_bytes = 0
        self._uid = 0
        self._flt = self.es.enter_context(
            self.nc.sbuf_tensor("_flush", [1, 8], F32))

    def uid(self):
        self._uid += 1
        return self._uid

    def dram(self, name, shape, dtype=F32, kind="ExternalInput"):
        return self.nc.dram_tensor(name, shape, dtype, kind=kind).ap()

    def dram_internal(self, name, shape, dtype=F32):
        return self.nc.dram_tensor(name, shape, dtype)

    def sb(self, name, shape, dtype=F32):
        t = self.es.enter_context(self.nc.sbuf_tensor(name, shape, dtype))
        esize = 2 if dtype == BF16 else 4
        self.sbuf_bytes += shape[0] * int(np.prod(shape[1:])) * esize
        return t

    def ps(self, name, shape, dtype=F32):
        return self.es.enter_context(self.nc.psum_tensor(name, shape, dtype))

    def dma(self, dst, src, deps=(), q="sync"):
        return self.p.do(q, lambda e: e.dma_start(out=dst, in_=src),
                         deps=deps, dma=True)

    def dma_t(self, dst, src, deps=(), q="sync"):
        return self.p.do(q, lambda e: e.dma_start_transpose(dst, src),
                         deps=deps, dma=True)

    def loadw(self, dst, src, deps=()):
        return self.p.do("gpsimd", lambda e: e.dma_start(out=dst, in_=src),
                         deps=deps, dma=True)

    def mm(self, out, lhsT, rhs, deps=(), start=True, stop=True, inc=True,
           sgc=False):
        return self.p.do(
            "tensor",
            lambda e: e.matmul(out, lhsT, rhs, start=start, stop=stop,
                               skip_group_check=sgc),
            deps=deps, inc=inc)

    def mm_group(self, out, pairs, deps=(), extra=()):
        allp = list(pairs) + list(extra)
        tok = None
        for i, (l, r) in enumerate(allp):
            tok = self.mm(out, l, r,
                          deps=deps if i == 0 else (),
                          start=(i == 0), stop=(i == len(allp) - 1),
                          inc=(i == len(allp) - 1))
        return tok

    def transpose(self, out_psum, in_sbuf, ident, deps=(), inc=True):
        return self.p.do(
            "tensor", lambda e: e.transpose(out_psum, in_sbuf, ident),
            deps=deps, inc=inc)

    def copy(self, dst, src, deps=(), eng="vector"):
        return self.p.do(eng, lambda e: e.tensor_copy(dst, src), deps=deps)

    def tt(self, dst, a, b, op, deps=(), eng="vector"):
        return self.p.do(eng, lambda e: e.tensor_tensor(dst, a, b, op=op),
                         deps=deps)

    def ts(self, dst, a, scal, op, deps=(), eng="vector", accum=None):
        return self.p.do(
            eng,
            lambda e: e.tensor_scalar(dst, a, scal, None, op, accum_out=accum),
            deps=deps)

    def stt(self, dst, in0, scal, in1, op0, op1, deps=(), eng="vector"):
        return self.p.do(
            eng,
            lambda e: e.scalar_tensor_tensor(dst, in0, scal, in1, op0, op1),
            deps=deps)

    def act(self, dst, src, func, deps=(), bias=None, scale=None, accum=None):
        kw = {}
        if bias is not None:
            kw["bias"] = bias
        if scale is not None:
            kw["scale"] = scale
        if accum is not None:
            kw["accum_out"] = accum
        return self.p.do(
            "scalar", lambda e: e.activation(out=dst, in_=src, func=func, **kw),
            deps=deps)

    def reduce(self, dst, src, op, deps=(), negate=False, eng="vector"):
        return self.p.do(
            eng,
            lambda e: e.tensor_reduce(dst, src, axis=AX.X, op=op,
                                      negate=negate),
            deps=deps)

    def recip(self, dst, src, deps=()):
        return self.p.do("vector", lambda e: e.reciprocal(dst, src), deps=deps)

    def memset(self, ap, val, deps=(), eng="vector"):
        return self.p.do(eng, lambda e: e.memset(ap, val), deps=deps)

    def vflush(self, deps):
        return self.p.do("vector", lambda e: e.memset(self._flt[:], 0.0),
                         deps=deps)

    def sflush(self, deps):
        return self.p.do("scalar", lambda e: e.memzero(self._flt[0:1, 0:4]),
                         deps=deps)

    def collective(self, kind, op, groups, ins, outs, deps=()):
        sem = self.p.raw_sem("cc%d" % self.uid())

        def fn(e):
            inst = e.collective_compute(
                kind, op, replica_groups=groups, ins=ins, outs=outs)
            inst.then_inc(sem)
            return inst
        self.p.do("gpsimd", fn, deps=deps, inc=False)
        return (sem, 1)

    def finish(self, out_dma_tokens):
        for tok in out_dma_tokens:
            if tok is None:
                continue
            se, _ = tok
            eng = se if isinstance(se, str) else "sync"
            self.p.do(eng, lambda e: e.nop(), deps=[tok], inc=False)
        self.p.emit()
        self.es.close()
        return self.nc


def _fm(x2d):
    return np.ascontiguousarray(x2d.T)


def _chunk_cols(v):
    return np.ascontiguousarray(v.reshape(-1, 128).T.astype(np.float32))


def _row(v):
    return np.ascontiguousarray(v.reshape(1, -1).astype(np.float32))


def _ident_bf16():
    import ml_dtypes
    return np.eye(128, dtype=ml_dtypes.bfloat16)


def _shards(full):
    out = []
    for c in range(NCORES):
        b, r = divmod(c, 4)
        out.append(np.ascontiguousarray(full[b, r * TPC:(r + 1) * TPC]))
    return out



def _r3(dram_ap, p=128):
    return dram_ap.rearrange("(c p) t -> p c t", p=p)


def _s3(tile_ap, t):
    return tile_ap.rearrange("p (c t) -> p c t", t=t)

def build_a5():
    k = KB("a5")
    nc = k.nc
    x2t = k.dram("x2t", [D, TPC], F32)
    Wq = k.dram("Wq", [D, D], F32)
    Wk = k.dram("Wk", [D, D], F32)
    Wv = k.dram("Wv", [D, D], F32)
    bq_pc = k.dram("bq_pc", [128, NCH], F32)
    bk_row = k.dram("bk_row", [1, D], F32)
    bv_row = k.dram("bv_row", [1, D], F32)
    q_fm = k.dram("q_fm", [D, TPC], BF16, kind="ExternalOutput")
    k_tm = k.dram("k_tm", [TPC, D], BF16, kind="ExternalOutput")
    v_tm = k.dram("v_tm", [TPC, D], BF16, kind="ExternalOutput")

    xb = k.sb("xb", [128, NCH * TPC], BF16)
    qs = k.sb("qs", [128, NCH * TPC], BF16)
    ks = k.sb("ks", [128, 8 * D], BF16)
    vs = k.sb("vs", [128, 8 * D], BF16)
    wblk = [k.sb("wblk%d" % i, [128, NCH * 512], BF16) for i in range(2)]
    bqs = k.sb("bqs", [128, NCH], F32)
    bks = k.sb("bks", [1, D], BF16)
    bvs = k.sb("bvs", [1, D], BF16)
    ones1 = k.sb("ones1", [1, 128], BF16)
    banks = [k.ps("bank%d" % i, [128, 512], F32) for i in range(4)]

    tok_x = k.loadw(_s3(xb[:], TPC), _r3(x2t))
    tok_bq = k.dma(bqs[:], bq_pc[:, :])
    tok_bk = k.loadw(bks[:], bk_row[:, :])
    tok_bv = k.loadw(bvs[:], bv_row[:, :])
    tok_ones = k.memset(ones1[:], 1.0)

    out_toks = []

    def wload(W, blk, buf, deps):
        src = _r3(W[:, blk * 512:(blk + 1) * 512])
        return k.loadw(_s3(wblk[buf][:], 512), src, deps=deps)

    wtoks = [None, None]
    lastuse = [None, None]
    bank_last = [None] * 4

    wtoks[0] = wload(Wq, 0, 0, [])
    seq = []
    for blk in range(4):
        seq.append(("q", Wq, blk))
    for blk in range(4):
        seq.append(("k", Wk, blk))
    for blk in range(4):
        seq.append(("v", Wv, blk))

    for i, (kind, W, blk) in enumerate(seq):
        buf = i % 2
        if i + 1 < len(seq):
            nk, nW, nblk = seq[i + 1]
            wtoks[(i + 1) % 2] = wload(nW, nblk, (i + 1) % 2,
                                       [lastuse[(i + 1) % 2]])
        wt = wtoks[buf]
        wb = wblk[buf]
        if kind == "q":
            for j in range(4):
                fch = blk * 4 + j
                for th in range(2):
                    bank = (j * 2 + th) % 4
                    pairs = [(wb[:, kc * 512 + j * 128: kc * 512 + j * 128 + 128],
                              xb[:, kc * TPC + th * 512: kc * TPC + th * 512 + 512])
                             for kc in range(NCH)]
                    mt = k.mm_group(banks[bank][:], pairs,
                                    deps=[wt, tok_x, bank_last[bank]])
                    et = k.act(qs[:, fch * TPC + th * 512:
                                   fch * TPC + th * 512 + 512],
                               banks[bank][:], AF.Identity,
                               deps=[mt, tok_bq],
                               bias=bqs[:, fch:fch + 1], scale=1.0)
                    bank_last[bank] = et
                    lastuse[buf] = mt
        else:
            dst = ks if kind == "k" else vs
            brow = bks if kind == "k" else bvs
            btok = tok_bk if kind == "k" else tok_bv
            for jt in range(8):
                bank = jt % 4
                pairs = [(xb[:, kc * TPC + jt * 128: kc * TPC + jt * 128 + 128],
                          wb[:, kc * 512:(kc + 1) * 512])
                         for kc in range(NCH)]
                extra = [(ones1[:, :],
                          brow[:, blk * 512:(blk + 1) * 512])]
                mt = k.mm_group(banks[bank][:], pairs, extra=extra,
                                deps=[wt, tok_x, btok, tok_ones,
                                      bank_last[bank]])
                et = k.copy(dst[:, jt * D + blk * 512: jt * D + blk * 512 + 512],
                            banks[bank][:], deps=[mt])
                bank_last[bank] = et
                lastuse[buf] = mt

    vf5 = k.vflush([t for t in bank_last])
    sf5 = k.sflush([t for t in bank_last])
    out_toks.append(k.dma(_r3(q_fm), _s3(qs[:], TPC),
                          deps=[t for t in bank_last] + [vf5, sf5]))
    out_toks.append(k.dma(_r3(k_tm), _s3(ks[:], D),
                          deps=[t for t in bank_last] + [vf5, sf5]))
    out_toks.append(k.dma(_r3(v_tm), _s3(vs[:], D),
                          deps=[t for t in bank_last] + [vf5, sf5]))
    return k.finish(out_toks)


def stage_a5(core, x2_fm_shard, inputs):
    return {
        "x2t": x2_fm_shard,
        "Wq": inputs["L1_attn_Wq"], "Wk": inputs["L1_attn_Wk"],
        "Wv": inputs["L1_attn_Wv"],
        "bq_pc": _chunk_cols(inputs["L1_attn_bq"]),
        "bk_row": _row(inputs["L1_attn_bk"]),
        "bv_row": _row(inputs["L1_attn_bv"]),
    }


def build_a6():
    k = KB("a6")
    q_fm = k.dram("q_fm", [D, TPC], BF16)
    ktm = k.dram("ktm", [N, D], BF16)
    vtm = k.dram("vtm", [N, D], BF16)
    x2fm = k.dram("x2fm", [D, TPC], F32)
    Wo = k.dram("Wo", [D, D], F32)
    bo_pc = k.dram("bo_pc", [128, NCH], F32)
    x3fm = k.dram("x3fm", [D, TPC], F32, kind="ExternalOutput")

    qsb = [k.sb("qsb%d" % i, [128, TPC], BF16) for i in range(2)]
    kf = [k.sb("kf%d" % i, [128, N], BF16) for i in range(2)]
    vta = [k.sb("vta%d" % i, [128, 32 * 130], BF16) for i in range(2)]
    pt = [k.sb("pt%d" % i, [128, 1024], BF16) for i in range(2)]
    of = k.sb("of", [128, NCH * TPC], BF16)
    pre = k.sb("pre", [128, NCH * TPC], F32)
    wblk = [k.sb("wblk%d" % i, [128, NCH * 512], BF16) for i in range(2)]
    x2ch = [k.sb("x2ch%d" % i, [128, TPC], F32) for i in range(2)]
    bos = k.sb("bos", [128, NCH], F32)
    ones1 = k.sb("ones1", [1, 128], BF16)
    onescol = k.sb("onescol", [128, 1], BF16)
    linv = k.sb("linv", [1, 512], F32)
    linvb = k.sb("linvb", [1, 512], BF16)
    lbc = k.sb("lbc", [128, 512], F32)
    sqs = k.sb("sqs", [128, 512], BF16)
    preb = k.sb("preb", [128, 512], BF16)
    rowm = k.sb("rowm", [1, 512], F32)
    rowa = k.sb("rowa", [1, 512], F32)
    rowv = k.sb("rowv", [1, 512], F32)
    rowab = k.sb("rowab", [1, 512], BF16)
    rowbb = k.sb("rowbb", [1, 512], BF16)
    tmp = k.sb("tmp", [128, 512], F32)

    st4 = k.ps("st4", [128, 2048], F32)
    oA = k.ps("oA", [128, 512], F32)
    oB = k.ps("oB", [128, 512], F32)
    bc = k.ps("bc", [128, 512], F32)

    tok_ones = k.memset(ones1[:], 1.0)
    tok_onesc = k.memset(onescol[:], 1.0)
    tok_bo = k.dma(bos[:], bo_pc[:, :])
    epst = k.sb("epst", [1, 1], F32)
    tok_eps = k.memset(epst[:], 1e-5)
    vset = [k.memset(vta[i][:], 1.0) for i in range(2)]

    kf_tok = [None, None]
    q_tok = [None, None]
    vta_tok = [None, None]
    kf_use = [None, None]
    q_use = [None, None]
    vta_use = [None, None]

    def load_pair(ch, buf):
        c0 = ch * 128
        t1 = k.dma_t(kf[buf][:], ktm[:, c0:c0 + 128], deps=[kf_use[buf]])
        t2 = k.dma(qsb[buf][:], q_fm[c0:c0 + 128, :], deps=[q_use[buf]])
        dst = vta[buf][:].rearrange("p (j c) -> p j c", c=130)
        t3a = k.dma(dst[:, :, 0:64],
                    vtm[:, ch * 128:ch * 128 + 64].rearrange(
                        "(j p) f -> p j f", p=128),
                    deps=[vta_use[buf], vset[buf]], q="gpsimd")
        t3b = k.dma(dst[:, :, 65:129],
                    vtm[:, ch * 128 + 64:ch * 128 + 128].rearrange(
                        "(j p) f -> p j f", p=128),
                    deps=[vta_use[buf], vset[buf]], q="gpsimd")
        return t1, t2, (t3a, t3b)

    kf_tok[0], q_tok[0], vta_tok[0] = load_pair(0, 0)

    exp_tok = [None, None]
    oA_last = None
    oB_last = None
    ep_done = []

    for ch in range(16):
        buf = ch % 2
        if ch + 1 < 16:
            nb = (ch + 1) % 2
            kf_tok[nb], q_tok[nb], vta_tok[nb] = load_pair(ch + 1, nb)
        vA = vta[buf][:].rearrange("p (j c) -> p j c", c=130)
        for qm in range(2):
            pv_a = None
            pv_b = None
            for jt in range(32):
                half = jt % 2
                scol = half * 1024
                dep0 = [kf_tok[buf], q_tok[buf], exp_tok[half],
                        oA_last if jt == 0 else None,
                        oB_last if jt == 0 else None]
                mtA = k.mm(st4[:, scol:scol + 512],
                           kf[buf][0:64, jt * 128:(jt + 1) * 128],
                           qsb[buf][0:64, qm * 512:(qm + 1) * 512],
                           deps=dep0, inc=False)
                mtB = k.mm(st4[:, scol + 512:scol + 1024],
                           kf[buf][64:128, jt * 128:(jt + 1) * 128],
                           qsb[buf][64:128, qm * 512:(qm + 1) * 512],
                           deps=[])
                e = k.act(pt[half][:], st4[:, scol:scol + 1024], AF.Exp,
                          deps=[mtB], scale=float(1.0 / np.sqrt(32.0)))
                exp_tok[half] = e
                pv_a = k.mm(oA[0:65, :], vA[:, jt, 0:65], pt[half][:, 0:512],
                            deps=[e, vta_tok[buf][0], vta_tok[buf][1],
                                  oA_last if jt == 0 else None],
                            start=(jt == 0), stop=(jt == 31),
                            inc=(jt == 31))
                pv_b = k.mm(oB[0:65, :], vA[:, jt, 65:130], pt[half][:, 512:1024],
                            deps=[oB_last if jt == 0 else None],
                            start=(jt == 0), stop=(jt == 31),
                            inc=(jt == 31))
            for s, (bank, pv) in enumerate(((oA, pv_a), (oB, pv_b))):
                t1 = k.recip(linv[:], bank[64:65, :], deps=[pv, ep_done[-1] if ep_done else None])
                t2 = k.copy(linvb[:], linv[:], deps=[t1])
                t3 = k.mm(bc[0:64, :], ones1[:, 0:64], linvb[:], deps=[t2, tok_ones])
                t4 = k.copy(lbc[0:64, :], bc[0:64, :], deps=[t3])
                t5 = k.tt(of[64 * s:64 * s + 64,
                             ch * TPC + qm * 512: ch * TPC + qm * 512 + 512],
                          bank[0:64, :], lbc[0:64, :], ALU.mult, deps=[t4])
                ep_done.append(t5)
                if s == 0:
                    oA_last = t5
                else:
                    oB_last = t5
        kf_use[buf] = exp_tok[0]
        q_use[buf] = exp_tok[1] or exp_tok[0]
        vta_use[buf] = ep_done[-1]

    wtoks = [None, None]
    lastw = [None, None]
    x2tok = [None, None]
    x2use = [None, None]

    def wload(blk, buf):
        src = _r3(Wo[:, blk * 512:(blk + 1) * 512])
        return k.loadw(_s3(wblk[buf][:], 512), src, deps=[lastw[buf]])

    wtoks[0] = wload(0, 0)
    bankR = {0: None, 1: None}
    pre_toks = []
    for blk in range(4):
        buf = blk % 2
        if blk + 1 < 4:
            wtoks[(blk + 1) % 2] = wload(blk + 1, (blk + 1) % 2)
        for j in range(4):
            fch = blk * 4 + j
            xb2 = j % 2
            xt = k.dma(x2ch[xb2][:], x2fm[fch * 128:(fch + 1) * 128, :],
                       deps=[x2use[xb2]])
            for th in range(2):
                bank = st4[:, (j % 2) * 512:(j % 2) * 512 + 512]
                pairs = [(wblk[buf][:, kc * 512 + j * 128: kc * 512 + j * 128 + 128],
                          of[:, kc * TPC + th * 512: kc * TPC + th * 512 + 512])
                         for kc in range(NCH)]
                mt = k.mm_group(bank, pairs,
                                deps=[wtoks[buf], exp_tok[0], exp_tok[1],
                                      ep_done[-1], bankR[j % 2]])
                lastw[buf] = mt
                et = k.stt(pre[:, fch * TPC + th * 512: fch * TPC + th * 512 + 512],
                           bank, bos[:, fch:fch + 1], x2ch[xb2][:, th * 512:th * 512 + 512],
                           ALU.add, ALU.add, deps=[mt, xt, tok_bo])
                bankR[j % 2] = et
                x2use[xb2] = et
                pre_toks.append(et)

    out_toks = []
    t_s1 = None
    t_s2 = None
    for th in range(2):
        s1 = st4[0:1, 1024:1536]
        s2 = st4[0:1, 1536:2048]
        sq_tok = None
        pb_tok = None
        for kc in range(NCH):
            seg = pre[:, kc * TPC + th * 512: kc * TPC + th * 512 + 512]
            pb_tok = k.copy(preb[:], seg, deps=[pre_toks[-1], t_s1])
            sq_tok = k.act(sqs[:], seg, AF.Square, deps=[pre_toks[-1], t_s2])
            t_s1 = k.mm(s1, onescol[:], preb[:],
                        deps=[pb_tok, tok_onesc], start=(kc == 0),
                        stop=(kc == NCH - 1), inc=True)
            t_s2 = k.mm(s2, onescol[:], sqs[:],
                        deps=[sq_tok], start=(kc == 0),
                        stop=(kc == NCH - 1), inc=True)
        t_m = k.ts(rowm[:], s1, 1.0 / D, ALU.mult, deps=[t_s1])
        t_v = k.ts(rowv[:], s2, 1.0 / D, ALU.mult, deps=[t_s2])
        t_m2 = k.tt(tmp[0:1, :], rowm[:], rowm[:], ALU.mult, deps=[t_m])
        t_v2 = k.tt(rowv[:], rowv[:], tmp[0:1, :], ALU.subtract,
                    deps=[t_v, t_m2])
        t_sd = k.act(tmp[0:1, :], rowv[:], AF.Sqrt, deps=[t_v2, tok_eps],
                     bias=epst[0:1, 0:1], scale=1.0)
        t_r = k.recip(rowa[:], tmp[0:1, :], deps=[t_sd])
        t_ab = k.copy(rowab[:], rowa[:], deps=[t_r])
        t_bb = k.tt(tmp[0:1, :], rowm[:], rowa[:], ALU.mult, deps=[t_r, t_ab])
        t_bn = k.ts(tmp[0:1, :], tmp[0:1, :], -1.0, ALU.mult, deps=[t_bb])
        t_bbb = k.copy(rowbb[:], tmp[0:1, :], deps=[t_bn])
        tba = k.mm(bc[:, :], ones1[:, :], rowab[:], deps=[t_ab, tok_ones])
        ebA = k.copy(lbc[:, :], bc[:, :], deps=[tba])
        tbb = k.mm(bc[:, :], ones1[:, :], rowbb[:], deps=[t_bbb, ebA])
        tm2 = None
        tm2s = []
        for kc in range(NCH):
            seg = pre[:, kc * TPC + th * 512: kc * TPC + th * 512 + 512]
            tm1 = k.tt(tmp[:, :], seg, lbc[:, :], ALU.mult,
                       deps=[tba, ebA, t_s1, t_s2, sq_tok, pb_tok, tm2])
            tm2 = k.tt(seg, tmp[:, :], bc[:, :], ALU.add, deps=[tm1, tbb])
            tm2s.append(tm2)
        vf = k.vflush([tm2s[-1]])
        for kc in range(NCH):
            seg = pre[:, kc * TPC + th * 512: kc * TPC + th * 512 + 512]
            ot = k.dma(x3fm[kc * 128:(kc + 1) * 128,
                            th * 512:th * 512 + 512],
                       seg, deps=[tm2s[kc], vf])
            out_toks.append(ot)
    return k.finish(out_toks)


def stage_a6(core, q_fm, ktm_full, vtm_full, x2_fm_shard, inputs):
    return {
        "q_fm": q_fm, "ktm": ktm_full, "vtm": vtm_full, "x2fm": x2_fm_shard,
        "Wo": inputs["L1_attn_Wo"],
        "bo_pc": _chunk_cols(inputs["L1_attn_bo"]),
    }


DEBUG_A7 = False


def build_a7():
    k = KB("a7")
    x3fm = k.dram("x3fm", [D, TPC], F32)
    gwT = k.dram("gwT", [D, 8], F32)
    ebias_bc = k.dram("ebias_bc", [128, 8], F32)
    w1t = k.dram("w1t", [8, D, 1024], F32)
    w3t = k.dram("w3t", [8, D, 1024], F32)
    w2t = k.dram("w2t", [8, 1024, D], F32)
    sw1t = k.dram("sw1t", [D, 1024], F32)
    sw3t = k.dram("sw3t", [D, 1024], F32)
    sw2t = k.dram("sw2t", [1024, D], F32)
    identf = k.dram("identf", [128, 128], F32)
    yout = k.dram("yout", [TPC, D], F32, kind="ExternalOutput")
    wscr = k.dram_internal("wscr", [8, TPC], F32)
    wdbg = k.dram("wdbg", [8, TPC], F32, kind="ExternalOutput") if DEBUG_A7 else None

    x3b = k.sb("x3b", [128, NCH * TPC], BF16)
    gws = k.sb("gws", [128, NCH * 8], F32)
    biasb = k.sb("biasb", [128, 8], F32)
    idf = k.sb("idf", [128, 128], F32)
    ones1 = k.sb("ones1", [1, 128], BF16)
    onescol = k.sb("onescol", [128, 1], BF16)
    epst = k.sb("epst", [1, 1], F32)
    pr = k.sb("pr", [128, 8], F32)
    lfs = k.sb("lfs", [8, 1024], F32)
    spb = k.sb("spb", [128, 8], F32)
    m1n = k.sb("m1n", [128, 1], F32)
    spm = k.sb("spm", [128, 8], F32)
    mask1 = k.sb("mask1", [128, 8], F32)
    m2 = k.sb("m2", [128, 1], F32)
    selm = k.sb("selm", [128, 8], F32)
    wtt = k.sb("wtt", [128, 8], F32)
    wrow = k.sb("wrow", [1, 512], F32)
    wrowb = k.sb("wrowb", [1, 512], BF16)
    wbc = k.sb("wbc", [128, TPC], BF16)
    rin = k.sb("rin", [128, NCH * 512], BF16)
    gs = k.sb("gs", [128, 8 * 512], BF16)
    hb = k.sb("hb", [128, 8 * 512], BF16)
    wpool = [k.sb("wpool%d" % i, [128, 16 * 512], BF16) for i in range(3)]
    acc = k.sb("acc", [128, NCH * 512], F32)
    x3ch = k.sb("x3ch", [128, TPC], F32)
    ytm = k.sb("ytm", [128, 2048], F32)
    sqs = k.sb("sqs", [128, 512], BF16)
    preb = k.sb("preb", [128, 512], BF16)
    rowm = k.sb("rowm", [1, 512], F32)
    rowa = k.sb("rowa", [1, 512], F32)
    rowv = k.sb("rowv", [1, 512], F32)
    rowab = k.sb("rowab", [1, 512], BF16)
    rowbb = k.sb("rowbb", [1, 512], BF16)
    tmp = k.sb("tmp", [128, 512], F32)
    lbc = k.sb("lbc", [128, 512], F32)

    st4 = k.ps("st4", [128, 2048], F32)
    bc = k.ps("bc", [128, 512], F32)
    rt = k.ps("rt", [128, 64], F32)
    trp = k.ps("trp", [128, 128], F32)

    tok_ones = k.memset(ones1[:], 1.0)
    tok_onesc = k.memset(onescol[:], 1.0)
    tok_eps = k.memset(epst[:], 1e-5)
    tok_x3b = k.loadw(_s3(x3b[:], TPC), _r3(x3fm))
    tok_gws = k.dma(gws[:].rearrange("p (c e) -> p c e", e=8),
                    _r3(gwT))
    tok_bias = k.dma(biasb[:], ebias_bc[:, :])
    tok_idf = k.dma(idf[:], identf[:, :])

    wt_toks = []
    tr_last = None
    xck_prev = None
    mt_l = [None, None]
    for kc in range(NCH):
        xck = k.dma(x3ch[:], x3fm[kc * 128:(kc + 1) * 128, :],
                    deps=[xck_prev])
        for half in range(2):
            mt_l[half] = k.mm(st4[0:8, half * 512:(half + 1) * 512],
                              gws[:, kc * 8:(kc + 1) * 8],
                              x3ch[:, half * 512:(half + 1) * 512],
                              deps=[xck, tok_gws] if half == 0 else [],
                              start=(kc == 0), stop=(kc == NCH - 1),
                              inc=True)
        xck_prev = mt_l[1]
    ev_l = k.copy(lfs[0:8, :], st4[0:8, 0:1024], deps=[mt_l[0], mt_l[1]])
    t8_prev = None
    for jt in range(8):
        tt_l = k.transpose(trp[:, 0:8], lfs[0:8, jt * 128:(jt + 1) * 128],
                           idf[0:8, 0:8], deps=[ev_l, tok_idf, tr_last])
        t0 = k.act(pr[:], trp[:, 0:8], AF.Sigmoid, deps=[tt_l, t8_prev])
        t1 = k.tt(spb[:], pr[:], biasb[:], ALU.add, deps=[t0, tok_bias])
        t2 = k.reduce(m1n[:], spb[:], ALU.max, negate=True, deps=[t1])
        t3 = k.ts(spm[:], spb[:], m1n[:, 0:1], ALU.add, deps=[t2, t1])
        t4 = k.ts(mask1[:], spm[:], 0.0, ALU.is_ge, deps=[t3])
        t5 = k.stt(spm[:], mask1[:], 1e30, spb[:], ALU.mult, ALU.subtract,
                   deps=[t4])
        t6 = k.reduce(m2[:], spm[:], ALU.min, negate=True, deps=[t5])
        t7 = k.ts(selm[:], spb[:], m2[:, 0:1], ALU.is_ge, deps=[t6, t5])
        t8 = k.tt(wtt[:], pr[:], selm[:], ALU.mult, deps=[t7, t0])
        t8_prev = t8
        tt_ = k.transpose(trp[0:8, :], wtt[:], idf[:],
                          deps=[t8, tok_idf, tr_last])
        ev = k.copy(tmp[0:8, 0:128], trp[0:8, 0:128], deps=[tt_])
        tr_last = ev
        vfw = k.vflush([ev])
        wt_toks.append(k.dma(wscr.ap()[:, jt * 128:(jt + 1) * 128],
                             tmp[0:8, 0:128], deps=[ev, vfw]))
    wrow_prev = [None]
    dbg_toks = []
    if DEBUG_A7:
        dbg_toks.append(k.dma(wdbg, wscr.ap(), deps=wt_toks))


    out_toks = []
    wload_tok = [None, None, None]
    wuse_tok = [None, None, None]
    slot = [0]

    def wload(src_ap, inner):
        s = slot[0]
        slot[0] = (s + 1) % 3
        t = k.loadw(wpool[s][:].rearrange("p (c t) -> p c t", t=inner),
                    _r3(src_ap), deps=[wuse_tok[s]])
        wload_tok[s] = t
        return s, t

    lnstat_prev = [None, None]
    for th in range(2):
        tof = th * 512
        acc_tok = [None] * NCH
        xck_last = None
        for kc in range(NCH):
            xck = k.dma(x3ch[:], x3fm[kc * 128:(kc + 1) * 128, :],
                        deps=[acc_tok[kc - 1] if kc else
                              (out_toks[-1] if out_toks else None)])
            at = k.copy(acc[:, kc * 512:(kc + 1) * 512],
                        x3ch[:, tof:tof + 512], deps=[xck])
            acc_tok[kc] = at
            xck_last = at

        bankR = [None, None, None, None]
        rinR = None
        wbcR = None
        h_prev = [None] * 8
        for e in range(9):
            shared = (e == 8)
            if not shared:
                t_wr = k.dma(wrow[:], wscr.ap()[e:e + 1, tof:tof + 512],
                             deps=wt_toks + [wrow_prev[0]])
                t_wrb = k.copy(wrowb[:], wrow[:], deps=[t_wr, wrow_prev[0]])
                tb = k.mm(bc[:, :], ones1[:, :], wrowb[0:1, 0:512],
                          deps=[t_wrb, wbcR])
                wrow_prev[0] = tb
                eb = k.copy(wbc[:, 0:512], bc[:, :], deps=[tb, rinR])
                rt_toks = []
                for kc in range(NCH):
                    rt_ = k.tt(rin[:, kc * 512:(kc + 1) * 512],
                               x3b[:, kc * TPC + tof: kc * TPC + tof + 512],
                               wbc[:, 0:512], ALU.mult,
                               deps=[eb, tok_x3b, rinR])
                    rt_toks.append(rt_)
                wbcR = rt_toks[-1]
                rdeps = rt_toks

                def rsl(kc):
                    return rin[:, kc * 512:(kc + 1) * 512]
            else:
                rdeps = [tok_x3b]

                def rsl(kc):
                    return x3b[:, kc * TPC + tof: kc * TPC + tof + 512]

            g_toks = [None] * 8
            for hh in range(2):
                wa = sw1t if shared else w1t[e]
                s1, lt = wload(wa[:, hh * 512:(hh + 1) * 512], 512)
                for hc in range(4):
                    hix = hh * 4 + hc
                    bsl = hc % 2
                    bank = st4[:, bsl * 512: bsl * 512 + 512]
                    pairs = [(wpool[s1][:, kc * 512 + hc * 128:
                                       kc * 512 + hc * 128 + 128],
                              rsl(kc)) for kc in range(NCH)]
                    mt = k.mm_group(bank, pairs,
                                    deps=[lt, bankR[bsl]] + rdeps)
                    gt = k.act(gs[:, hix * 512: hix * 512 + 512],
                               bank, AF.Silu, deps=[mt, h_prev[hix]])
                    g_toks[hix] = gt
                    bankR[bsl] = gt
                    wuse_tok[s1] = mt
                    rinR = mt
            h_toks = [None] * 8
            for hh in range(2):
                wa = sw3t if shared else w3t[e]
                s1, lt = wload(wa[:, hh * 512:(hh + 1) * 512], 512)
                for hc in range(4):
                    hix = hh * 4 + hc
                    bsl = 2 + hc % 2
                    bank = st4[:, bsl * 512: bsl * 512 + 512]
                    pairs = [(wpool[s1][:, kc * 512 + hc * 128:
                                       kc * 512 + hc * 128 + 128],
                              rsl(kc)) for kc in range(NCH)]
                    mt = k.mm_group(bank, pairs,
                                    deps=[lt, bankR[bsl]] + rdeps)
                    ht = k.tt(hb[:, hix * 512: hix * 512 + 512],
                              bank, gs[:, hix * 512: hix * 512 + 512],
                              ALU.mult, deps=[mt, g_toks[hix]])
                    h_toks[hix] = ht
                    bankR[bsl] = ht
                    wuse_tok[s1] = mt
                    rinR = mt
            h_prev = h_toks
            for fh in range(2):
                wa = sw2t if shared else w2t[e]
                s1, lt = wload(wa[:, fh * 1024:(fh + 1) * 1024], 1024)
                for fc in range(8):
                    fch = fh * 8 + fc
                    bsl = fc % 2
                    bank = st4[:, bsl * 512: bsl * 512 + 512]
                    pairs = [(wpool[s1][:, hc * 1024 + fc * 128:
                                       hc * 1024 + fc * 128 + 128],
                              hb[:, hc * 512:(hc + 1) * 512])
                             for hc in range(8)]
                    mt = k.mm_group(bank, pairs,
                                    deps=[lt, bankR[bsl]] + h_toks)
                    at = k.tt(acc[:, fch * 512: fch * 512 + 512],
                              acc[:, fch * 512: fch * 512 + 512],
                              bank, ALU.add,
                              deps=[mt, acc_tok[fch]])
                    acc_tok[fch] = at
                    bankR[bsl] = at
                    wuse_tok[s1] = mt

        s1 = st4[0:1, 1024:1536]
        s2 = st4[0:1, 1536:2048]
        t_s1 = lnstat_prev[0]
        t_s2 = lnstat_prev[1]
        sq_tok = None
        pb_tok = None
        for kc in range(NCH):
            seg = acc[:, kc * 512:(kc + 1) * 512]
            pb_tok = k.copy(preb[:], seg, deps=[acc_tok[kc], t_s1, bankR[2]])
            sq_tok = k.act(sqs[:], seg, AF.Square, deps=[acc_tok[kc], t_s2,
                                                         bankR[3]])
            t_s1 = k.mm(s1, onescol[:], preb[:],
                        deps=[pb_tok, tok_onesc], start=(kc == 0),
                        stop=(kc == NCH - 1), inc=True)
            t_s2 = k.mm(s2, onescol[:], sqs[:],
                        deps=[sq_tok], start=(kc == 0),
                        stop=(kc == NCH - 1), inc=True)
        bankR[2] = t_s1
        bankR[3] = t_s2
        lnstat_prev = [t_s1, t_s2]
        t_m = k.ts(rowm[:], s1, 1.0 / D, ALU.mult, deps=[t_s1])
        t_v = k.ts(rowv[:], s2, 1.0 / D, ALU.mult, deps=[t_s2])
        t_m2 = k.tt(tmp[0:1, :], rowm[:], rowm[:], ALU.mult, deps=[t_m])
        t_v2 = k.tt(rowv[:], rowv[:], tmp[0:1, :], ALU.subtract,
                    deps=[t_v, t_m2])
        t_sd = k.act(tmp[0:1, :], rowv[:], AF.Sqrt, deps=[t_v2, tok_eps],
                     bias=epst[0:1, 0:1], scale=1.0)
        t_r = k.recip(rowa[:], tmp[0:1, :], deps=[t_sd])
        t_ab = k.copy(rowab[:], rowa[:], deps=[t_r])
        t_bb = k.tt(tmp[0:1, :], rowm[:], rowa[:], ALU.mult, deps=[t_r, t_ab])
        t_bn = k.ts(tmp[0:1, :], tmp[0:1, :], -1.0, ALU.mult, deps=[t_bb])
        t_bbb = k.copy(rowbb[:], tmp[0:1, :], deps=[t_bn])
        tba = k.mm(bc[:, :], ones1[:, :], rowab[:], deps=[t_ab, tok_ones,
                                                          wbcR])
        ebA = k.copy(lbc[:, :], bc[:, :], deps=[tba])
        tbb = k.mm(bc[:, :], ones1[:, :], rowbb[:], deps=[t_bbb, ebA])
        tm2 = None
        y_toks = [None] * NCH
        for kc in range(NCH):
            seg = acc[:, kc * 512:(kc + 1) * 512]
            tm1 = k.tt(tmp[:, :], seg, lbc[:, :], ALU.mult,
                       deps=[tba, ebA, t_s1, t_s2, sq_tok, pb_tok, tm2])
            tm2 = k.tt(seg, tmp[:, :], bc[:, :], ALU.add, deps=[tm1, tbb])
            y_toks[kc] = tm2
        wbcR = tbb

        trR = None
        for jt in range(4):
            ev_toks = []
            for kc in range(NCH):
                tt_ = k.transpose(trp[:, :],
                                  acc[:, kc * 512 + jt * 128:
                                      kc * 512 + jt * 128 + 128],
                                  idf[:], deps=[y_toks[kc], tok_idf, trR])
                ev = k.copy(ytm[:, kc * 128:(kc + 1) * 128], trp[:, :],
                            deps=[tt_])
                trR = ev
                ev_toks.append(ev)
            vfy = k.vflush([ev_toks[-1]])
            ot = k.dma(yout[(th * 512 + jt * 128):(th * 512 + jt * 128 + 128), :],
                       ytm[:, :], deps=ev_toks + [vfy])
            out_toks.append(ot)
            trR = ot
    return k.finish(out_toks + dbg_toks)


def stage_a7(core, x3_fm_shard, inputs):
    w1t = np.ascontiguousarray(np.transpose(inputs["L1_ffn_w1"], (0, 2, 1)))
    w3t = np.ascontiguousarray(np.transpose(inputs["L1_ffn_w3"], (0, 2, 1)))
    w2t = np.ascontiguousarray(np.transpose(inputs["L1_ffn_w2"], (0, 2, 1)))
    return {
        "x3fm": x3_fm_shard,
        "gwT": np.ascontiguousarray(inputs["L1_ffn_gate_w"].T),
        "ebias_bc": np.ascontiguousarray(
            np.broadcast_to(inputs["L1_ffn_expert_bias"], (128, 8))).astype(np.float32),
        "w1t": w1t, "w3t": w3t, "w2t": w2t,
        "sw1t": np.ascontiguousarray(inputs["L1_ffn_sw1"].T),
        "sw3t": np.ascontiguousarray(inputs["L1_ffn_sw3"].T),
        "sw2t": np.ascontiguousarray(inputs["L1_ffn_sw2"].T),
        "identf": np.eye(128, dtype=np.float32),
    }




class _BCView:
    def __init__(self, t, off_bf, width_bf):
        self.t, self.off, self.width = t, off_bf, width_bf

    def __getitem__(self, idx):
        if isinstance(idx, tuple):
            p, f = idx
        else:
            p, f = idx, slice(None)
        fs = f.start if f.start is not None else 0
        fe = f.stop if f.stop is not None else self.width
        fs += self.off
        fe += self.off
        assert fs % 2 == 0 and fe % 2 == 0, (fs, fe)
        return self.t[p, fs // 2: fe // 2].bitcast(BF16)


class _ColView:
    def __init__(self, t, off, width):
        self.t, self.off, self.width = t, off, width

    def __getitem__(self, idx):
        if isinstance(idx, tuple):
            p, f = idx
        else:
            p, f = idx, slice(None)
        fs = f.start if f.start is not None else 0
        fe = f.stop if f.stop is not None else self.width
        return self.t[p, self.off + fs: self.off + fe]

DEBUG_A1 = False
SERIAL_PHASE = None


def build_a1():
    k = KB("a1")
    x0fm = k.dram("x0fm", [D, TPC], F32)
    Wq = k.dram("Wq", [D, D], F32)
    Wk = k.dram("Wk", [D, D], F32)
    Wv = k.dram("Wv", [D, D], F32)
    Wo = k.dram("Wo", [D, D], F32)
    Wg1 = k.dram("Wg1", [D, 128], F32)
    Wg2 = k.dram("Wg2", [128, D], F32)
    normw_pc = k.dram("normw_pc", [128, NCH], F32)
    ld_rep = k.dram("ld_rep", [128, 2048], F32)
    ld_rep4 = k.dram("ld_rep4", [128, 64], F32)
    ld_bc = k.dram("ld_bc", [128, 16], F32)
    cposrep = k.dram("cposrep", [128, 2048], F32)
    diffrep = k.dram("diffrep", [128, 2048], F32)
    chainrep = k.dram("chainrep", [128, 64], F32)
    kpos = k.dram("kpos", [128, 1], F32)
    identb = k.dram("identb", [128, 128], BF16)
    x1fm = k.dram("x1fm", [D, TPC], F32, kind="ExternalOutput")
    agin = k.dram_internal("agin", [D, 128], F32)
    agout = k.dram_internal("agout", [4 * D, 128], F32)
    if DEBUG_A1:
        qf_d = k.dram("qf_d", [D, TPC], BF16, kind="ExternalOutput")
        kf_d = k.dram("kf_d", [D, TPC], BF16, kind="ExternalOutput")
        vt_d = k.dram("vt_d", [8 * 128, D], BF16, kind="ExternalOutput")
        ag_d = k.dram("ag_d", [4 * D, 128], F32, kind="ExternalOutput")
        sin_d = k.dram("sin_d", [D, 128], F32, kind="ExternalOutput")
        on_d = k.dram("on_d", [D, TPC], BF16, kind="ExternalOutput")
        g1_d = k.dram("g1_d", [128, TPC], BF16, kind="ExternalOutput")

    qf = k.sb("qf", [128, NCH * TPC], BF16)
    kf = k.sb("kff", [128, NCH * TPC], BF16)
    vt = k.sb("vt", [128, 8 * D], BF16)
    onrm = k.sb("onrm", [128, NCH * TPC], BF16)
    xbf = onrm
    wblk = [k.sb("wblk%d" % i, [128, NCH * 256], BF16) for i in range(2)]
    wg2s = wblk[0]
    g1s = k.sb("g1s", [128, TPC], BF16)
    wg1s = k.sb("wg1s", [128, NCH * 128], BF16)
    prebig = k.sb("prebig", [128, 4096], F32)
    Ast = _ColView(prebig, 0, 2048)
    ldr = Ast
    agt = _ColView(prebig, 2048, 2048)
    scr2 = agt
    Sst = k.sb("Sst", [128, NCH * 128], F32)
    dtall = k.sb("dtall", [128, 2048], BF16)
    qdecb = k.sb("qdecb", [128, 2048], BF16)
    wchain = k.sb("wchain", [128, 64], F32)
    kdec = k.sb("kdec", [128, 16], F32)
    cdv = k.sb("cdv", [128, 16], F32)
    ldbc = k.sb("ldbc", [128, 16], F32)
    kposs = k.sb("kposs", [128, 1], F32)
    Sbf = k.sb("Sbf", [128, 128], BF16)
    kts = k.sb("kts", [128, 128], BF16)
    pts = k.sb("pts", [128, 128], BF16)
    qsc = k.sb("qsc", [128, 128], BF16)
    ogs = k.sb("ogs", [128, 128], F32)
    sgb = k.sb("sgb", [128, 128], BF16)
    sqb = k.sb("sqb", [128, 128], BF16)
    rr = k.sb("rr", [1, 128], F32)
    rrb = k.sb("rrb", [1, 128], BF16)
    nwse = k.sb("nwse", [128, NCH], F32)
    ones1 = k.sb("ones1", [1, 128], BF16)
    onescol = k.sb("onescol", [128, 1], BF16)
    epst = k.sb("epst", [1, 1], F32)
    idb = k.sb("idb", [128, 128], BF16)
    x0ch = [k.sb("x0ch%d" % i, [128, 256], F32) for i in range(2)]
    pre = prebig
    sqs = k.sb("sqs", [128, 256], BF16)
    preb = k.sb("preb", [128, 256], BF16)
    rowm = k.sb("rowm", [1, 256], F32)
    rowa = k.sb("rowa", [1, 256], F32)
    rowv = k.sb("rowv", [1, 256], F32)
    rowab = k.sb("rowab", [1, 256], BF16)
    rowbb = k.sb("rowbb", [1, 256], BF16)
    tmp = k.sb("tmp", [128, 256], F32)
    lbc = k.sb("lbc", [128, 256], F32)

    st4 = k.ps("st4", [128, 2048], F32)
    pdel = k.ps("pdel", [128, 128], F32)
    ptr = k.ps("ptr", [128, 128], BF16)
    prow = k.ps("prow", [1, 128], F32)
    pbc = k.ps("pbc", [128, 128], F32)

    tok_ones = k.memset(ones1[:], 1.0)
    tok_onesc = k.memset(onescol[:], 1.0)
    tok_eps = k.memset(epst[:], 1e-5)
    tok_id = k.dma(idb[:], identb[:, :])
    tok_nw = k.dma(nwse[:], normw_pc[:, :])
    tok_kp = k.dma(kposs[:], kpos[:, :])
    tok_ldbc = k.dma(ldbc[:], ld_bc[:, :])

    t1 = k.dma(ldr[:], ld_rep[:, 0:2048].rearrange("p (a t) -> p a t", t=128))
    t2 = k.dma(scr2[:], cposrep[:, 0:2048].rearrange("p (a t) -> p a t", t=128))
    t3 = k.tt(scr2[:], scr2[:], ldr[:], ALU.mult, deps=[t1, t2])
    t4 = k.act(qdecb[:], scr2[:], AF.Exp, deps=[t3])
    t5 = k.dma(scr2[:], diffrep[:, 0:2048].rearrange("p (a t) -> p a t", t=128),
               deps=[t4])
    t6 = k.tt(scr2[:], scr2[:], ldr[:], ALU.mult, deps=[t5, t4])
    t7 = k.act(dtall[:], scr2[:], AF.Exp, deps=[t6])
    t8 = k.dma(scr2[:, 0:64], chainrep[:, :], deps=[t7])
    t9 = k.dma(scr2[:, 64:128], ld_rep4[:, :], deps=[t7])
    t10 = k.tt(wchain[:], scr2[:, 0:64], scr2[:, 64:128], ALU.mult,
               deps=[t8, t9])
    t11 = k.act(wchain[:], wchain[:], AF.Exp, deps=[t10])
    t12 = k.ts(kdec[:], ldbc[:], kposs[:, 0:1], ALU.mult,
               deps=[tok_ldbc, tok_kp])
    t13 = k.act(kdec[:], kdec[:], AF.Exp, deps=[t12])
    t14 = k.ts(cdv[:], ldbc[:], 128.0, ALU.mult, deps=[tok_ldbc, t13])
    t15 = k.act(cdv[:], cdv[:], AF.Exp, deps=[t14])
    tab_tok = [t11, t13, t15, t4, t7]

    tok_x = k.loadw(_s3(xbf[:], TPC), _r3(x0fm))

    wt = [None, None]
    wuse = [None, None]

    def wload(W, blk, buf, extra=()):
        src = _r3(W[:, blk * 256:(blk + 1) * 256])
        return k.loadw(wblk[buf][:].rearrange("p (c t) -> p c t", t=256), src,
                       deps=[wuse[buf]] + list(extra))

    seq = [("q", Wq, b) for b in range(8)] + [("k", Wk, b) for b in range(8)] \
        + [("v", Wv, b) for b in range(8)]
    wt[0] = wload(Wq, 0, 0)
    bankR4 = [None, None, None, None]
    proj_last = None
    for i, (kind, W, blk) in enumerate(seq):
        buf = i % 2
        if i + 1 < len(seq):
            nk, nW, nblk = seq[i + 1]
            wt[(i + 1) % 2] = wload(nW, nblk, (i + 1) % 2)
        wb = wblk[buf]
        if kind in ("q", "k"):
            dst = qf if kind == "q" else kf
            for j in range(2):
                fch = blk * 2 + j
                for th in range(2):
                    bsl = (j * 2 + th) % 4
                    bank = st4[:, bsl * 512: bsl * 512 + 512]
                    pairs = [(wb[:, kc * 256 + j * 128: kc * 256 + j * 128 + 128],
                              xbf[:, kc * TPC + th * 512: kc * TPC + th * 512 + 512])
                             for kc in range(NCH)]
                    mt = k.mm_group(bank, pairs,
                                    deps=[wt[buf], tok_x, bankR4[bsl]])
                    et = k.act(dst[:, fch * TPC + th * 512:
                                   fch * TPC + th * 512 + 512],
                               bank, AF.Silu, deps=[mt])
                    bankR4[bsl] = et
                    wuse[buf] = mt
                    proj_last = mt
        else:
            for jt in range(8):
                bsl = jt % 4
                bank = st4[:, bsl * 512: bsl * 512 + 256]
                pairs = [(xbf[:, kc * TPC + jt * 128: kc * TPC + jt * 128 + 128],
                          wb[:, kc * 256:(kc + 1) * 256])
                         for kc in range(NCH)]
                mt = k.mm_group(bank, pairs, deps=[wt[buf], tok_x, bankR4[bsl]])
                et = k.act(vt[:, jt * D + blk * 256: jt * D + blk * 256 + 256],
                           bank, AF.Silu, deps=[mt])
                bankR4[bsl] = et
                wuse[buf] = mt
                proj_last = mt

    tok_wg1 = k.loadw(wg1s[:].rearrange("p (c t) -> p c t", t=128),
                      _r3(Wg1), deps=[])

    for th in range(2):
        bsl = th % 2
        bank = st4[:, bsl * 512: bsl * 512 + 512]
        pairs = [(wg1s[:].rearrange("p (c t) -> p c t", t=128)[:, kc, :],
                  xbf[:, kc * TPC + th * 512: kc * TPC + th * 512 + 512])
                 for kc in range(NCH)]
        mt = k.mm_group(bank, pairs, deps=[tok_wg1, tok_x, bankR4[bsl]])
        et = k.copy(g1s[:, th * 512: th * 512 + 512], bank, deps=[mt])
        bankR4[bsl] = et
        proj_last = mt

    tok_wg2 = k.loadw(wg2s[:, 0:2048], Wg2[:, :],
                      deps=[wuse[0], proj_last])

    A_tok = [None] * NCH
    ktr_R = None
    kts_R = None
    del_R = None
    for j in range(8):
        for h in range(NCH):
            ttk = k.transpose(ptr[:, :], kf[:, h * TPC + j * 128:
                                            h * TPC + j * 128 + 128],
                              idb[:], deps=[tok_id, ktr_R,
                                            proj_last if j == 0 else None])
            kev = k.ts(kts[:], ptr[:, :], kdec[:, h:h + 1], ALU.mult,
                       deps=[ttk, t13, kts_R])
            ktr_R = kev
            dmt = k.mm(pdel[:, :], kts[:], vt[:, j * D + h * 128:
                                              j * D + h * 128 + 128],
                       deps=[kev, del_R, proj_last])
            kts_R = dmt
            if j == 0:
                at = k.copy(Ast[:, h * 128:(h + 1) * 128], pdel[:, :],
                            deps=[dmt] + tab_tok)
            else:
                at = k.stt(Ast[:, h * 128:(h + 1) * 128],
                           Ast[:, h * 128:(h + 1) * 128],
                           cdv[:, h:h + 1], pdel[:, :], ALU.mult, ALU.add,
                           deps=[dmt, A_tok[h], t15])
            del_R = at
            A_tok[h] = at

    vfa = k.vflush(A_tok)
    ag_in_tok = k.dma(agin.ap().rearrange("(h p) e -> p h e", p=128), _s3(Ast[:], 128),
                      deps=A_tok + [vfa])
    cc_tok = k.collective("AllGather", ALU.bypass,
                          [[0, 1, 2, 3], [4, 5, 6, 7]],
                          [agin.ap().opt()], [agout.ap().opt()],
                          deps=[ag_in_tok])
    sin_tok = [None] * NCH
    z_toks = []
    for h in range(NCH):
        z_toks.append(k.memset(Sst[:, h * 128:(h + 1) * 128], 0.0,
                               deps=[]))
    agt_tok = None
    for cp in range(4):
        at_ = k.dma(_s3(agt[:], 128),
                    agout.ap()[cp * D:(cp + 1) * D, :].rearrange(
                        "(h p) e -> p h e", p=128),
                    deps=[cc_tok, agt_tok] + z_toks)
        newtok = None
        for h in range(NCH):
            st_ = k.stt(Sst[:, h * 128:(h + 1) * 128],
                        agt[:, h * 128:(h + 1) * 128],
                        wchain[:, h * 4 + cp: h * 4 + cp + 1],
                        Sst[:, h * 128:(h + 1) * 128],
                        ALU.mult, ALU.add,
                        deps=[at_, t11, z_toks[h], sin_tok[h]])
            sin_tok[h] = st_
            newtok = st_
        agt_tok = newtok

    dbg_toks = []
    if DEBUG_A1:
        dbg_toks.append(k.dma(_r3(qf_d), _s3(qf[:], TPC), deps=[proj_last]))
        dbg_toks.append(k.dma(_r3(kf_d), _s3(kf[:], TPC), deps=[proj_last]))
        dbg_toks.append(k.dma(vt_d.rearrange("(j p) f -> p j f", p=128),
                              _s3(vt[:], D), deps=[proj_last]))
        dbg_toks.append(k.dma(ag_d, agout.ap(), deps=[cc_tok]))
        sin_dump = k.dma(sin_d.rearrange("(h p) e -> p h e", p=128),
                             _s3(Sst[:], 128), deps=sin_tok)
        dbg_toks.append(sin_dump)
        for _h in range(NCH):
            sin_tok[_h] = sin_dump
        dbg_toks.append(k.dma(g1_d, g1s[:],
                              deps=[bankR4[0], bankR4[1]]))

    on_first = []
    oR = None
    sR = None
    gR = None
    ptsR = None
    qscR = None
    ogR = None
    sqR = None
    rowR = None
    sbfR = None
    for j in range(8):
        for h in range(NCH):
            qoff = h * TPC + j * 128
            smt = k.mm(st4[:, 0:128], kf[:, qoff:qoff + 128],
                       qf[:, qoff:qoff + 128],
                       deps=[ptsR, proj_last if j == 0 else None,
                             bankR4[0] if (j == 0 and h == 0) else None])
            ptt = k.tt(pts[:], st4[:, 0:128], dtall[:, h * 128:(h + 1) * 128],
                       ALU.mult, deps=[smt, t7])
            ptsR = ptt
            qst = k.tt(qsc[:], qf[:, qoff:qoff + 128],
                       qdecb[:, h * 128:(h + 1) * 128], ALU.mult,
                       deps=[t4, qscR])
            sbt = k.copy(Sbf[:], Sst[:, h * 128:(h + 1) * 128],
                         deps=[sin_tok[h], sbfR])
            om1 = k.mm(st4[:, 512:640], vt[:, j * D + h * 128: j * D + h * 128 + 128],
                       pts[:], deps=[ptt, oR,
                                     bankR4[1] if (j == 0 and h == 0) else None],
                       start=True, stop=False, inc=False)
            om2 = k.mm(st4[:, 512:640], Sbf[:], qsc[:],
                       deps=[sbt, qst], start=False, stop=True)
            sbfR = om2
            qscR = om2
            gmt = k.mm(st4[:, 1024:1152], wg2s[:, h * 128:(h + 1) * 128],
                       g1s[:, j * 128:(j + 1) * 128],
                       deps=[tok_wg2, gR, bankR4[2], bankR4[3], oR])
            sgt = k.act(sgb[:], st4[:, 1024:1152], AF.Sigmoid,
                        deps=[gmt, oR])
            gR = sgt
            ogt = k.tt(ogs[:], sgb[:], st4[:, 512:640], ALU.mult,
                       deps=[om2, sgt, ogR])
            oR = ogt
            sqt = k.act(sqb[:], ogs[:], AF.Square, deps=[ogt, sqR])
            ssm = k.mm(prow[:, :], onescol[:], sqb[:],
                       deps=[sqt, tok_onesc, rowR])
            rsd = k.act(rr[:], prow[:, :], AF.Sqrt, deps=[ssm, tok_eps],
                        bias=epst[0:1, 0:1], scale=1.0 / 128.0)
            rcp = k.recip(rr[:], rr[:], deps=[rsd])
            rcb = k.copy(rrb[:], rr[:], deps=[rcp])
            bct = k.mm(pbc[:, :], ones1[:, :], rrb[:], deps=[rcb, tok_ones,
                                                             ogR])
            ont = k.stt(onrm[:, qoff:qoff + 128], ogs[:],
                        nwse[:, h:h + 1], pbc[:, :], ALU.mult, ALU.mult,
                        deps=[bct, tok_nw, ogt,
                              proj_last if (j == 0 and h == 0) else None,
                              tok_x if (j == 0 and h == 0) else None])
            rowR = ont
            ogR = ont
            sqR = ssm
            ttk = k.transpose(ptr[:, :], kf[:, qoff:qoff + 128], idb[:],
                              deps=[tok_id, ktr_R])
            kev = k.ts(kts[:], ptr[:, :], kdec[:, h:h + 1], ALU.mult,
                       deps=[ttk, kts_R])
            ktr_R = kev
            dmt = k.mm(pdel[:, :], kts[:], vt[:, j * D + h * 128:
                                              j * D + h * 128 + 128],
                       deps=[kev, del_R])
            kts_R = dmt
            sup = k.stt(Sst[:, h * 128:(h + 1) * 128],
                        Sst[:, h * 128:(h + 1) * 128],
                        cdv[:, h:h + 1], pdel[:, :], ALU.mult, ALU.add,
                        deps=[dmt, sbt, t15])
            del_R = sup
            sin_tok[h] = sup
            on_first.append(ont)

    o_done = on_first[-1]
    if DEBUG_A1:
        dbg_toks.append(k.dma(_r3(on_d), _s3(onrm[:], TPC), deps=[o_done]))

    if SERIAL_PHASE in ("D", "D1"):
        Prog.SERIAL = True
    fence = k.p.fence_toks()
    SEG = 256
    out_toks = []
    x2use = [None, None]
    bankR2 = [None, None]
    lnstat_prev = [None, None]
    outprev = [None] * NCH
    for th in range(4):
        t0c = th * SEG
        pre_toks = []
        for blk in range(8):
            buf = blk % 2
            wlf = k.p.fence_toks()
            wt[buf] = wload(Wo, blk, buf, extra=[o_done, gR] + wlf)
            for j in range(2):
                fch = blk * 2 + j
                xb2 = j % 2
                xt = k.dma(x0ch[xb2][:, 0:SEG],
                           x0fm[fch * 128:(fch + 1) * 128, t0c:t0c + SEG],
                           deps=[x2use[xb2]])
                bank = st4[:, (j % 2) * 512:(j % 2) * 512 + SEG]
                pairs = [(wblk[buf][:, kc * 256 + j * 128: kc * 256 + j * 128 + 128],
                          onrm[:, kc * TPC + t0c: kc * TPC + t0c + SEG])
                         for kc in range(NCH)]
                mmf = k.p.fence_toks()
                mt = k.mm_group(bank, pairs,
                                deps=[wt[buf], o_done, bankR2[j % 2]] + mmf)
                et = k.tt(pre[:, fch * SEG:(fch + 1) * SEG],
                          x0ch[xb2][:, 0:SEG], bank, ALU.add,
                          deps=[mt, xt, outprev[fch]])
                bankR2[j % 2] = et
                x2use[xb2] = et
                wuse[buf] = mt
                pre_toks.append(et)
        if SERIAL_PHASE == "D1":
            Prog.SERIAL = False
        if SERIAL_PHASE == "D2":
            Prog.SERIAL = True
        s1 = st4[0:1, 1024:1280]
        s2 = st4[0:1, 1536:1792]
        t_s1 = lnstat_prev[0]
        t_s2 = lnstat_prev[1]
        sq_tok = None
        pb_tok = None
        for kc in range(NCH):
            seg = pre[:, kc * SEG:(kc + 1) * SEG]
            pb_tok = k.copy(preb[:, 0:SEG], seg, deps=[pre_toks[kc], t_s1])
            sq_tok = k.act(sqs[:, 0:SEG], seg, AF.Square,
                           deps=[pre_toks[kc], t_s2])
            t_s1 = k.mm(s1, onescol[:], preb[:, 0:SEG],
                        deps=[pb_tok, tok_onesc], start=(kc == 0),
                        stop=(kc == NCH - 1), inc=True)
            t_s2 = k.mm(s2, onescol[:], sqs[:, 0:SEG],
                        deps=[sq_tok], start=(kc == 0),
                        stop=(kc == NCH - 1), inc=True)
        lnstat_prev = [t_s1, t_s2]
        t_m = k.ts(rowm[0:1, 0:SEG], s1, 1.0 / D, ALU.mult, deps=[t_s1])
        t_v = k.ts(rowv[0:1, 0:SEG], s2, 1.0 / D, ALU.mult, deps=[t_s2])
        t_m2 = k.tt(tmp[0:1, 0:SEG], rowm[0:1, 0:SEG], rowm[0:1, 0:SEG],
                    ALU.mult, deps=[t_m])
        t_v2 = k.tt(rowv[0:1, 0:SEG], rowv[0:1, 0:SEG], tmp[0:1, 0:SEG],
                    ALU.subtract, deps=[t_v, t_m2])
        t_sd = k.act(tmp[0:1, 0:SEG], rowv[0:1, 0:SEG], AF.Sqrt,
                     deps=[t_v2, tok_eps], bias=epst[0:1, 0:1], scale=1.0)
        t_r = k.recip(rowa[0:1, 0:SEG], tmp[0:1, 0:SEG], deps=[t_sd])
        t_ab = k.copy(rowab[0:1, 0:SEG], rowa[0:1, 0:SEG], deps=[t_r])
        t_bb = k.tt(tmp[0:1, 0:SEG], rowm[0:1, 0:SEG], rowa[0:1, 0:SEG],
                    ALU.mult, deps=[t_r, t_ab])
        t_bn = k.ts(tmp[0:1, 0:SEG], tmp[0:1, 0:SEG], -1.0, ALU.mult,
                    deps=[t_bb])
        t_bbb = k.copy(rowbb[0:1, 0:SEG], tmp[0:1, 0:SEG], deps=[t_bn])
        tba = k.mm(st4[:, 1024:1024 + SEG], ones1[:, :], rowab[0:1, 0:SEG],
                   deps=[t_ab, tok_ones, t_s1, t_s2])
        ebA = k.copy(lbc[:, 0:SEG], st4[:, 1024:1024 + SEG], deps=[tba])
        tbb = k.mm(st4[:, 1024:1024 + SEG], ones1[:, :], rowbb[0:1, 0:SEG],
                   deps=[t_bbb, ebA])
        tm2 = None
        tm2s = []
        for kc in range(NCH):
            seg = pre[:, kc * SEG:(kc + 1) * SEG]
            tm1 = k.tt(tmp[:, 0:SEG], seg, lbc[:, 0:SEG], ALU.mult,
                       deps=[tba, ebA, sq_tok, pb_tok, tm2])
            tm2 = k.tt(seg, tmp[:, 0:SEG], st4[:, 1024:1024 + SEG], ALU.add,
                       deps=[tm1, tbb])
            tm2s.append(tm2)
        vf = k.vflush([tm2s[-1]])
        for kc in range(NCH):
            seg = pre[:, kc * SEG:(kc + 1) * SEG]
            ot = k.dma(x1fm[kc * 128:(kc + 1) * 128, t0c:t0c + SEG],
                       seg, deps=[tm2s[kc], vf])
            outprev[kc] = ot
            out_toks.append(ot)
        if SERIAL_PHASE in ("D1", "D2"):
            Prog.SERIAL = (SERIAL_PHASE == "D1")
    Prog.SERIAL = False
    return k.finish(out_toks + dbg_toks)


def stage_a1(core, x0_fm_shard, inputs):
    b, r = divmod(core, 4)
    ld = np.asarray(inputs["L0_attn_log_decay"], np.float32)
    ld_rep = np.broadcast_to(np.repeat(ld, 128), (128, 2048))
    ld_rep4 = np.broadcast_to(np.repeat(ld, 4), (128, 64))
    ld_bc = np.broadcast_to(ld, (128, 16))
    cpos = np.arange(1, 129, dtype=np.float32)
    cposrep = np.broadcast_to(np.tile(cpos, 16), (128, 2048))
    s_ = np.arange(128)
    c_ = np.arange(128)
    diffmT = (c_[None, :] - s_[:, None]).astype(np.float32)
    diffmT[diffmT < 0] = 1e9
    diffrep = np.tile(diffmT, (1, 16))
    chain = np.full(4, 1e9, np.float32)
    for j in range(4):
        if j < r:
            chain[j] = 1024.0 * (r - 1 - j)
    chainrep = np.broadcast_to(np.tile(chain, 16), (128, 64))
    kposv = (127 - np.arange(128)).astype(np.float32).reshape(128, 1)
    return {
        "x0fm": x0_fm_shard,
        "Wq": inputs["L0_attn_Wq"], "Wk": inputs["L0_attn_Wk"],
        "Wv": inputs["L0_attn_Wv"], "Wo": inputs["L0_attn_Wo"],
        "Wg1": inputs["L0_attn_Wg1"], "Wg2": inputs["L0_attn_Wg2"],
        "normw_pc": _chunk_cols(inputs["L0_attn_norm_w"]),
        "ld_rep": np.ascontiguousarray(ld_rep, np.float32),
        "ld_rep4": np.ascontiguousarray(ld_rep4, np.float32),
        "ld_bc": np.ascontiguousarray(ld_bc, np.float32),
        "cposrep": np.ascontiguousarray(cposrep, np.float32),
        "diffrep": np.ascontiguousarray(diffrep, np.float32),
        "chainrep": np.ascontiguousarray(chainrep, np.float32),
        "kpos": kposv,
        "identb": _ident_bf16(),
    }


def build_a2():
    k = KB("a2")
    x1fm = k.dram("x1fm", [D, TPC], F32)
    ln1_pc = k.dram("ln1_pc", [128, NCH], F32)
    kv_d_w = k.dram("kv_d_w", [D, 512], F32)
    q_d_w = k.dram("q_d_w", [D, 512], F32)
    k_u_w = k.dram("k_u_w", [512, 1024], F32)
    v_u_w = k.dram("v_u_w", [512, D], F32)
    q_u_w = k.dram("q_u_w", [512, 1024], F32)
    rope_k_w = k.dram("rope_k_w", [D, 1024], F32)
    rope_q_w = k.dram("rope_q_w", [512, 1024], F32)
    cost = k.dram("cost", [128, 8 * 64], F32)
    sint = k.dram("sint", [128, 8 * 64], F32)
    k_tm = k.dram("k_tm", [TPC, D], BF16, kind="ExternalOutput")
    q_tm = k.dram("q_tm", [TPC, D], BF16, kind="ExternalOutput")
    v_tm = k.dram("v_tm", [TPC, D], BF16, kind="ExternalOutput")

    x1f = k.sb("x1f", [128, NCH * TPC], F32)
    kall = _BCView(x1f, 0, 8 * D)
    qall = _BCView(x1f, 8 * D, 8 * D)
    vall = k.sb("vall", [128, 8 * D], BF16)
    xn = k.sb("xn", [128, NCH * TPC], BF16)
    kvf = k.sb("kvf", [128, 4 * TPC], BF16)
    qdf = k.sb("qdf", [128, 4 * TPC], BF16)
    wblk = [k.sb("wblk%d" % i, [128, NCH * 512], BF16) for i in range(2)]
    costs = k.sb("costs", [128, 8 * 64], F32)
    sints = k.sb("sints", [128, 8 * 64], F32)
    lnw = k.sb("lnw", [128, NCH], F32)
    ones1 = k.sb("ones1", [1, 128], BF16)
    onescol = k.sb("onescol", [128, 1], BF16)
    epst = k.sb("epst", [1, 1], F32)
    sqs = k.sb("sqs", [128, 512], BF16)
    rowv = k.sb("rowv", [1, 512], F32)
    rowab = k.sb("rowab", [1, 512], BF16)
    lbc = k.sb("lbc", [128, 512], F32)
    rsc1 = k.sb("rsc1", [128, 64], F32)
    rsc2 = k.sb("rsc2", [128, 64], F32)

    st4 = k.ps("st4", [128, 2048], F32)
    bc = k.ps("bc", [128, 512], F32)
    srow = k.ps("srow", [1, 1024], F32)

    tok_ones = k.memset(ones1[:], 1.0)
    tok_onesc = k.memset(onescol[:], 1.0)
    tok_eps = k.memset(epst[:], 1e-5)
    tok_ln = k.dma(lnw[:], ln1_pc[:, :])
    tok_cos = k.dma(costs[:], cost[:, :])
    tok_sin = k.dma(sints[:], sint[:, :])
    tok_x1 = k.dma(_s3(x1f[:], TPC), _r3(x1fm))

    xn_toks = []
    sq_prev = None
    s2_tok = [None, None]
    for th in range(2):
        s2 = srow[0:1, th * 512:(th + 1) * 512]
        for kc in range(NCH):
            sq = k.act(sqs[:], x1f[:, kc * TPC + th * 512: kc * TPC + th * 512 + 512],
                       AF.Square, deps=[tok_x1, s2_tok[th] if kc else sq_prev])
            s2_tok[th] = k.mm(s2, onescol[:], sqs[:], deps=[sq, tok_onesc],
                              start=(kc == 0), stop=(kc == NCH - 1), inc=True)
            sq_prev = s2_tok[th]
    xn_last = None
    for th in range(2):
        s2 = srow[0:1, th * 512:(th + 1) * 512]
        t_v = k.ts(rowv[:], s2, 1.0 / D, ALU.mult, deps=[s2_tok[th]])
        t_sd = k.act(rowv[:], rowv[:], AF.Sqrt, deps=[t_v, tok_eps],
                     bias=epst[0:1, 0:1], scale=1.0)
        t_r = k.recip(rowv[:], rowv[:], deps=[t_sd])
        t_ab = k.copy(rowab[:], rowv[:], deps=[t_r])
        tb = k.mm(bc[:, :], ones1[:, :], rowab[:], deps=[t_ab, tok_ones,
                                                         xn_last])
        eb = k.copy(lbc[:, :], bc[:, :], deps=[tb])
        for kc in range(NCH):
            xt = k.stt(xn[:, kc * TPC + th * 512: kc * TPC + th * 512 + 512],
                       x1f[:, kc * TPC + th * 512: kc * TPC + th * 512 + 512],
                       lnw[:, kc:kc + 1], lbc[:, :], ALU.mult, ALU.mult,
                       deps=[eb, tok_ln])
            xn_toks.append(xt)
            xn_last = xt

    wuse = [None, None]
    wt = [None, None]

    def wload(W, c0, c1, buf, inner):
        w = c1 - c0
        src = _r3(W[:, c0:c1])
        return k.loadw(wblk[buf][:, 0:NCH * w].rearrange(
            "p (c t) -> p c t", t=w), src, deps=[wuse[buf]])

    bankR = [None, None, None, None]
    wt[0] = wload(kv_d_w, 0, 512, 0, None)
    wt[1] = wload(q_d_w, 0, 512, 1, None)
    for which, (dst, buf) in enumerate(((kvf, 0), (qdf, 1))):
        for lc in range(4):
            for th in range(2):
                bsl = (lc * 2 + th) % 4
                bank = st4[:, bsl * 512:bsl * 512 + 512]
                pairs = [(wblk[buf][:, kc * 512 + lc * 128: kc * 512 + lc * 128 + 128],
                          xn[:, kc * TPC + th * 512: kc * TPC + th * 512 + 512])
                         for kc in range(NCH)]
                mt = k.mm_group(bank, pairs, deps=[wt[buf], bankR[bsl]] + xn_toks)
                et = k.copy(dst[:, lc * TPC + th * 512: lc * TPC + th * 512 + 512],
                            bank, deps=[mt])
                bankR[bsl] = et
                wuse[buf] = mt

    kv_toks = []

    def tm_proj(bank, rhs_w, jt, nk, lat, deps):
        pairs = [(lat[:, kc * TPC + jt * 128: kc * TPC + jt * 128 + 128],
                  rhs_w[:, kc * 512:(kc + 1) * 512] if nk == 16 else
                  rhs_w[:, kc * 512:(kc + 1) * 512])
                 for kc in range(nk)]
        return k.mm_group(bank, pairs, deps=deps)

    seqs = [
        ("kp", k_u_w, 1024, kvf, 4, kall, 0),
        ("qp", q_u_w, 1024, qdf, 4, qall, 0),
        ("v",  v_u_w, 2048, kvf, 4, vall, None),
        ("kr", rope_k_w, 1024, xn, NCH, kall, 64),
        ("qr", rope_q_w, 1024, qdf, 4, qall, 64),
    ]
    buf = 0
    rope_R = None
    for name, W, wcols, lat, nk, dstall, coff in seqs:
        for blk in range(wcols // 512):
            buf = (buf + 1) % 2
            src = _r3(W[:, blk * 512:(blk + 1) * 512])
            wtk = k.loadw(wblk[buf][:, 0:nk * 512].rearrange(
                "p (c t) -> p c t", t=512), src, deps=[wuse[buf]])
            for jt in range(8):
                bsl = jt % 4
                bank = st4[:, bsl * 512:bsl * 512 + 512]
                pairs = [(lat[:, kc * TPC + jt * 128: kc * TPC + jt * 128 + 128],
                          wblk[buf][:, kc * 512:(kc + 1) * 512])
                         for kc in range(nk)]
                mt = k.mm_group(bank, pairs, deps=[wtk, bankR[bsl]] + xn_toks)
                wuse[buf] = mt
                if name == "v":
                    et = k.copy(vall[:, jt * D + blk * 512: jt * D + blk * 512 + 512],
                                bank, deps=[mt])
                elif name in ("kp", "qp"):
                    dst3 = dstall[:].rearrange("p (j f) -> p j f", f=D)[
                        :, jt, :].rearrange("p (h c) -> p h c", c=128)
                    et = k.copy(dst3[:, blk * 8:(blk + 1) * 8, 0:64],
                                bank[:].rearrange("p (h c) -> p h c", c=64),
                                deps=[mt])
                else:
                    ets = []
                    for hh in range(8):
                        h = blk * 8 + hh
                        pseg = bank[:, hh * 64:hh * 64 + 64]
                        pperm = bank[:, hh * 64:hh * 64 + 64].rearrange(
                            "p (a b) -> p a b", b=32)[:, ::-1, :] \
                            if False else None
                        t1 = k.tt(rsc1[:, 0:64], pseg,
                                  costs[:, jt * 64:(jt + 1) * 64], ALU.mult,
                                  deps=[mt, tok_cos, rope_R])
                        t2a = k.tt(rsc2[:, 0:32], bank[:, hh * 64 + 32:hh * 64 + 64],
                                   sints[:, jt * 64:jt * 64 + 32], ALU.mult,
                                   deps=[mt, tok_sin, rope_R])
                        t2b = k.tt(rsc2[:, 32:64], bank[:, hh * 64:hh * 64 + 32],
                                   sints[:, jt * 64 + 32:jt * 64 + 64], ALU.mult,
                                   deps=[mt, tok_sin, rope_R])
                        dst3 = dstall[:].rearrange("p (j f) -> p j f", f=D)[
                            :, jt, :]
                        t3 = k.tt(dst3[:, h * 128 + 64:h * 128 + 128],
                                  rsc1[:, 0:64], rsc2[:, 0:64], ALU.add,
                                  deps=[t1, t2a, t2b])
                        ets.append(t3)
                        rope_R = t3
                    et = ets[-1]
                bankR[bsl] = et
                kv_toks.append(et)

    vf = k.vflush([kv_toks[-1]])
    deps_all = kv_toks[-8:] + [vf]
    o1 = k.dma(_r3(k_tm), _s3(kall[:], D), deps=deps_all)
    o2 = k.dma(_r3(q_tm), _s3(qall[:], D), deps=deps_all)
    o3 = k.dma(_r3(v_tm), _s3(vall[:], D), deps=deps_all)
    return k.finish([o1, o2, o3])


def stage_a2(core, x1_fm_shard, inputs):
    b, r = divmod(core, 4)
    RD = 64
    inv = 1.0 / 10000.0 ** (np.arange(0, RD, 2, dtype=np.float32) / RD)
    t = np.arange(N, dtype=np.float32)
    ang = np.outer(t, inv)
    emb = np.concatenate([ang, ang], axis=1)
    cosf = np.cos(emb)
    sinf = np.sin(emb)
    sgn = np.ones(64, np.float32)
    sgn[:32] = -1.0
    sinf = sinf * sgn[None, :]
    sl = slice(r * TPC, (r + 1) * TPC)
    cos_c = cosf[sl].reshape(8, 128, 64).transpose(1, 0, 2).reshape(128, 8 * 64)
    sin_c = sinf[sl].reshape(8, 128, 64).transpose(1, 0, 2).reshape(128, 8 * 64)
    return {
        "x1fm": x1_fm_shard,
        "ln1_pc": _chunk_cols(inputs["L0_ffn_ln1_w"]),
        "kv_d_w": inputs["L0_ffn_kv_d_w"], "q_d_w": inputs["L0_ffn_q_d_w"],
        "k_u_w": inputs["L0_ffn_k_u_w"], "v_u_w": inputs["L0_ffn_v_u_w"],
        "q_u_w": inputs["L0_ffn_q_u_w"],
        "rope_k_w": inputs["L0_ffn_rope_k_w"],
        "rope_q_w": inputs["L0_ffn_rope_q_w"],
        "cost": np.ascontiguousarray(cos_c),
        "sint": np.ascontiguousarray(sin_c),
    }


def build_a3():
    k = KB("a3")
    ktm = k.dram("ktm", [N, 512], BF16)
    qtm = k.dram("qtm", [N, 512], BF16)
    vtm = k.dram("vtm", [N, 512], BF16)
    amask = k.dram("amask", [128, 4 * 512], F32)
    mmask = k.dram("mmask", [128, 4 * 512], BF16)
    o_fm = k.dram("o_fm", [512, N], BF16, kind="ExternalOutput")
    mscr = k.dram_internal("mscr", [32, 128], F32)

    kf = [k.sb("kf%d" % i, [128, N], BF16) for i in range(2)]
    qf = [k.sb("qf%d" % i, [128, N], BF16) for i in range(2)]
    vt = [k.sb("vt%d" % i, [128, 32 * 128], BF16) for i in range(2)]
    of = k.sb("of", [128, N], BF16)
    am = k.sb("am", [128, 4 * 512], F32)
    mm_ = k.sb("mm_", [128, 4 * 512], BF16)
    pt = k.sb("pt", [128, 1024], BF16)
    smsk = k.sb("smsk", [128, 512], F32)
    runm = k.sb("runm", [128, 1], F32)
    tmpc = k.sb("tmpc", [128, 1], F32)
    mneg = k.sb("mneg", [128, 32], F32)
    mnegT = k.sb("mnegT", [32, 128], F32)
    negrow = k.sb("negrow", [1, 512], F32)
    negrowb = k.sb("negrowb", [1, 512], BF16)
    linv = k.sb("linv", [1, 512], F32)
    linvb = k.sb("linvb", [1, 512], BF16)
    lbc = k.sb("lbc", [128, 512], F32)
    idb = k.sb("idb", [128, 128], BF16)
    idf = k.sb("idf", [128, 128], F32)
    ones1 = k.sb("ones1", [1, 128], BF16)
    onescol = k.sb("onescol", [128, 1], BF16)
    identb = k.dram("identb", [128, 128], BF16)
    identf = k.dram("identf", [128, 128], F32)

    st4 = k.ps("st4", [128, 2048], F32)
    pO = k.ps("pO", [128, 512], F32)
    pL = k.ps("pL", [128, 512], F32)
    pB = k.ps("pB", [128, 512], F32)

    tok_ones = k.memset(ones1[:], 1.0)
    tok_onesc = k.memset(onescol[:], 1.0)
    tok_id = k.dma(idb[:], identb[:, :])
    tok_idf = k.dma(idf[:], identf[:, :])
    tok_am = k.dma(am[:], amask[:, :])
    tok_mm = k.loadw(mm_[:], mmask[:, :].bitcast(F32)) if False else \
        k.dma(mm_[:], mmask[:, :])

    ld_tok = [None, None]
    use_tok = [None, None]

    def load_head(i, buf):
        c0 = i * 128
        t1 = k.dma_t(kf[buf][:], ktm[:, c0:c0 + 128], deps=[use_tok[buf]])
        t2 = k.dma_t(qf[buf][:], qtm[:, c0:c0 + 128], deps=[use_tok[buf]])
        t3 = k.dma(vt[buf][:].rearrange("p (j c) -> p j c", c=128),
                   vtm[:, c0:c0 + 128].rearrange("(j p) f -> p j f", p=128),
                   deps=[use_tok[buf]], q="gpsimd")
        return (t1, t2, t3)

    ld_tok[0] = load_head(0, 0)
    out_toks = []
    bankR = [None, None, None, None]
    oR = None
    lR = None
    bR = None
    ptR = [None, None]
    mskR = None
    redR = None
    of_use = None

    for i in range(4):
        buf = i % 2
        if i + 1 < 4:
            ld_tok[(i + 1) % 2] = load_head(i + 1, (i + 1) % 2)
        lt = ld_tok[buf]
        mn_toks = []
        for qt in range(32):
            nkm = qt // 4 + 1
            first = True
            rtok = None
            for km in range(nkm):
                bsl = km % 4
                smt = k.mm(st4[:, bsl * 512:bsl * 512 + 512],
                           qf[buf][:, qt * 128:(qt + 1) * 128],
                           kf[buf][:, km * 512:(km + 1) * 512],
                           deps=[lt[0], lt[1], bankR[bsl]])
                if km == qt // 4:
                    pat = qt % 4
                    ms = k.tt(smsk[:], st4[:, bsl * 512:bsl * 512 + 512],
                              am[:, pat * 512:(pat + 1) * 512], ALU.add,
                              deps=[smt, tok_am, mskR])
                    rd = k.reduce(tmpc[:] if not first else runm[:],
                                  smsk[:], ALU.max, deps=[ms, redR])
                    mskR = rd
                    bankR[bsl] = rd
                else:
                    rd = k.reduce(tmpc[:] if not first else runm[:],
                                  st4[:, bsl * 512:bsl * 512 + 512],
                                  ALU.max, deps=[smt, redR])
                    bankR[bsl] = rd
                if not first:
                    rtok = k.tt(runm[:], runm[:], tmpc[:], ALU.max,
                                deps=[rd, rtok])
                else:
                    rtok = rd
                    first = False
                redR = rtok
            mt_ = k.ts(mneg[:, qt:qt + 1], runm[:], -1.0, ALU.mult,
                       deps=[rtok])
            mn_toks.append(mt_)
            redR = mt_
        ttm = k.transpose(pB[0:32, 0:128], mneg[:], idf[:],
                          deps=[tok_idf] + mn_toks[-4:])
        evm = k.copy(mnegT[:, :], pB[0:32, 0:128], deps=[ttm])
        vfm = k.vflush([evm])
        dmm = k.dma(mscr.ap(), mnegT[:, :], deps=[evm, vfm])
        prevE2 = [None, None]
        for qm in range(8):
            nrow = k.dma(negrow[:].rearrange("o (a b) -> o a b", b=128),
                         mscr.ap()[qm * 4:(qm + 1) * 4, :].unsqueeze(0),
                         deps=[dmm, bR])
            nrb = k.copy(negrowb[:], negrow[:], deps=[nrow])
            pv = None
            lmm = None
            for kt in range(qm * 4 + 4):
                half = kt % 2
                scol = half * 1024
                mtA = k.mm(st4[:, scol:scol + 512],
                           kf[buf][:, kt * 128:(kt + 1) * 128],
                           qf[buf][:, qm * 512:(qm + 1) * 512],
                           deps=[lt[0], lt[1], ptR[half], bankR[half]],
                           inc=False, start=True, stop=False)
                mtB = k.mm(st4[:, scol:scol + 512], ones1[:, :], negrowb[:],
                           deps=[nrb], start=False, stop=True)
                e = k.act(pt[:, half * 512:half * 512 + 512],
                          st4[:, scol:scol + 512], AF.Exp,
                          deps=[mtB, prevE2[half]],
                          scale=float(1.0 / np.sqrt(128.0)))
                ptR[half] = e
                pdep = [e]
                if kt >= qm * 4:
                    pat = kt - qm * 4
                    e2 = k.tt(pt[:, half * 512:half * 512 + 512],
                              pt[:, half * 512:half * 512 + 512],
                              mm_[:, pat * 512:(pat + 1) * 512], ALU.mult,
                              deps=[e, tok_mm])
                    ptR[half] = e2
                    prevE2[half] = e2
                    pdep = [e2]
                lmm = k.mm(pL[0:1, :], onescol[:],
                           pt[:, half * 512:half * 512 + 512],
                           deps=pdep + [tok_onesc, lR if kt == 0 else None],
                           start=(kt == 0), stop=(kt == qm * 4 + 3),
                           inc=True)
                pv = k.mm(pO[:, :], vt[buf][:, kt * 128:(kt + 1) * 128],
                          pt[:, half * 512:half * 512 + 512],
                          deps=pdep + [oR if kt == 0 else None],
                          start=(kt == 0), stop=(kt == qm * 4 + 3),
                          inc=True)
            t1 = k.recip(linv[:], pL[0:1, :], deps=[lmm, lR])
            t2 = k.copy(linvb[:], linv[:], deps=[t1])
            t3 = k.mm(pB[:, :], ones1[:, :], linvb[:], deps=[t2, tok_ones, bR])
            t4 = k.copy(lbc[:, :], pB[:, :], deps=[t3])
            t5 = k.tt(of[:, qm * 512:(qm + 1) * 512], pO[:, :], lbc[:, :],
                      ALU.mult, deps=[t4, pv, of_use])
            oR = t5
            lR = t5
            bR = t4
        vfo = k.vflush([oR])
        ot = k.dma(o_fm[i * 128:(i + 1) * 128, :], of[:], deps=[oR, vfo])
        out_toks.append(ot)
        of_use = ot
        use_tok[buf] = oR
        bankR[0] = ptR[0]
        bankR[1] = ptR[1]
    return k.finish(out_toks)


def stage_a3(core, ktm_my, qtm_my, vtm_my):
    am = np.zeros((128, 4, 512), np.float32)
    mmk = np.zeros((128, 4, 512), np.float32)
    row = np.arange(128)
    col = np.arange(512)
    for p in range(4):
        qg = p * 128 + row
        vis = (col[None, :] <= qg[:, None])
        am[:, p, :] = np.where(vis, 0.0, -1e30)
    for p in range(4):
        kg = p * 128 + row
        vis = (kg[:, None] <= col[None, :])
        mmk[:, p, :] = np.where(vis, 1.0, 0.0)
    import ml_dtypes
    return {
        "ktm": ktm_my, "qtm": qtm_my, "vtm": vtm_my,
        "amask": np.ascontiguousarray(am.reshape(128, 4 * 512)),
        "mmask": np.ascontiguousarray(mmk.reshape(128, 4 * 512)).astype(
            ml_dtypes.bfloat16),
        "identb": _ident_bf16(),
        "identf": np.eye(128, dtype=np.float32),
    }


def build_a4():
    k = KB("a4")
    o_in = k.dram("o_in", [D, TPC], BF16)
    x1fm = k.dram("x1fm", [D, TPC], F32)
    o_w = k.dram("o_w", [D, D], F32)
    ln2_pc = k.dram("ln2_pc", [128, NCH], F32)
    router_w = k.dram("router_w", [D, 7], F32)
    router_b = k.dram("router_b", [1, 7], F32)
    sh_gate = k.dram("sh_gate", [D, 1024], F32)
    sh_up = k.dram("sh_up", [D, 1024], F32)
    sh_down = k.dram("sh_down", [1024, D], F32)
    r_gate = k.dram("r_gate", [7, D, 1024], F32)
    r_up = k.dram("r_up", [7, D, 1024], F32)
    r_down = k.dram("r_down", [7, 1024, D], F32)
    identf = k.dram("identf", [128, 128], F32)
    x2fm = k.dram("x2fm", [D, TPC], F32, kind="ExternalOutput")
    wscr = k.dram_internal("wscr", [7, TPC], F32)

    h_f = k.sb("h_f", [128, NCH * TPC], F32)
    oin = k.sb("oin", [128, NCH * TPC], BF16)
    xn2 = oin
    xnf = [k.sb("xnf%d" % i, [128, 512], F32) for i in range(2)]
    wpool = [k.sb("wpool%d" % i, [128, 16 * 512], BF16) for i in range(3)]
    rws = k.sb("rws", [128, NCH * 8], F32)
    rbs = k.sb("rbs", [1, 7], F32)
    idf = k.sb("idf", [128, 128], F32)
    pr = k.sb("pr", [128, 8], F32)
    m1n = k.sb("m1n", [128, 1], F32)
    spm = k.sb("spm", [128, 8], F32)
    mask1 = k.sb("mask1", [128, 8], F32)
    m2 = k.sb("m2", [128, 1], F32)
    selm = k.sb("selm", [128, 8], F32)
    dn = k.sb("dn", [128, 1], F32)
    wtt = k.sb("wtt", [128, 8], F32)
    wrow = k.sb("wrow", [1, 512], F32)
    wrowb = k.sb("wrowb", [1, 512], BF16)
    wbc = k.sb("wbc", [128, 512], BF16)
    gs = k.sb("gs", [128, 8 * 512], BF16)
    lfs = gs[0:8, 0:2048].bitcast(F32)
    hb = k.sb("hb", [128, 8 * 512], BF16)
    us = k.sb("us", [128, 512], BF16)
    acc = k.sb("acc", [128, NCH * 512], F32)
    x1ch = k.sb("x1ch", [128, 512], F32)
    ones1 = k.sb("ones1", [1, 128], BF16)
    onescol = k.sb("onescol", [128, 1], BF16)
    epst = k.sb("epst", [1, 1], F32)
    sqs = k.sb("sqs", [128, 512], BF16)
    preb = k.sb("preb", [128, 512], BF16)
    rowbb = k.sb("rowbb", [1, 512], BF16)
    tmp = hb[:, 0:1024].bitcast(F32)
    tmpr0 = hb[0:1, 0:1024].bitcast(F32)
    lbc = hb[:, 1024:2048].bitcast(F32)
    rowa = hb[0:1, 2048:3072].bitcast(F32)
    rowv = hb[0:1, 3072:4096].bitcast(F32)
    rowm = wrow
    rowab = wrowb

    st4 = k.ps("st4", [128, 2048], F32)
    bc = k.ps("bc", [128, 512], F32)
    trp = k.ps("trp", [128, 128], F32)
    srow = k.ps("srow", [1, 1024], F32)

    tok_ones = k.memset(ones1[:], 1.0)
    tok_onesc = k.memset(onescol[:], 1.0)
    tok_eps = k.memset(epst[:], 1e-5)
    tok_idf = k.dma(idf[:], identf[:, :])
    tok_rw = k.dma(rws[:].rearrange("p (c e) -> p c e", e=8)[:, :, 0:7],
                   _r3(router_w))
    tok_rb = k.dma(rbs[:], router_b[:, :])
    tok_oin = k.dma(_s3(oin[:], TPC), _r3(o_in))
    tok_x1 = None

    wuse = [None, None, None]
    slot = [0]

    def wload(src_ap, inner, extra=()):
        s = slot[0]
        slot[0] = (s + 1) % 3
        t = k.loadw(wpool[s][:, 0:(src_ap.shape[0] // 128) * inner].rearrange(
            "p (c t) -> p c t", t=inner), _r3(src_ap),
            deps=[wuse[s]] + list(extra))
        return s, t

    bankR = [None, None, None, None]
    h_toks = []
    x1use = None
    for blk in range(4):
        s1_, lt = wload(o_w[:, blk * 512:(blk + 1) * 512], 512)
        for j in range(4):
            fch = blk * 4 + j
            for th in range(2):
                bsl = (j * 2 + th) % 4
                bank = st4[:, bsl * 512:bsl * 512 + 512]
                pairs = [(wpool[s1_][:, kc * 512 + j * 128: kc * 512 + j * 128 + 128],
                          oin[:, kc * TPC + th * 512: kc * TPC + th * 512 + 512])
                         for kc in range(NCH)]
                xt = k.dma(x1ch[:], x1fm[fch * 128:(fch + 1) * 128,
                                          th * 512:th * 512 + 512],
                           deps=[x1use])
                mt = k.mm_group(bank, pairs, deps=[lt, tok_oin, bankR[bsl]] + k.p.fence_toks())
                et = k.tt(h_f[:, fch * TPC + th * 512: fch * TPC + th * 512 + 512],
                          x1ch[:, :], bank, ALU.add,
                          deps=[mt, xt])
                bankR[bsl] = et
                wuse[s1_] = mt
                x1use = et
                h_toks.append(et)

    lnw = k.sb("lnw4", [128, NCH], F32)
    tok_ln = k.dma(lnw[:], ln2_pc[:, :])
    s2_tok = [None, None]
    sq_prev = None
    for th in range(2):
        s2 = srow[0:1, th * 512:(th + 1) * 512]
        for kc in range(NCH):
            sq = k.act(sqs[:], h_f[:, kc * TPC + th * 512: kc * TPC + th * 512 + 512],
                       AF.Square, deps=[h_toks[-1], s2_tok[th] if kc else sq_prev])
            s2_tok[th] = k.mm(s2, onescol[:], sqs[:], deps=[sq, tok_onesc],
                              start=(kc == 0), stop=(kc == NCH - 1), inc=True)
            sq_prev = s2_tok[th]
    xn_toks = []
    mt_l = [None, None]
    xn_last = None
    xnf_use = [None, None]
    for th in range(2):
        s2 = srow[0:1, th * 512:(th + 1) * 512]
        t_v = k.ts(rowv, s2, 1.0 / D, ALU.mult, deps=[s2_tok[th]])
        t_sd = k.act(rowv, rowv, AF.Sqrt, deps=[t_v, tok_eps],
                     bias=epst[0:1, 0:1], scale=1.0)
        t_r = k.recip(rowv, rowv, deps=[t_sd])
        t_ab = k.copy(rowab[0:1, 0:512], rowv, deps=[t_r])
        tb = k.mm(bc[:, :], ones1[:, :], rowab[0:1, 0:512], deps=[t_ab, tok_ones,
                                                         xn_last])
        eb = k.copy(lbc, bc[:, :], deps=[tb])
        for kc in range(NCH):
            xb2 = kc % 2
            xf = k.stt(xnf[xb2][:, :],
                       h_f[:, kc * TPC + th * 512: kc * TPC + th * 512 + 512],
                       lnw[:, kc:kc + 1], lbc, ALU.mult, ALU.mult,
                       deps=[eb, tok_ln, xnf_use[xb2]])
            xc = k.copy(xn2[:, kc * TPC + th * 512: kc * TPC + th * 512 + 512],
                        xnf[xb2][:, :], deps=[xf])
            mt_l[th] = k.mm(st4[0:7, 1024 + th * 512:1024 + th * 512 + 512],
                            rws[:, kc * 8:kc * 8 + 7],
                            xnf[xb2][:, :],
                            deps=[xf, tok_rw, bankR[2 + th] if kc == 0 else None],
                            start=(kc == 0), stop=(kc == NCH - 1), inc=True)
            xnf_use[xb2] = mt_l[th]
            xn_toks.append(xc)
            xn_last = xc
    bankR[2] = mt_l[0]
    bankR[3] = mt_l[1]

    ev_l = k.copy(gs[0:7, 0:2048].bitcast(F32), st4[0:7, 1024:2048], deps=[mt_l[0], mt_l[1]])
    rbb = k.sb("rbb", [8, 1], F32)
    tok_rbT = k.dma(rbb[0:7, 0:1], router_b.rearrange("a b -> b a"), deps=[])
    badd = k.ts(gs[0:7, 0:2048].bitcast(F32), gs[0:7, 0:2048].bitcast(F32),
                rbb[0:7, 0:1], ALU.add, deps=[ev_l, tok_rbT])
    wt_toks = []
    tr_last = None
    t8_prev = None
    for jt in range(8):
        tt_l = k.transpose(trp[:, 0:7],
                           gs[0:7, jt * 256:jt * 256 + 256].bitcast(F32),
                           idf[0:7, 0:7], deps=[badd, tok_idf, tr_last])
        t0 = k.act(pr[:, 0:7], trp[:, 0:7], AF.Sigmoid, deps=[tt_l, t8_prev])
        t2 = k.reduce(m1n[:], pr[:, 0:7], ALU.max, negate=True, deps=[t0])
        t3 = k.ts(spm[:, 0:7], pr[:, 0:7], m1n[:, 0:1], ALU.add, deps=[t2, t0])
        t4 = k.ts(mask1[:, 0:7], spm[:, 0:7], 0.0, ALU.is_ge, deps=[t3])
        t5 = k.stt(spm[:, 0:7], mask1[:, 0:7], 1e30, pr[:, 0:7], ALU.mult,
                   ALU.subtract, deps=[t4])
        t6 = k.reduce(m2[:], spm[:, 0:7], ALU.min, negate=True, deps=[t5])
        t7 = k.ts(selm[:, 0:7], pr[:, 0:7], m2[:, 0:1], ALU.is_ge,
                  deps=[t6, t5])
        td = k.tt(dn[:], m2[:], m1n[:], ALU.subtract, deps=[t6, t2])
        tdr = k.recip(dn[:], dn[:], deps=[td])
        tds = k.ts(dn[:], dn[:], 0.125, ALU.mult, deps=[tdr])
        t8a = k.tt(wtt[:, 0:7], pr[:, 0:7], selm[:, 0:7], ALU.mult,
                   deps=[t7, t0])
        t8 = k.ts(wtt[:, 0:7], wtt[:, 0:7], dn[:, 0:1], ALU.mult,
                  deps=[t8a, tds])
        t8_prev = t8
        tt_ = k.transpose(trp[0:7, :], wtt[:, 0:7] if False else wtt[:, 0:8],
                          idf[:], deps=[t8, tok_idf, tr_last]) if False else \
            k.transpose(trp[0:8, :], wtt[:, 0:8], idf[:],
                        deps=[t8, tok_idf, tr_last])
        ev = k.copy(hb[0:8, 0:256].bitcast(F32), trp[0:8, 0:128], deps=[tt_])
        tr_last = ev
        vfw = k.vflush([ev])
        wt_toks.append(k.dma(wscr.ap()[:, jt * 128:(jt + 1) * 128],
                             tmp[0:7, 0:128], deps=[ev, vfw]))

    out_toks = []
    lnstat_prev = [None, None]
    outprev = [None] * NCH
    wrow_prev = [None]
    bankR4 = [bankR[0], bankR[1], None, None]
    wbcR = None
    for th in range(2):
        tof = th * 512
        acc_tok = [None] * NCH
        for kc in range(NCH):
            xck = k.dma(x1ch[:], x1fm[kc * 128:(kc + 1) * 128,
                                       tof:tof + 512],
                        deps=[acc_tok[kc - 1] if kc else
                              (out_toks[-1] if out_toks else x1use)])
            at = k.tt(acc[:, kc * 512:(kc + 1) * 512],
                      x1ch[:, :],
                      h_f[:, kc * TPC + tof: kc * TPC + tof + 512], ALU.add,
                      deps=[xck, outprev[kc]])
            acc_tok[kc] = at
        rinR = None
        h_prev = [None] * 8
        for e in range(8):
            shared = (e == 0)
            if not shared:
                t_wr = k.dma(wrow[:], wscr.ap()[e - 1:e, tof:tof + 512],
                             deps=wt_toks + [wrow_prev[0]])
                t_wrb = k.copy(wrowb[:], wrow[:], deps=[t_wr, wrow_prev[0]])
                tb = k.mm(bc[:, :], ones1[:, :], wrowb[0:1, 0:512],
                          deps=[t_wrb, wbcR])
                eb2 = k.copy(wbc[:, 0:512], bc[:, :], deps=[tb, rinR])
                wrow_prev[0] = tb
                wbcR = eb2
            g_toks = [None] * 8
            for hh in range(2):
                wa = sh_gate if shared else r_gate[e - 1]
                s1_, lt = wload(wa[:, hh * 512:(hh + 1) * 512], 512)
                for hc in range(4):
                    hix = hh * 4 + hc
                    bsl = hc % 2
                    bank = st4[:, bsl * 512: bsl * 512 + 512]
                    pairs = [(wpool[s1_][:, kc * 512 + hc * 128:
                                       kc * 512 + hc * 128 + 128],
                              xn2[:, kc * TPC + tof: kc * TPC + tof + 512])
                             for kc in range(NCH)]
                    mt = k.mm_group(bank, pairs,
                                    deps=[lt, bankR4[bsl]] + k.p.fence_toks())
                    gt = k.act(gs[:, hix * 512: hix * 512 + 512],
                               bank, AF.Silu, deps=[mt, h_prev[hix],
                                                    wt_toks[-1]])
                    g_toks[hix] = gt
                    bankR4[bsl] = gt
                    wuse[s1_] = mt
            h_toks2 = [None] * 8
            for hh in range(2):
                wa = sh_up if shared else r_up[e - 1]
                s1_, lt = wload(wa[:, hh * 512:(hh + 1) * 512], 512)
                for hc in range(4):
                    hix = hh * 4 + hc
                    bsl = 2 + hc % 2
                    bank = st4[:, bsl * 512: bsl * 512 + 512]
                    pairs = [(wpool[s1_][:, kc * 512 + hc * 128:
                                       kc * 512 + hc * 128 + 128],
                              xn2[:, kc * TPC + tof: kc * TPC + tof + 512])
                             for kc in range(NCH)]
                    mt = k.mm_group(bank, pairs,
                                    deps=[lt, bankR4[bsl]] + k.p.fence_toks())
                    if shared:
                        ut = k.ts(us[:], bank, 0.125, ALU.mult, deps=[mt])
                    else:
                        ut = k.tt(us[:], bank, wbc[:, 0:512], ALU.mult,
                                  deps=[mt, wbcR])
                    ht = k.tt(hb[:, hix * 512: hix * 512 + 512],
                              us[:], gs[:, hix * 512: hix * 512 + 512],
                              ALU.mult, deps=[ut, g_toks[hix],
                                              wt_toks[-1]])
                    h_toks2[hix] = ht
                    bankR4[bsl] = ht
                    wuse[s1_] = mt
                    rinR = mt
            h_prev = h_toks2
            for fh in range(2):
                wa = sh_down if shared else r_down[e - 1]
                s1_, lt = wload(wa[:, fh * 1024:(fh + 1) * 1024], 1024)
                for fc in range(8):
                    fch = fh * 8 + fc
                    bsl = fc % 2
                    bank = st4[:, bsl * 512: bsl * 512 + 512]
                    pairs = [(wpool[s1_][:, hc * 1024 + fc * 128:
                                       hc * 1024 + fc * 128 + 128],
                              hb[:, hc * 512:(hc + 1) * 512])
                             for hc in range(8)]
                    mt = k.mm_group(bank, pairs,
                                    deps=[lt, bankR4[bsl]] + k.p.fence_toks())
                    at = k.tt(acc[:, fch * 512: fch * 512 + 512],
                              acc[:, fch * 512: fch * 512 + 512],
                              bank, ALU.add, deps=[mt, acc_tok[fch]])
                    acc_tok[fch] = at
                    bankR4[bsl] = at
                    wuse[s1_] = mt
        s1 = st4[0:1, 1024:1536]
        s2 = st4[0:1, 1536:2048]
        t_s1 = lnstat_prev[0]
        t_s2 = lnstat_prev[1]
        sq_tok = None
        pb_tok = None
        for kc in range(NCH):
            seg = acc[:, kc * 512:(kc + 1) * 512]
            pb_tok = k.copy(preb[:], seg, deps=[acc_tok[kc], t_s1, bankR[2]])
            sq_tok = k.act(sqs[:], seg, AF.Square, deps=[acc_tok[kc], t_s2,
                                                         bankR[3]])
            t_s1 = k.mm(s1, onescol[:], preb[:],
                        deps=[pb_tok, tok_onesc], start=(kc == 0),
                        stop=(kc == NCH - 1), inc=True)
            t_s2 = k.mm(s2, onescol[:], sqs[:],
                        deps=[sq_tok], start=(kc == 0),
                        stop=(kc == NCH - 1), inc=True)
        lnstat_prev = [t_s1, t_s2]
        bankR[2] = t_s1
        bankR[3] = t_s2
        t_m = k.ts(rowm[0:1, 0:512], s1, 1.0 / D, ALU.mult,
                   deps=[t_s1, wrow_prev[0]])
        t_v = k.ts(rowv, s2, 1.0 / D, ALU.mult, deps=[t_s2])
        t_m2 = k.tt(tmpr0, rowm[0:1, 0:512], rowm[0:1, 0:512], ALU.mult, deps=[t_m])
        t_v2 = k.tt(rowv, rowv, tmpr0, ALU.subtract,
                    deps=[t_v, t_m2])
        t_sd = k.act(tmpr0, rowv, AF.Sqrt, deps=[t_v2, tok_eps],
                     bias=epst[0:1, 0:1], scale=1.0)
        t_r = k.recip(rowa, tmpr0, deps=[t_sd])
        t_ab = k.copy(rowab[0:1, 0:512], rowa, deps=[t_r])
        t_bb = k.tt(tmpr0, rowm[0:1, 0:512], rowa, ALU.mult, deps=[t_r, t_ab])
        t_bn = k.ts(tmpr0, tmpr0, -1.0, ALU.mult, deps=[t_bb])
        t_bbb = k.copy(rowbb[:], tmpr0, deps=[t_bn])
        tba = k.mm(bc[:, :], ones1[:, :], rowab[0:1, 0:512], deps=[t_ab, tok_ones,
                                                          wbcR])
        ebA = k.copy(lbc, bc[:, :], deps=[tba])
        tbb = k.mm(bc[:, :], ones1[:, :], rowbb[:], deps=[t_bbb, ebA])
        tm2 = None
        tm2s = []
        for kc in range(NCH):
            seg = acc[:, kc * 512:(kc + 1) * 512]
            tm1 = k.tt(tmp, seg, lbc, ALU.mult,
                       deps=[tba, ebA, sq_tok, pb_tok, tm2])
            tm2 = k.tt(seg, tmp, bc[:, :], ALU.add, deps=[tm1, tbb])
            tm2s.append(tm2)
        vf = k.vflush([tm2s[-1]])
        for kc in range(NCH):
            ot = k.dma(x2fm[kc * 128:(kc + 1) * 128, tof:tof + 512],
                       acc[:, kc * 512:(kc + 1) * 512], deps=[tm2s[kc], vf])
            outprev[kc] = ot
            out_toks.append(ot)
        wbcR = tbb
        wrow_prev[0] = t_bbb
    return k.finish(out_toks)


def stage_a4(core, o_fm_slice, x1_fm_shard, inputs):
    return {
        "o_in": o_fm_slice, "x1fm": x1_fm_shard,
        "o_w": inputs["L0_ffn_o_w"],
        "ln2_pc": _chunk_cols(inputs["L0_ffn_ln2_w"]),
        "router_w": inputs["L0_ffn_router_w"],
        "router_b": _row(inputs["L0_ffn_router_b"]),
        "sh_gate": inputs["L0_ffn_sh_gate"], "sh_up": inputs["L0_ffn_sh_up"],
        "sh_down": inputs["L0_ffn_sh_down"],
        "r_gate": inputs["L0_ffn_r_gate"], "r_up": inputs["L0_ffn_r_up"],
        "r_down": inputs["L0_ffn_r_down"],
        "identf": np.eye(128, dtype=np.float32),
    }


_NC_CACHE = {}


def _get_nc(name):
    if name not in _NC_CACHE:
        _NC_CACHE[name] = globals()["build_" + name]()
    return _NC_CACHE[name]


def kernel(**inputs):
    cores = list(range(NCORES))
    x0 = np.asarray(inputs["L0_attn_x"], dtype=np.float32)

    def run(name, maps):
        return run_bass_kernel_spmd(_get_nc(name), maps, core_ids=cores)

    def shard_fm(full):
        out = []
        for c in range(NCORES):
            b, r = divmod(c, 4)
            out.append(_fm(full[b, r * TPC:(r + 1) * TPC]))
        return out

    r1 = run("a1", [stage_a1(c, shard_fm(x0)[c], inputs) for c in cores])
    x1fm = [r1.results[c]["x1fm"] for c in cores]

    r2 = run("a2", [stage_a2(c, x1fm[c], inputs) for c in cores])

    maps3 = []
    kfull = {}
    qfull = {}
    vfull = {}
    for b in range(2):
        kfull[b] = np.concatenate(
            [r2.results[4 * b + j]["k_tm"] for j in range(4)], axis=0)
        qfull[b] = np.concatenate(
            [r2.results[4 * b + j]["q_tm"] for j in range(4)], axis=0)
        vfull[b] = np.concatenate(
            [r2.results[4 * b + j]["v_tm"] for j in range(4)], axis=0)
    for c in cores:
        b, r = divmod(c, 4)
        hs = slice(r * 512, (r + 1) * 512)
        maps3.append(stage_a3(
            c, np.ascontiguousarray(kfull[b][:, hs]),
            np.ascontiguousarray(qfull[b][:, hs]),
            np.ascontiguousarray(vfull[b][:, hs])))
    r3 = run("a3", maps3)

    maps4 = []
    for c in cores:
        b, r = divmod(c, 4)
        o_full = np.concatenate(
            [r3.results[4 * b + j]["o_fm"] for j in range(4)], axis=0)
        o_slice = np.ascontiguousarray(o_full[:, r * TPC:(r + 1) * TPC])
        maps4.append(stage_a4(c, o_slice, x1fm[c], inputs))
    r4 = run("a4", maps4)
    x2fm = [r4.results[c]["x2fm"] for c in cores]

    r5 = run("a5", [stage_a5(c, x2fm[c], inputs) for c in cores])

    maps6 = []
    for c in cores:
        b, r = divmod(c, 4)
        ktm_full = np.concatenate(
            [r5.results[4 * b + j]["k_tm"] for j in range(4)], axis=0)
        vtm_full = np.concatenate(
            [r5.results[4 * b + j]["v_tm"] for j in range(4)], axis=0)
        maps6.append(stage_a6(c, r5.results[c]["q_fm"], ktm_full, vtm_full,
                              x2fm[c], inputs))
    r6 = run("a6", maps6)
    x3fm = [r6.results[c]["x3fm"] for c in cores]

    r7 = run("a7", [stage_a7(c, x3fm[c], inputs) for c in cores])

    out = np.empty((B, N, D), dtype=np.float32)
    for c in cores:
        b, r = divmod(c, 4)
        out[b, r * TPC:(r + 1) * TPC] = r7.results[c]["yout"]
    return out


# version 02
# speedup vs baseline: 1.1949x; 1.1949x over previous
import numpy as np
from contextlib import ExitStack

import concourse.bass as bass
import concourse.mybir as mybir
from concourse.bass_utils import run_bass_kernel_spmd

F32 = mybir.dt.float32
BF16 = mybir.dt.bfloat16
AF = mybir.ActivationFunctionType
ALU = mybir.AluOpType
AX = mybir.AxisListType

NCORES = 8
B = 2
N = 4096
D = 2048
TPC = 1024
NCH = D // 128


class Prog:
    ENGINES = ("tensor", "vector", "scalar", "gpsimd", "sync")

    SERIAL = False

    def __init__(self, nc, es):
        self.nc = nc
        self.es = es
        self.sems = {e: es.enter_context(nc.semaphore("sem_" + e))
                     for e in self.ENGINES}
        self.cnt = {e: 0 for e in self.ENGINES}
        self.stream = {e: [] for e in self.ENGINES}
        self.hwm = {}
        self.extra_sems = []

    def do(self, e, fn, deps=(), dma=False, inc=True):
        if Prog.SERIAL:
            deps = list(deps) + [(e2, self.cnt[e2]) for e2 in self.ENGINES
                                 if self.cnt[e2] > 0]
        waits = []
        for tok in deps:
            if tok is None:
                continue
            se, v = tok
            if isinstance(se, str):
                key = (e, se)
                if self.hwm.get(key, 0) >= v:
                    continue
                self.hwm[key] = v
                waits.append((self.sems[se], v))
            else:
                key = (e, id(se))
                if self.hwm.get(key, 0) >= v:
                    continue
                self.hwm[key] = v
                waits.append((se, v))
        step = 16 if dma else 1
        tok = None
        incs = []
        if inc:
            self.cnt[e] += step
            tok = (e, self.cnt[e])
            incs = [(self.sems[e], step)]
        self.stream[e].append((waits, fn, incs))
        return tok

    def fence_toks(self):
        return [(e, self.cnt[e]) for e in self.ENGINES if self.cnt[e] > 0]

    def raw_sem(self, name):
        s = self.es.enter_context(self.nc.semaphore(name))
        self.extra_sems.append(s)
        return s

    def emit(self):
        nc = self.nc
        with nc.Block() as block:
            def mk(e):
                def body(eng):
                    for waits, fn, incs in self.stream[e]:
                        for sem, v in waits:
                            eng.wait_ge(sem, v)
                        inst = fn(eng)
                        for sem, n in incs:
                            inst.then_inc(sem, n)
                return body
            block.tensor(mk("tensor"))
            block.vector(mk("vector"))
            block.scalar(mk("scalar"))
            block.gpsimd(mk("gpsimd"))
            block.sync(mk("sync"))


class KB:

    def __init__(self, name):
        self.nc = bass.Bass("TRN2", target_bir_lowering=False)
        self.es = ExitStack()
        self.p = Prog(self.nc, self.es)
        self.sbuf_bytes = 0
        self._uid = 0
        self._flt = self.es.enter_context(
            self.nc.sbuf_tensor("_flush", [1, 8], F32))

    def uid(self):
        self._uid += 1
        return self._uid

    def dram(self, name, shape, dtype=F32, kind="ExternalInput"):
        return self.nc.dram_tensor(name, shape, dtype, kind=kind).ap()

    def dram_internal(self, name, shape, dtype=F32):
        return self.nc.dram_tensor(name, shape, dtype)

    def sb(self, name, shape, dtype=F32):
        t = self.es.enter_context(self.nc.sbuf_tensor(name, shape, dtype))
        esize = 2 if dtype == BF16 else 4
        self.sbuf_bytes += shape[0] * int(np.prod(shape[1:])) * esize
        return t

    def ps(self, name, shape, dtype=F32):
        return self.es.enter_context(self.nc.psum_tensor(name, shape, dtype))

    def dma(self, dst, src, deps=(), q="sync"):
        return self.p.do(q, lambda e: e.dma_start(out=dst, in_=src),
                         deps=deps, dma=True)

    def dma_t(self, dst, src, deps=(), q="sync"):
        return self.p.do(q, lambda e: e.dma_start_transpose(dst, src),
                         deps=deps, dma=True)

    def loadw(self, dst, src, deps=()):
        return self.p.do("gpsimd", lambda e: e.dma_start(out=dst, in_=src),
                         deps=deps, dma=True)

    def mm(self, out, lhsT, rhs, deps=(), start=True, stop=True, inc=True,
           sgc=False):
        return self.p.do(
            "tensor",
            lambda e: e.matmul(out, lhsT, rhs, start=start, stop=stop,
                               skip_group_check=sgc),
            deps=deps, inc=inc)

    def mm_group(self, out, pairs, deps=(), extra=()):
        allp = list(pairs) + list(extra)
        tok = None
        for i, (l, r) in enumerate(allp):
            tok = self.mm(out, l, r,
                          deps=deps if i == 0 else (),
                          start=(i == 0), stop=(i == len(allp) - 1),
                          inc=(i == len(allp) - 1))
        return tok

    def transpose(self, out_psum, in_sbuf, ident, deps=(), inc=True):
        return self.p.do(
            "tensor", lambda e: e.transpose(out_psum, in_sbuf, ident),
            deps=deps, inc=inc)

    def copy(self, dst, src, deps=(), eng="vector"):
        return self.p.do(eng, lambda e: e.tensor_copy(dst, src), deps=deps)

    def tt(self, dst, a, b, op, deps=(), eng="vector"):
        return self.p.do(eng, lambda e: e.tensor_tensor(dst, a, b, op=op),
                         deps=deps)

    def ts(self, dst, a, scal, op, deps=(), eng="vector", accum=None):
        return self.p.do(
            eng,
            lambda e: e.tensor_scalar(dst, a, scal, None, op, accum_out=accum),
            deps=deps)

    def stt(self, dst, in0, scal, in1, op0, op1, deps=(), eng="vector"):
        return self.p.do(
            eng,
            lambda e: e.scalar_tensor_tensor(dst, in0, scal, in1, op0, op1),
            deps=deps)

    def act(self, dst, src, func, deps=(), bias=None, scale=None, accum=None):
        kw = {}
        if bias is not None:
            kw["bias"] = bias
        if scale is not None:
            kw["scale"] = scale
        if accum is not None:
            kw["accum_out"] = accum
        return self.p.do(
            "scalar", lambda e: e.activation(out=dst, in_=src, func=func, **kw),
            deps=deps)

    def reduce(self, dst, src, op, deps=(), negate=False, eng="vector"):
        return self.p.do(
            eng,
            lambda e: e.tensor_reduce(dst, src, axis=AX.X, op=op,
                                      negate=negate),
            deps=deps)

    def recip(self, dst, src, deps=()):
        return self.p.do("vector", lambda e: e.reciprocal(dst, src), deps=deps)

    def memset(self, ap, val, deps=(), eng="vector"):
        return self.p.do(eng, lambda e: e.memset(ap, val), deps=deps)

    def vflush(self, deps):
        return self.p.do("vector", lambda e: e.memset(self._flt[:], 0.0),
                         deps=deps)

    def sflush(self, deps):
        return self.p.do("scalar", lambda e: e.memzero(self._flt[0:1, 0:4]),
                         deps=deps)

    def collective(self, kind, op, groups, ins, outs, deps=()):
        sem = self.p.raw_sem("cc%d" % self.uid())

        def fn(e):
            inst = e.collective_compute(
                kind, op, replica_groups=groups, ins=ins, outs=outs)
            inst.then_inc(sem)
            return inst
        self.p.do("gpsimd", fn, deps=deps, inc=False)
        return (sem, 1)

    def finish(self, out_dma_tokens):
        for tok in out_dma_tokens:
            if tok is None:
                continue
            se, _ = tok
            eng = se if isinstance(se, str) else "sync"
            self.p.do(eng, lambda e: e.nop(), deps=[tok], inc=False)
        self.p.emit()
        self.es.close()
        return self.nc


def _fm(x2d):
    return np.ascontiguousarray(x2d.T)


def _chunk_cols(v):
    return np.ascontiguousarray(v.reshape(-1, 128).T.astype(np.float32))


def _row(v):
    return np.ascontiguousarray(v.reshape(1, -1).astype(np.float32))


def _ident_bf16():
    import ml_dtypes
    return np.eye(128, dtype=ml_dtypes.bfloat16)


def _shards(full):
    out = []
    for c in range(NCORES):
        b, r = divmod(c, 4)
        out.append(np.ascontiguousarray(full[b, r * TPC:(r + 1) * TPC]))
    return out



def _r3(dram_ap, p=128):
    return dram_ap.rearrange("(c p) t -> p c t", p=p)


def _s3(tile_ap, t):
    return tile_ap.rearrange("p (c t) -> p c t", t=t)

def build_a5():
    k = KB("a5")
    nc = k.nc
    x2t = k.dram("x2t", [D, TPC], F32)
    Wq = k.dram("Wq", [D, D], F32)
    Wk = k.dram("Wk", [D, D], F32)
    Wv = k.dram("Wv", [D, D], F32)
    bq_pc = k.dram("bq_pc", [128, NCH], F32)
    bk_row = k.dram("bk_row", [1, D], F32)
    bv_row = k.dram("bv_row", [1, D], F32)
    q_fm = k.dram("q_fm", [D, TPC], BF16, kind="ExternalOutput")
    k_tm = k.dram("k_tm", [TPC, D], BF16, kind="ExternalOutput")
    v_tm = k.dram("v_tm", [TPC, D], BF16, kind="ExternalOutput")

    xb = k.sb("xb", [128, NCH * TPC], BF16)
    qs = k.sb("qs", [128, NCH * TPC], BF16)
    ks = k.sb("ks", [128, 8 * D], BF16)
    vs = k.sb("vs", [128, 8 * D], BF16)
    wblk = [k.sb("wblk%d" % i, [128, NCH * 512], BF16) for i in range(2)]
    bqs = k.sb("bqs", [128, NCH], F32)
    bks = k.sb("bks", [1, D], BF16)
    bvs = k.sb("bvs", [1, D], BF16)
    ones1 = k.sb("ones1", [1, 128], BF16)
    banks = [k.ps("bank%d" % i, [128, 512], F32) for i in range(4)]

    tok_x = k.loadw(_s3(xb[:], TPC), _r3(x2t))
    tok_bq = k.dma(bqs[:], bq_pc[:, :])
    tok_bk = k.loadw(bks[:], bk_row[:, :])
    tok_bv = k.loadw(bvs[:], bv_row[:, :])
    tok_ones = k.memset(ones1[:], 1.0)

    out_toks = []

    def wload(W, blk, buf, deps):
        src = _r3(W[:, blk * 512:(blk + 1) * 512])
        return k.loadw(_s3(wblk[buf][:], 512), src, deps=deps)

    wtoks = [None, None]
    lastuse = [None, None]
    bank_last = [None] * 4

    wtoks[0] = wload(Wq, 0, 0, [])
    seq = []
    for blk in range(4):
        seq.append(("q", Wq, blk))
    for blk in range(4):
        seq.append(("k", Wk, blk))
    for blk in range(4):
        seq.append(("v", Wv, blk))

    for i, (kind, W, blk) in enumerate(seq):
        buf = i % 2
        if i + 1 < len(seq):
            nk, nW, nblk = seq[i + 1]
            wtoks[(i + 1) % 2] = wload(nW, nblk, (i + 1) % 2,
                                       [lastuse[(i + 1) % 2]])
        wt = wtoks[buf]
        wb = wblk[buf]
        if kind == "q":
            for j in range(4):
                fch = blk * 4 + j
                for th in range(2):
                    bank = (j * 2 + th) % 4
                    pairs = [(wb[:, kc * 512 + j * 128: kc * 512 + j * 128 + 128],
                              xb[:, kc * TPC + th * 512: kc * TPC + th * 512 + 512])
                             for kc in range(NCH)]
                    mt = k.mm_group(banks[bank][:], pairs,
                                    deps=[wt, tok_x, bank_last[bank]])
                    et = k.act(qs[:, fch * TPC + th * 512:
                                   fch * TPC + th * 512 + 512],
                               banks[bank][:], AF.Identity,
                               deps=[mt, tok_bq],
                               bias=bqs[:, fch:fch + 1], scale=1.0)
                    bank_last[bank] = et
                    lastuse[buf] = mt
        else:
            dst = ks if kind == "k" else vs
            brow = bks if kind == "k" else bvs
            btok = tok_bk if kind == "k" else tok_bv
            for jt in range(8):
                bank = jt % 4
                pairs = [(xb[:, kc * TPC + jt * 128: kc * TPC + jt * 128 + 128],
                          wb[:, kc * 512:(kc + 1) * 512])
                         for kc in range(NCH)]
                extra = [(ones1[:, :],
                          brow[:, blk * 512:(blk + 1) * 512])]
                mt = k.mm_group(banks[bank][:], pairs, extra=extra,
                                deps=[wt, tok_x, btok, tok_ones,
                                      bank_last[bank]])
                et = k.copy(dst[:, jt * D + blk * 512: jt * D + blk * 512 + 512],
                            banks[bank][:], deps=[mt])
                bank_last[bank] = et
                lastuse[buf] = mt

    vf5 = k.vflush([t for t in bank_last])
    sf5 = k.sflush([t for t in bank_last])
    out_toks.append(k.dma(_r3(q_fm), _s3(qs[:], TPC),
                          deps=[t for t in bank_last] + [vf5, sf5]))
    out_toks.append(k.dma(_r3(k_tm), _s3(ks[:], D),
                          deps=[t for t in bank_last] + [vf5, sf5]))
    out_toks.append(k.dma(_r3(v_tm), _s3(vs[:], D),
                          deps=[t for t in bank_last] + [vf5, sf5]))
    return k.finish(out_toks)


def stage_a5(core, x2_fm_shard, inputs):
    return {
        "x2t": x2_fm_shard,
        "Wq": inputs["L1_attn_Wq"], "Wk": inputs["L1_attn_Wk"],
        "Wv": inputs["L1_attn_Wv"],
        "bq_pc": _chunk_cols(inputs["L1_attn_bq"]),
        "bk_row": _row(inputs["L1_attn_bk"]),
        "bv_row": _row(inputs["L1_attn_bv"]),
    }


def build_a6():
    k = KB("a6")
    q_fm = k.dram("q_fm", [D, TPC], BF16)
    ktm = k.dram("ktm", [N, D], BF16)
    vtm = k.dram("vtm", [N, D], BF16)
    x2fm = k.dram("x2fm", [D, TPC], F32)
    Wo = k.dram("Wo", [D, D], F32)
    bo_pc = k.dram("bo_pc", [128, NCH], F32)
    x3fm = k.dram("x3fm", [D, TPC], F32, kind="ExternalOutput")

    qsb = [k.sb("qsb%d" % i, [128, TPC], BF16) for i in range(2)]
    kf = [k.sb("kf%d" % i, [128, N], BF16) for i in range(2)]
    vta = [k.sb("vta%d" % i, [128, 32 * 130], BF16) for i in range(2)]
    pt = [k.sb("pt%d" % i, [128, 1024], BF16) for i in range(4)]
    of = k.sb("of", [128, NCH * TPC], BF16)
    pre = k.sb("pre", [128, NCH * TPC], F32)
    wblk = [k.sb("wblk%d" % i, [128, NCH * 512], BF16) for i in range(2)]
    x2ch = [k.sb("x2ch%d" % i, [128, TPC], F32) for i in range(2)]
    bos = k.sb("bos", [128, NCH], F32)
    ones1 = k.sb("ones1", [1, 128], BF16)
    onescol = k.sb("onescol", [128, 1], BF16)
    linv = k.sb("linv", [1, 512], F32)
    linvb = k.sb("linvb", [1, 512], BF16)
    lbc = k.sb("lbc", [128, 512], F32)
    sqs = k.sb("sqs", [128, 512], BF16)
    preb = k.sb("preb", [128, 512], BF16)
    rowm = k.sb("rowm", [1, 512], F32)
    rowa = k.sb("rowa", [1, 512], F32)
    rowv = k.sb("rowv", [1, 512], F32)
    rowab = k.sb("rowab", [1, 512], BF16)
    rowbb = k.sb("rowbb", [1, 512], BF16)
    tmp = k.sb("tmp", [128, 512], F32)

    st4 = k.ps("st4", [128, 2048], F32)
    oA = k.ps("oA", [128, 512], F32)
    oB = k.ps("oB", [128, 512], F32)
    bc = k.ps("bc", [128, 512], F32)

    tok_ones = k.memset(ones1[:], 1.0)
    tok_onesc = k.memset(onescol[:], 1.0)
    tok_bo = k.dma(bos[:], bo_pc[:, :])
    epst = k.sb("epst", [1, 1], F32)
    tok_eps = k.memset(epst[:], 1e-5)
    vset = [k.memset(vta[i][:], 1.0) for i in range(2)]

    kf_tok = [None, None]
    q_tok = [None, None]
    vta_tok = [None, None]
    kf_use = [None, None]
    q_use = [None, None]
    vta_use = [None, None]

    def load_pair(ch, buf):
        c0 = ch * 128
        t1 = k.dma_t(kf[buf][:], ktm[:, c0:c0 + 128], deps=[kf_use[buf]])
        t2 = k.dma(qsb[buf][:], q_fm[c0:c0 + 128, :], deps=[q_use[buf]])
        dst = vta[buf][:].rearrange("p (j c) -> p j c", c=130)
        t3a = k.dma(dst[:, :, 0:64],
                    vtm[:, ch * 128:ch * 128 + 64].rearrange(
                        "(j p) f -> p j f", p=128),
                    deps=[vta_use[buf], vset[buf]], q="gpsimd")
        t3b = k.dma(dst[:, :, 65:129],
                    vtm[:, ch * 128 + 64:ch * 128 + 128].rearrange(
                        "(j p) f -> p j f", p=128),
                    deps=[vta_use[buf], vset[buf]], q="gpsimd")
        return t1, t2, (t3a, t3b)

    kf_tok[0], q_tok[0], vta_tok[0] = load_pair(0, 0)

    exp_tok = [None, None]
    pt_tok = [None] * 4
    oA_last = None
    oB_last = None
    ep_done = []

    for ch in range(16):
        buf = ch % 2
        if ch + 1 < 16:
            nb = (ch + 1) % 2
            kf_tok[nb], q_tok[nb], vta_tok[nb] = load_pair(ch + 1, nb)
        vA = vta[buf][:].rearrange("p (j c) -> p j c", c=130)
        for qm in range(2):
            pv_a = None
            pv_b = None
            equeue = []
            for jt in range(32 + 3):
                if jt < 32:
                    half = jt % 2
                    pslot = jt % 4
                    scol = half * 1024
                    dep0 = [kf_tok[buf], q_tok[buf], exp_tok[half],
                            oA_last if jt == 0 else None,
                            oB_last if jt == 0 else None]
                    mtA = k.mm(st4[:, scol:scol + 512],
                               kf[buf][0:64, jt * 128:(jt + 1) * 128],
                               qsb[buf][0:64, qm * 512:(qm + 1) * 512],
                               deps=dep0, inc=False)
                    mtB = k.mm(st4[:, scol + 512:scol + 1024],
                               kf[buf][64:128, jt * 128:(jt + 1) * 128],
                               qsb[buf][64:128, qm * 512:(qm + 1) * 512],
                               deps=[])
                    e = k.act(pt[pslot][:], st4[:, scol:scol + 1024], AF.Exp,
                              deps=[mtB, pt_tok[pslot]],
                              scale=float(1.0 / np.sqrt(32.0)))
                    exp_tok[half] = e
                    equeue.append((jt, pslot, e))
                if equeue and (jt >= 32 or len(equeue) > 2):
                    pj, pslot_, pe = equeue.pop(0)
                    pe_gate = equeue[0][2] if equeue else k.sflush([pe])
                    pv_a = k.mm(oA[0:65, :], vA[:, pj, 0:65],
                                pt[pslot_][:, 0:512],
                                deps=[pe, pe_gate, vta_tok[buf][0],
                                      vta_tok[buf][1],
                                      oA_last if pj == 0 else None],
                                start=(pj == 0), stop=(pj == 31),
                                inc=(pj == 31))
                    pv_b = k.mm(oB[0:65, :], vA[:, pj, 65:130],
                                pt[pslot_][:, 512:1024],
                                deps=[oB_last if pj == 0 else None],
                                start=(pj == 0), stop=(pj == 31),
                                inc=(pj == 31))
                    pt_tok[pslot_] = pv_b
            for s, (bank, pv) in enumerate(((oA, pv_a), (oB, pv_b))):
                t1 = k.recip(linv[:], bank[64:65, :], deps=[pv, ep_done[-1] if ep_done else None])
                t2 = k.copy(linvb[:], linv[:], deps=[t1])
                t3 = k.mm(bc[0:64, :], ones1[:, 0:64], linvb[:], deps=[t2, tok_ones])
                t4 = k.copy(lbc[0:64, :], bc[0:64, :], deps=[t3])
                t5 = k.tt(of[64 * s:64 * s + 64,
                             ch * TPC + qm * 512: ch * TPC + qm * 512 + 512],
                          bank[0:64, :], lbc[0:64, :], ALU.mult, deps=[t4])
                ep_done.append(t5)
                if s == 0:
                    oA_last = t5
                else:
                    oB_last = t5
        kf_use[buf] = exp_tok[0]
        q_use[buf] = exp_tok[1] or exp_tok[0]
        vta_use[buf] = ep_done[-1]

    wtoks = [None, None]
    lastw = [None, None]
    x2tok = [None, None]
    x2use = [None, None]

    def wload(blk, buf):
        src = _r3(Wo[:, blk * 512:(blk + 1) * 512])
        return k.loadw(_s3(wblk[buf][:], 512), src, deps=[lastw[buf]])

    vf_of = k.vflush([ep_done[-1]])
    wtoks[0] = wload(0, 0)
    bankR = {0: None, 1: None}
    pre_toks = []
    for blk in range(4):
        buf = blk % 2
        if blk + 1 < 4:
            wtoks[(blk + 1) % 2] = wload(blk + 1, (blk + 1) % 2)
        for j in range(4):
            fch = blk * 4 + j
            xb2 = j % 2
            xt = k.dma(x2ch[xb2][:], x2fm[fch * 128:(fch + 1) * 128, :],
                       deps=[x2use[xb2]])
            for th in range(2):
                bank = st4[:, (j % 2) * 512:(j % 2) * 512 + 512]
                pairs = [(wblk[buf][:, kc * 512 + j * 128: kc * 512 + j * 128 + 128],
                          of[:, kc * TPC + th * 512: kc * TPC + th * 512 + 512])
                         for kc in range(NCH)]
                mt = k.mm_group(bank, pairs,
                                deps=[wtoks[buf], exp_tok[0], exp_tok[1],
                                      ep_done[-1], vf_of, bankR[j % 2]])
                lastw[buf] = mt
                et = k.stt(pre[:, fch * TPC + th * 512: fch * TPC + th * 512 + 512],
                           bank, bos[:, fch:fch + 1], x2ch[xb2][:, th * 512:th * 512 + 512],
                           ALU.add, ALU.add, deps=[mt, xt, tok_bo])
                bankR[j % 2] = et
                x2use[xb2] = et
                pre_toks.append(et)

    out_toks = []
    t_s1 = None
    t_s2 = None
    for th in range(2):
        s1 = st4[0:1, 1024:1536]
        s2 = st4[0:1, 1536:2048]
        sq_tok = None
        pb_tok = None
        for kc in range(NCH):
            seg = pre[:, kc * TPC + th * 512: kc * TPC + th * 512 + 512]
            pb_tok = k.copy(preb[:], seg, deps=[pre_toks[-1], t_s1])
            sq_tok = k.act(sqs[:], seg, AF.Square, deps=[pre_toks[-1], t_s2])
            t_s1 = k.mm(s1, onescol[:], preb[:],
                        deps=[pb_tok, tok_onesc], start=(kc == 0),
                        stop=(kc == NCH - 1), inc=True)
            t_s2 = k.mm(s2, onescol[:], sqs[:],
                        deps=[sq_tok], start=(kc == 0),
                        stop=(kc == NCH - 1), inc=True)
        t_m = k.ts(rowm[:], s1, 1.0 / D, ALU.mult, deps=[t_s1])
        t_v = k.ts(rowv[:], s2, 1.0 / D, ALU.mult, deps=[t_s2])
        t_m2 = k.tt(tmp[0:1, :], rowm[:], rowm[:], ALU.mult, deps=[t_m])
        t_v2 = k.tt(rowv[:], rowv[:], tmp[0:1, :], ALU.subtract,
                    deps=[t_v, t_m2])
        t_sd = k.act(tmp[0:1, :], rowv[:], AF.Sqrt, deps=[t_v2, tok_eps],
                     bias=epst[0:1, 0:1], scale=1.0)
        t_r = k.recip(rowa[:], tmp[0:1, :], deps=[t_sd])
        t_ab = k.copy(rowab[:], rowa[:], deps=[t_r])
        t_bb = k.tt(tmp[0:1, :], rowm[:], rowa[:], ALU.mult, deps=[t_r, t_ab])
        t_bn = k.ts(tmp[0:1, :], tmp[0:1, :], -1.0, ALU.mult, deps=[t_bb])
        t_bbb = k.copy(rowbb[:], tmp[0:1, :], deps=[t_bn])
        tba = k.mm(bc[:, :], ones1[:, :], rowab[:], deps=[t_ab, tok_ones])
        ebA = k.copy(lbc[:, :], bc[:, :], deps=[tba])
        tbb = k.mm(bc[:, :], ones1[:, :], rowbb[:], deps=[t_bbb, ebA])
        tm2 = None
        tm2s = []
        for kc in range(NCH):
            seg = pre[:, kc * TPC + th * 512: kc * TPC + th * 512 + 512]
            tm1 = k.tt(tmp[:, :], seg, lbc[:, :], ALU.mult,
                       deps=[tba, ebA, t_s1, t_s2, sq_tok, pb_tok, tm2])
            tm2 = k.tt(seg, tmp[:, :], bc[:, :], ALU.add, deps=[tm1, tbb])
            tm2s.append(tm2)
        vf = k.vflush([tm2s[-1]])
        for kc in range(NCH):
            seg = pre[:, kc * TPC + th * 512: kc * TPC + th * 512 + 512]
            ot = k.dma(x3fm[kc * 128:(kc + 1) * 128,
                            th * 512:th * 512 + 512],
                       seg, deps=[tm2s[kc], vf])
            out_toks.append(ot)
    return k.finish(out_toks)


def stage_a6(core, q_fm, ktm_full, vtm_full, x2_fm_shard, inputs):
    return {
        "q_fm": q_fm, "ktm": ktm_full, "vtm": vtm_full, "x2fm": x2_fm_shard,
        "Wo": inputs["L1_attn_Wo"],
        "bo_pc": _chunk_cols(inputs["L1_attn_bo"]),
    }


DEBUG_A7 = False


def build_a7():
    k = KB("a7")
    x3fm = k.dram("x3fm", [D, TPC], F32)
    gwT = k.dram("gwT", [D, 8], F32)
    ebias_bc = k.dram("ebias_bc", [128, 8], F32)
    w1t = k.dram("w1t", [8, D, 1024], F32)
    w3t = k.dram("w3t", [8, D, 1024], F32)
    w2t = k.dram("w2t", [8, 1024, D], F32)
    sw1t = k.dram("sw1t", [D, 1024], F32)
    sw3t = k.dram("sw3t", [D, 1024], F32)
    sw2t = k.dram("sw2t", [1024, D], F32)
    identf = k.dram("identf", [128, 128], F32)
    yout = k.dram("yout", [TPC, D], F32, kind="ExternalOutput")
    wscr = k.dram_internal("wscr", [8, TPC], F32)
    wdbg = k.dram("wdbg", [8, TPC], F32, kind="ExternalOutput") if DEBUG_A7 else None

    x3b = k.sb("x3b", [128, NCH * TPC], BF16)
    gws = k.sb("gws", [128, NCH * 8], F32)
    biasb = k.sb("biasb", [128, 8], F32)
    idf = k.sb("idf", [128, 128], F32)
    ones1 = k.sb("ones1", [1, 128], BF16)
    onescol = k.sb("onescol", [128, 1], BF16)
    epst = k.sb("epst", [1, 1], F32)
    pr = k.sb("pr", [128, 8], F32)
    lfs = k.sb("lfs", [8, 1024], F32)
    spb = k.sb("spb", [128, 8], F32)
    m1n = k.sb("m1n", [128, 1], F32)
    spm = k.sb("spm", [128, 8], F32)
    mask1 = k.sb("mask1", [128, 8], F32)
    m2 = k.sb("m2", [128, 1], F32)
    selm = k.sb("selm", [128, 8], F32)
    wtt = k.sb("wtt", [128, 8], F32)
    wrow = k.sb("wrow", [1, 512], F32)
    wrowb = k.sb("wrowb", [1, 512], BF16)
    wbc = k.sb("wbc", [128, TPC], BF16)
    rin = k.sb("rin", [128, NCH * 512], BF16)
    gs = k.sb("gs", [128, 8 * 512], BF16)
    hb = k.sb("hb", [128, 8 * 512], BF16)
    wpool = [k.sb("wpool%d" % i, [128, 16 * 512], BF16) for i in range(3)]
    acc = k.sb("acc", [128, NCH * 512], F32)
    x3ch = k.sb("x3ch", [128, TPC], F32)
    ytm = k.sb("ytm", [128, 2048], F32)
    sqs = k.sb("sqs", [128, 512], BF16)
    preb = k.sb("preb", [128, 512], BF16)
    rowm = k.sb("rowm", [1, 512], F32)
    rowa = k.sb("rowa", [1, 512], F32)
    rowv = k.sb("rowv", [1, 512], F32)
    rowab = k.sb("rowab", [1, 512], BF16)
    rowbb = k.sb("rowbb", [1, 512], BF16)
    tmp = k.sb("tmp", [128, 512], F32)
    lbc = k.sb("lbc", [128, 512], F32)

    st4 = k.ps("st4", [128, 2048], F32)
    bc = k.ps("bc", [128, 512], F32)
    rt = k.ps("rt", [128, 64], F32)
    trp = k.ps("trp", [128, 128], F32)

    tok_ones = k.memset(ones1[:], 1.0)
    tok_onesc = k.memset(onescol[:], 1.0)
    tok_eps = k.memset(epst[:], 1e-5)
    tok_x3b = k.loadw(_s3(x3b[:], TPC), _r3(x3fm))
    tok_gws = k.dma(gws[:].rearrange("p (c e) -> p c e", e=8),
                    _r3(gwT))
    tok_bias = k.dma(biasb[:], ebias_bc[:, :])
    tok_idf = k.dma(idf[:], identf[:, :])

    wt_toks = []
    tr_last = None
    xck_prev = None
    mt_l = [None, None]
    for kc in range(NCH):
        xck = k.dma(x3ch[:], x3fm[kc * 128:(kc + 1) * 128, :],
                    deps=[xck_prev])
        for half in range(2):
            mt_l[half] = k.mm(st4[0:8, half * 512:(half + 1) * 512],
                              gws[:, kc * 8:(kc + 1) * 8],
                              x3ch[:, half * 512:(half + 1) * 512],
                              deps=[xck, tok_gws] if half == 0 else [],
                              start=(kc == 0), stop=(kc == NCH - 1),
                              inc=True)
        xck_prev = mt_l[1]
    ev_l = k.copy(lfs[0:8, :], st4[0:8, 0:1024], deps=[mt_l[0], mt_l[1]])
    t8_prev = None
    for jt in range(8):
        tt_l = k.transpose(trp[:, 0:8], lfs[0:8, jt * 128:(jt + 1) * 128],
                           idf[0:8, 0:8], deps=[ev_l, tok_idf, tr_last])
        t0 = k.act(pr[:], trp[:, 0:8], AF.Sigmoid, deps=[tt_l, t8_prev])
        t1 = k.tt(spb[:], pr[:], biasb[:], ALU.add, deps=[t0, tok_bias])
        t2 = k.reduce(m1n[:], spb[:], ALU.max, negate=True, deps=[t1])
        t3 = k.ts(spm[:], spb[:], m1n[:, 0:1], ALU.add, deps=[t2, t1])
        t4 = k.ts(mask1[:], spm[:], 0.0, ALU.is_ge, deps=[t3])
        t5 = k.stt(spm[:], mask1[:], 1e30, spb[:], ALU.mult, ALU.subtract,
                   deps=[t4])
        t6 = k.reduce(m2[:], spm[:], ALU.min, negate=True, deps=[t5])
        t7 = k.ts(selm[:], spb[:], m2[:, 0:1], ALU.is_ge, deps=[t6, t5])
        t8 = k.tt(wtt[:], pr[:], selm[:], ALU.mult, deps=[t7, t0])
        t8_prev = t8
        tt_ = k.transpose(trp[0:8, :], wtt[:], idf[:],
                          deps=[t8, tok_idf, tr_last])
        ev = k.copy(tmp[0:8, 0:128], trp[0:8, 0:128], deps=[tt_])
        tr_last = ev
        vfw = k.vflush([ev])
        wt_toks.append(k.dma(wscr.ap()[:, jt * 128:(jt + 1) * 128],
                             tmp[0:8, 0:128], deps=[ev, vfw]))
    wrow_prev = [None]
    dbg_toks = []
    if DEBUG_A7:
        dbg_toks.append(k.dma(wdbg, wscr.ap(), deps=wt_toks))


    out_toks = []
    wload_tok = [None, None, None]
    wuse_tok = [None, None, None]
    slot = [0]

    def wload(src_ap, inner):
        s = slot[0]
        slot[0] = (s + 1) % 3
        t = k.loadw(wpool[s][:].rearrange("p (c t) -> p c t", t=inner),
                    _r3(src_ap), deps=[wuse_tok[s]])
        wload_tok[s] = t
        return s, t

    lnstat_prev = [None, None]
    for th in range(2):
        tof = th * 512
        acc_tok = [None] * NCH
        xck_last = None
        for kc in range(NCH):
            xck = k.dma(x3ch[:], x3fm[kc * 128:(kc + 1) * 128, :],
                        deps=[acc_tok[kc - 1] if kc else
                              (out_toks[-1] if out_toks else None)])
            at = k.copy(acc[:, kc * 512:(kc + 1) * 512],
                        x3ch[:, tof:tof + 512], deps=[xck])
            acc_tok[kc] = at
            xck_last = at

        bankR = [None, None, None, None]
        rinR = None
        wbcR = None
        h_prev = [None] * 8
        for e in range(9):
            shared = (e == 8)
            if not shared:
                t_wr = k.dma(wrow[:], wscr.ap()[e:e + 1, tof:tof + 512],
                             deps=wt_toks + [wrow_prev[0]])
                t_wrb = k.copy(wrowb[:], wrow[:], deps=[t_wr, wrow_prev[0]])
                tb = k.mm(bc[:, :], ones1[:, :], wrowb[0:1, 0:512],
                          deps=[t_wrb, wbcR])
                wrow_prev[0] = tb
                eb = k.copy(wbc[:, 0:512], bc[:, :], deps=[tb, rinR])
                rt_toks = []
                for kc in range(NCH):
                    rt_ = k.tt(rin[:, kc * 512:(kc + 1) * 512],
                               x3b[:, kc * TPC + tof: kc * TPC + tof + 512],
                               wbc[:, 0:512], ALU.mult,
                               deps=[eb, tok_x3b, rinR])
                    rt_toks.append(rt_)
                wbcR = rt_toks[-1]
                rdeps = rt_toks

                def rsl(kc):
                    return rin[:, kc * 512:(kc + 1) * 512]
            else:
                rdeps = [tok_x3b]

                def rsl(kc):
                    return x3b[:, kc * TPC + tof: kc * TPC + tof + 512]

            g_toks = [None] * 8
            for hh in range(2):
                wa = sw1t if shared else w1t[e]
                s1, lt = wload(wa[:, hh * 512:(hh + 1) * 512], 512)
                for hc in range(4):
                    hix = hh * 4 + hc
                    bsl = hc % 2
                    bank = st4[:, bsl * 512: bsl * 512 + 512]
                    pairs = [(wpool[s1][:, kc * 512 + hc * 128:
                                       kc * 512 + hc * 128 + 128],
                              rsl(kc)) for kc in range(NCH)]
                    mt = k.mm_group(bank, pairs,
                                    deps=[lt, bankR[bsl]] + rdeps)
                    gt = k.act(gs[:, hix * 512: hix * 512 + 512],
                               bank, AF.Silu, deps=[mt, h_prev[hix]])
                    g_toks[hix] = gt
                    bankR[bsl] = gt
                    wuse_tok[s1] = mt
                    rinR = mt
            h_toks = [None] * 8
            for hh in range(2):
                wa = sw3t if shared else w3t[e]
                s1, lt = wload(wa[:, hh * 512:(hh + 1) * 512], 512)
                for hc in range(4):
                    hix = hh * 4 + hc
                    bsl = 2 + hc % 2
                    bank = st4[:, bsl * 512: bsl * 512 + 512]
                    pairs = [(wpool[s1][:, kc * 512 + hc * 128:
                                       kc * 512 + hc * 128 + 128],
                              rsl(kc)) for kc in range(NCH)]
                    mt = k.mm_group(bank, pairs,
                                    deps=[lt, bankR[bsl]] + rdeps)
                    ht = k.tt(hb[:, hix * 512: hix * 512 + 512],
                              bank, gs[:, hix * 512: hix * 512 + 512],
                              ALU.mult, deps=[mt, g_toks[hix]])
                    h_toks[hix] = ht
                    bankR[bsl] = ht
                    wuse_tok[s1] = mt
                    rinR = mt
            h_prev = h_toks
            for fh in range(2):
                wa = sw2t if shared else w2t[e]
                s1, lt = wload(wa[:, fh * 1024:(fh + 1) * 1024], 1024)
                for fc in range(8):
                    fch = fh * 8 + fc
                    bsl = fc % 2
                    bank = st4[:, bsl * 512: bsl * 512 + 512]
                    pairs = [(wpool[s1][:, hc * 1024 + fc * 128:
                                       hc * 1024 + fc * 128 + 128],
                              hb[:, hc * 512:(hc + 1) * 512])
                             for hc in range(8)]
                    mt = k.mm_group(bank, pairs,
                                    deps=[lt, bankR[bsl]] + h_toks)
                    at = k.tt(acc[:, fch * 512: fch * 512 + 512],
                              acc[:, fch * 512: fch * 512 + 512],
                              bank, ALU.add,
                              deps=[mt, acc_tok[fch]])
                    acc_tok[fch] = at
                    bankR[bsl] = at
                    wuse_tok[s1] = mt

        s1 = st4[0:1, 1024:1536]
        s2 = st4[0:1, 1536:2048]
        t_s1 = lnstat_prev[0]
        t_s2 = lnstat_prev[1]
        sq_tok = None
        pb_tok = None
        for kc in range(NCH):
            seg = acc[:, kc * 512:(kc + 1) * 512]
            pb_tok = k.copy(preb[:], seg, deps=[acc_tok[kc], t_s1, bankR[2]])
            sq_tok = k.act(sqs[:], seg, AF.Square, deps=[acc_tok[kc], t_s2,
                                                         bankR[3]])
            t_s1 = k.mm(s1, onescol[:], preb[:],
                        deps=[pb_tok, tok_onesc], start=(kc == 0),
                        stop=(kc == NCH - 1), inc=True)
            t_s2 = k.mm(s2, onescol[:], sqs[:],
                        deps=[sq_tok], start=(kc == 0),
                        stop=(kc == NCH - 1), inc=True)
        bankR[2] = t_s1
        bankR[3] = t_s2
        lnstat_prev = [t_s1, t_s2]
        t_m = k.ts(rowm[:], s1, 1.0 / D, ALU.mult, deps=[t_s1])
        t_v = k.ts(rowv[:], s2, 1.0 / D, ALU.mult, deps=[t_s2])
        t_m2 = k.tt(tmp[0:1, :], rowm[:], rowm[:], ALU.mult, deps=[t_m])
        t_v2 = k.tt(rowv[:], rowv[:], tmp[0:1, :], ALU.subtract,
                    deps=[t_v, t_m2])
        t_sd = k.act(tmp[0:1, :], rowv[:], AF.Sqrt, deps=[t_v2, tok_eps],
                     bias=epst[0:1, 0:1], scale=1.0)
        t_r = k.recip(rowa[:], tmp[0:1, :], deps=[t_sd])
        t_ab = k.copy(rowab[:], rowa[:], deps=[t_r])
        t_bb = k.tt(tmp[0:1, :], rowm[:], rowa[:], ALU.mult, deps=[t_r, t_ab])
        t_bn = k.ts(tmp[0:1, :], tmp[0:1, :], -1.0, ALU.mult, deps=[t_bb])
        t_bbb = k.copy(rowbb[:], tmp[0:1, :], deps=[t_bn])
        tba = k.mm(bc[:, :], ones1[:, :], rowab[:], deps=[t_ab, tok_ones,
                                                          wbcR])
        ebA = k.copy(lbc[:, :], bc[:, :], deps=[tba])
        tbb = k.mm(bc[:, :], ones1[:, :], rowbb[:], deps=[t_bbb, ebA])
        tm2 = None
        y_toks = [None] * NCH
        for kc in range(NCH):
            seg = acc[:, kc * 512:(kc + 1) * 512]
            tm1 = k.tt(tmp[:, :], seg, lbc[:, :], ALU.mult,
                       deps=[tba, ebA, t_s1, t_s2, sq_tok, pb_tok, tm2])
            tm2 = k.tt(seg, tmp[:, :], bc[:, :], ALU.add, deps=[tm1, tbb])
            y_toks[kc] = tm2
        wbcR = tbb

        trR = None
        for jt in range(4):
            ev_toks = []
            for kc in range(NCH):
                tt_ = k.transpose(trp[:, :],
                                  acc[:, kc * 512 + jt * 128:
                                      kc * 512 + jt * 128 + 128],
                                  idf[:], deps=[y_toks[kc], tok_idf, trR])
                ev = k.copy(ytm[:, kc * 128:(kc + 1) * 128], trp[:, :],
                            deps=[tt_])
                trR = ev
                ev_toks.append(ev)
            vfy = k.vflush([ev_toks[-1]])
            ot = k.dma(yout[(th * 512 + jt * 128):(th * 512 + jt * 128 + 128), :],
                       ytm[:, :], deps=ev_toks + [vfy])
            out_toks.append(ot)
            trR = ot
    return k.finish(out_toks + dbg_toks)


def stage_a7(core, x3_fm_shard, inputs):
    w1t = np.ascontiguousarray(np.transpose(inputs["L1_ffn_w1"], (0, 2, 1)))
    w3t = np.ascontiguousarray(np.transpose(inputs["L1_ffn_w3"], (0, 2, 1)))
    w2t = np.ascontiguousarray(np.transpose(inputs["L1_ffn_w2"], (0, 2, 1)))
    return {
        "x3fm": x3_fm_shard,
        "gwT": np.ascontiguousarray(inputs["L1_ffn_gate_w"].T),
        "ebias_bc": np.ascontiguousarray(
            np.broadcast_to(inputs["L1_ffn_expert_bias"], (128, 8))).astype(np.float32),
        "w1t": w1t, "w3t": w3t, "w2t": w2t,
        "sw1t": np.ascontiguousarray(inputs["L1_ffn_sw1"].T),
        "sw3t": np.ascontiguousarray(inputs["L1_ffn_sw3"].T),
        "sw2t": np.ascontiguousarray(inputs["L1_ffn_sw2"].T),
        "identf": np.eye(128, dtype=np.float32),
    }




class _BCView:
    def __init__(self, t, off_bf, width_bf):
        self.t, self.off, self.width = t, off_bf, width_bf

    def __getitem__(self, idx):
        if isinstance(idx, tuple):
            p, f = idx
        else:
            p, f = idx, slice(None)
        fs = f.start if f.start is not None else 0
        fe = f.stop if f.stop is not None else self.width
        fs += self.off
        fe += self.off
        assert fs % 2 == 0 and fe % 2 == 0, (fs, fe)
        return self.t[p, fs // 2: fe // 2].bitcast(BF16)


class _ColView:
    def __init__(self, t, off, width):
        self.t, self.off, self.width = t, off, width

    def __getitem__(self, idx):
        if isinstance(idx, tuple):
            p, f = idx
        else:
            p, f = idx, slice(None)
        fs = f.start if f.start is not None else 0
        fe = f.stop if f.stop is not None else self.width
        return self.t[p, self.off + fs: self.off + fe]

DEBUG_A1 = False
SERIAL_PHASE = None


def build_a1():
    k = KB("a1")
    x0fm = k.dram("x0fm", [D, TPC], F32)
    Wq = k.dram("Wq", [D, D], F32)
    Wk = k.dram("Wk", [D, D], F32)
    Wv = k.dram("Wv", [D, D], F32)
    Wo = k.dram("Wo", [D, D], F32)
    Wg1 = k.dram("Wg1", [D, 128], F32)
    Wg2 = k.dram("Wg2", [128, D], F32)
    normw_pc = k.dram("normw_pc", [128, NCH], F32)
    ld_rep = k.dram("ld_rep", [128, 2048], F32)
    ld_rep4 = k.dram("ld_rep4", [128, 64], F32)
    ld_bc = k.dram("ld_bc", [128, 16], F32)
    cposrep = k.dram("cposrep", [128, 2048], F32)
    diffrep = k.dram("diffrep", [128, 2048], F32)
    chainrep = k.dram("chainrep", [128, 64], F32)
    kpos = k.dram("kpos", [128, 1], F32)
    identb = k.dram("identb", [128, 128], BF16)
    x1fm = k.dram("x1fm", [D, TPC], F32, kind="ExternalOutput")
    agin = k.dram_internal("agin", [D, 128], F32)
    agout = k.dram_internal("agout", [4 * D, 128], F32)
    if DEBUG_A1:
        qf_d = k.dram("qf_d", [D, TPC], BF16, kind="ExternalOutput")
        kf_d = k.dram("kf_d", [D, TPC], BF16, kind="ExternalOutput")
        vt_d = k.dram("vt_d", [8 * 128, D], BF16, kind="ExternalOutput")
        ag_d = k.dram("ag_d", [4 * D, 128], F32, kind="ExternalOutput")
        sin_d = k.dram("sin_d", [D, 128], F32, kind="ExternalOutput")
        on_d = k.dram("on_d", [D, TPC], BF16, kind="ExternalOutput")
        g1_d = k.dram("g1_d", [128, TPC], BF16, kind="ExternalOutput")

    qf = k.sb("qf", [128, NCH * TPC], BF16)
    kf = k.sb("kff", [128, NCH * TPC], BF16)
    vt = k.sb("vt", [128, 8 * D], BF16)
    onrm = k.sb("onrm", [128, NCH * TPC], BF16)
    xbf = onrm
    wblk = [k.sb("wblk%d" % i, [128, NCH * 256], BF16) for i in range(2)]
    wg2s = wblk[0]
    g1s = k.sb("g1s", [128, TPC], BF16)
    wg1s = k.sb("wg1s", [128, NCH * 128], BF16)
    prebig = k.sb("prebig", [128, 4096], F32)
    Ast = _ColView(prebig, 0, 2048)
    ldr = Ast
    agt = _ColView(prebig, 2048, 2048)
    scr2 = agt
    Sst = k.sb("Sst", [128, NCH * 128], F32)
    dtall = k.sb("dtall", [128, 2048], BF16)
    qdecb = k.sb("qdecb", [128, 2048], BF16)
    wchain = k.sb("wchain", [128, 64], F32)
    kdec = k.sb("kdec", [128, 16], F32)
    cdv = k.sb("cdv", [128, 16], F32)
    ldbc = k.sb("ldbc", [128, 16], F32)
    kposs = k.sb("kposs", [128, 1], F32)
    Sbf = k.sb("Sbf", [128, 128], BF16)
    kts = k.sb("kts", [128, 128], BF16)
    pts = k.sb("pts", [128, 128], BF16)
    qsc = k.sb("qsc", [128, 128], BF16)
    ogs = k.sb("ogs", [128, 128], F32)
    sgb = k.sb("sgb", [128, 128], BF16)
    sqb = k.sb("sqb", [128, 128], BF16)
    rr = k.sb("rr", [1, 128], F32)
    rrb = k.sb("rrb", [1, 128], BF16)
    nwse = k.sb("nwse", [128, NCH], F32)
    ones1 = k.sb("ones1", [1, 128], BF16)
    onescol = k.sb("onescol", [128, 1], BF16)
    epst = k.sb("epst", [1, 1], F32)
    idb = k.sb("idb", [128, 128], BF16)
    x0ch = [k.sb("x0ch%d" % i, [128, 256], F32) for i in range(2)]
    pre = prebig
    sqs = k.sb("sqs", [128, 256], BF16)
    preb = k.sb("preb", [128, 256], BF16)
    rowm = k.sb("rowm", [1, 256], F32)
    rowa = k.sb("rowa", [1, 256], F32)
    rowv = k.sb("rowv", [1, 256], F32)
    rowab = k.sb("rowab", [1, 256], BF16)
    rowbb = k.sb("rowbb", [1, 256], BF16)
    tmp = k.sb("tmp", [128, 256], F32)
    lbc = k.sb("lbc", [128, 256], F32)

    st4 = k.ps("st4", [128, 2048], F32)
    pdel = k.ps("pdel", [128, 128], F32)
    ptr = k.ps("ptr", [128, 128], BF16)
    prow = k.ps("prow", [1, 128], F32)
    pbc = k.ps("pbc", [128, 128], F32)

    tok_ones = k.memset(ones1[:], 1.0)
    tok_onesc = k.memset(onescol[:], 1.0)
    tok_eps = k.memset(epst[:], 1e-5)
    tok_id = k.dma(idb[:], identb[:, :])
    tok_nw = k.dma(nwse[:], normw_pc[:, :])
    tok_kp = k.dma(kposs[:], kpos[:, :])
    tok_ldbc = k.dma(ldbc[:], ld_bc[:, :])

    t1 = k.dma(ldr[:], ld_rep[:, 0:2048].rearrange("p (a t) -> p a t", t=128))
    t2 = k.dma(scr2[:], cposrep[:, 0:2048].rearrange("p (a t) -> p a t", t=128))
    t3 = k.tt(scr2[:], scr2[:], ldr[:], ALU.mult, deps=[t1, t2])
    t4 = k.act(qdecb[:], scr2[:], AF.Exp, deps=[t3])
    t5 = k.dma(scr2[:], diffrep[:, 0:2048].rearrange("p (a t) -> p a t", t=128),
               deps=[t4])
    t6 = k.tt(scr2[:], scr2[:], ldr[:], ALU.mult, deps=[t5, t4])
    t7 = k.act(dtall[:], scr2[:], AF.Exp, deps=[t6])
    t8 = k.dma(scr2[:, 0:64], chainrep[:, :], deps=[t7])
    t9 = k.dma(scr2[:, 64:128], ld_rep4[:, :], deps=[t7])
    t10 = k.tt(wchain[:], scr2[:, 0:64], scr2[:, 64:128], ALU.mult,
               deps=[t8, t9])
    t11 = k.act(wchain[:], wchain[:], AF.Exp, deps=[t10])
    t12 = k.ts(kdec[:], ldbc[:], kposs[:, 0:1], ALU.mult,
               deps=[tok_ldbc, tok_kp])
    t13 = k.act(kdec[:], kdec[:], AF.Exp, deps=[t12])
    t14 = k.ts(cdv[:], ldbc[:], 128.0, ALU.mult, deps=[tok_ldbc, t13])
    t15 = k.act(cdv[:], cdv[:], AF.Exp, deps=[t14])
    tab_tok = [t11, t13, t15, t4, t7]

    tok_x = k.loadw(_s3(xbf[:], TPC), _r3(x0fm))

    wt = [None, None]
    wuse = [None, None]

    def wload(W, blk, buf, extra=()):
        src = _r3(W[:, blk * 256:(blk + 1) * 256])
        return k.loadw(wblk[buf][:].rearrange("p (c t) -> p c t", t=256), src,
                       deps=[wuse[buf]] + list(extra))

    seq = [("q", Wq, b) for b in range(8)] + [("k", Wk, b) for b in range(8)] \
        + [("v", Wv, b) for b in range(8)]
    wt[0] = wload(Wq, 0, 0)
    bankR4 = [None, None, None, None]
    proj_last = None
    for i, (kind, W, blk) in enumerate(seq):
        buf = i % 2
        if i + 1 < len(seq):
            nk, nW, nblk = seq[i + 1]
            wt[(i + 1) % 2] = wload(nW, nblk, (i + 1) % 2)
        wb = wblk[buf]
        if kind in ("q", "k"):
            dst = qf if kind == "q" else kf
            for j in range(2):
                fch = blk * 2 + j
                for th in range(2):
                    bsl = (j * 2 + th) % 4
                    bank = st4[:, bsl * 512: bsl * 512 + 512]
                    pairs = [(wb[:, kc * 256 + j * 128: kc * 256 + j * 128 + 128],
                              xbf[:, kc * TPC + th * 512: kc * TPC + th * 512 + 512])
                             for kc in range(NCH)]
                    mt = k.mm_group(bank, pairs,
                                    deps=[wt[buf], tok_x, bankR4[bsl]])
                    et = k.act(dst[:, fch * TPC + th * 512:
                                   fch * TPC + th * 512 + 512],
                               bank, AF.Silu, deps=[mt])
                    bankR4[bsl] = et
                    wuse[buf] = mt
                    proj_last = mt
        else:
            for jt in range(8):
                bsl = jt % 4
                bank = st4[:, bsl * 512: bsl * 512 + 256]
                pairs = [(xbf[:, kc * TPC + jt * 128: kc * TPC + jt * 128 + 128],
                          wb[:, kc * 256:(kc + 1) * 256])
                         for kc in range(NCH)]
                mt = k.mm_group(bank, pairs, deps=[wt[buf], tok_x, bankR4[bsl]])
                et = k.act(vt[:, jt * D + blk * 256: jt * D + blk * 256 + 256],
                           bank, AF.Silu, deps=[mt])
                bankR4[bsl] = et
                wuse[buf] = mt
                proj_last = mt

    tok_wg1 = k.loadw(wg1s[:].rearrange("p (c t) -> p c t", t=128),
                      _r3(Wg1), deps=[])

    for th in range(2):
        bsl = th % 2
        bank = st4[:, bsl * 512: bsl * 512 + 512]
        pairs = [(wg1s[:].rearrange("p (c t) -> p c t", t=128)[:, kc, :],
                  xbf[:, kc * TPC + th * 512: kc * TPC + th * 512 + 512])
                 for kc in range(NCH)]
        mt = k.mm_group(bank, pairs, deps=[tok_wg1, tok_x, bankR4[bsl]])
        et = k.copy(g1s[:, th * 512: th * 512 + 512], bank, deps=[mt])
        bankR4[bsl] = et
        proj_last = mt

    tok_wg2 = k.loadw(wg2s[:, 0:2048], Wg2[:, :],
                      deps=[wuse[0], proj_last])

    A_tok = [None] * NCH
    ktr_R = None
    kts_R = None
    del_R = None
    for j in range(8):
        for h in range(NCH):
            ttk = k.transpose(ptr[:, :], kf[:, h * TPC + j * 128:
                                            h * TPC + j * 128 + 128],
                              idb[:], deps=[tok_id, ktr_R,
                                            proj_last if j == 0 else None])
            kev = k.ts(kts[:], ptr[:, :], kdec[:, h:h + 1], ALU.mult,
                       deps=[ttk, t13, kts_R])
            ktr_R = kev
            dmt = k.mm(pdel[:, :], kts[:], vt[:, j * D + h * 128:
                                              j * D + h * 128 + 128],
                       deps=[kev, del_R, proj_last])
            kts_R = dmt
            if j == 0:
                at = k.copy(Ast[:, h * 128:(h + 1) * 128], pdel[:, :],
                            deps=[dmt] + tab_tok)
            else:
                at = k.stt(Ast[:, h * 128:(h + 1) * 128],
                           Ast[:, h * 128:(h + 1) * 128],
                           cdv[:, h:h + 1], pdel[:, :], ALU.mult, ALU.add,
                           deps=[dmt, A_tok[h], t15])
            del_R = at
            A_tok[h] = at

    vfa = k.vflush(A_tok)
    ag_in_tok = k.dma(agin.ap().rearrange("(h p) e -> p h e", p=128), _s3(Ast[:], 128),
                      deps=A_tok + [vfa])
    cc_tok = k.collective("AllGather", ALU.bypass,
                          [[0, 1, 2, 3], [4, 5, 6, 7]],
                          [agin.ap().opt()], [agout.ap().opt()],
                          deps=[ag_in_tok])
    sin_tok = [None] * NCH
    z_toks = []
    for h in range(NCH):
        z_toks.append(k.memset(Sst[:, h * 128:(h + 1) * 128], 0.0,
                               deps=[]))
    agt_tok = None
    for cp in range(4):
        at_ = k.dma(_s3(agt[:], 128),
                    agout.ap()[cp * D:(cp + 1) * D, :].rearrange(
                        "(h p) e -> p h e", p=128),
                    deps=[cc_tok, agt_tok] + z_toks)
        newtok = None
        for h in range(NCH):
            st_ = k.stt(Sst[:, h * 128:(h + 1) * 128],
                        agt[:, h * 128:(h + 1) * 128],
                        wchain[:, h * 4 + cp: h * 4 + cp + 1],
                        Sst[:, h * 128:(h + 1) * 128],
                        ALU.mult, ALU.add,
                        deps=[at_, t11, z_toks[h], sin_tok[h]])
            sin_tok[h] = st_
            newtok = st_
        agt_tok = newtok

    dbg_toks = []
    if DEBUG_A1:
        dbg_toks.append(k.dma(_r3(qf_d), _s3(qf[:], TPC), deps=[proj_last]))
        dbg_toks.append(k.dma(_r3(kf_d), _s3(kf[:], TPC), deps=[proj_last]))
        dbg_toks.append(k.dma(vt_d.rearrange("(j p) f -> p j f", p=128),
                              _s3(vt[:], D), deps=[proj_last]))
        dbg_toks.append(k.dma(ag_d, agout.ap(), deps=[cc_tok]))
        sin_dump = k.dma(sin_d.rearrange("(h p) e -> p h e", p=128),
                             _s3(Sst[:], 128), deps=sin_tok)
        dbg_toks.append(sin_dump)
        for _h in range(NCH):
            sin_tok[_h] = sin_dump
        dbg_toks.append(k.dma(g1_d, g1s[:],
                              deps=[bankR4[0], bankR4[1]]))

    on_first = []
    oR = None
    sR = None
    gR = None
    ptsR = None
    qscR = None
    ogR = None
    sqR = None
    rowR = None
    sbfR = None
    for j in range(8):
        for h in range(NCH):
            qoff = h * TPC + j * 128
            smt = k.mm(st4[:, 0:128], kf[:, qoff:qoff + 128],
                       qf[:, qoff:qoff + 128],
                       deps=[ptsR, proj_last if j == 0 else None,
                             bankR4[0] if (j == 0 and h == 0) else None])
            ptt = k.tt(pts[:], st4[:, 0:128], dtall[:, h * 128:(h + 1) * 128],
                       ALU.mult, deps=[smt, t7])
            ptsR = ptt
            qst = k.tt(qsc[:], qf[:, qoff:qoff + 128],
                       qdecb[:, h * 128:(h + 1) * 128], ALU.mult,
                       deps=[t4, qscR])
            sbt = k.copy(Sbf[:], Sst[:, h * 128:(h + 1) * 128],
                         deps=[sin_tok[h], sbfR])
            om1 = k.mm(st4[:, 512:640], vt[:, j * D + h * 128: j * D + h * 128 + 128],
                       pts[:], deps=[ptt, oR,
                                     bankR4[1] if (j == 0 and h == 0) else None],
                       start=True, stop=False, inc=False)
            om2 = k.mm(st4[:, 512:640], Sbf[:], qsc[:],
                       deps=[sbt, qst], start=False, stop=True)
            sbfR = om2
            qscR = om2
            gmt = k.mm(st4[:, 1024:1152], wg2s[:, h * 128:(h + 1) * 128],
                       g1s[:, j * 128:(j + 1) * 128],
                       deps=[tok_wg2, gR, bankR4[2], bankR4[3], oR])
            sgt = k.act(sgb[:], st4[:, 1024:1152], AF.Sigmoid,
                        deps=[gmt, oR])
            gR = sgt
            ogt = k.tt(ogs[:], sgb[:], st4[:, 512:640], ALU.mult,
                       deps=[om2, sgt, ogR])
            oR = ogt
            sqt = k.act(sqb[:], ogs[:], AF.Square, deps=[ogt, sqR])
            ssm = k.mm(prow[:, :], onescol[:], sqb[:],
                       deps=[sqt, tok_onesc, rowR])
            rsd = k.act(rr[:], prow[:, :], AF.Sqrt, deps=[ssm, tok_eps],
                        bias=epst[0:1, 0:1], scale=1.0 / 128.0)
            rcp = k.recip(rr[:], rr[:], deps=[rsd])
            rcb = k.copy(rrb[:], rr[:], deps=[rcp])
            bct = k.mm(pbc[:, :], ones1[:, :], rrb[:], deps=[rcb, tok_ones,
                                                             ogR])
            ont = k.stt(onrm[:, qoff:qoff + 128], ogs[:],
                        nwse[:, h:h + 1], pbc[:, :], ALU.mult, ALU.mult,
                        deps=[bct, tok_nw, ogt,
                              proj_last if (j == 0 and h == 0) else None,
                              tok_x if (j == 0 and h == 0) else None])
            rowR = ont
            ogR = ont
            sqR = ssm
            ttk = k.transpose(ptr[:, :], kf[:, qoff:qoff + 128], idb[:],
                              deps=[tok_id, ktr_R])
            kev = k.ts(kts[:], ptr[:, :], kdec[:, h:h + 1], ALU.mult,
                       deps=[ttk, kts_R])
            ktr_R = kev
            dmt = k.mm(pdel[:, :], kts[:], vt[:, j * D + h * 128:
                                              j * D + h * 128 + 128],
                       deps=[kev, del_R])
            kts_R = dmt
            sup = k.stt(Sst[:, h * 128:(h + 1) * 128],
                        Sst[:, h * 128:(h + 1) * 128],
                        cdv[:, h:h + 1], pdel[:, :], ALU.mult, ALU.add,
                        deps=[dmt, sbt, t15])
            del_R = sup
            sin_tok[h] = sup
            on_first.append(ont)

    o_done = on_first[-1]
    if DEBUG_A1:
        dbg_toks.append(k.dma(_r3(on_d), _s3(onrm[:], TPC), deps=[o_done]))

    if SERIAL_PHASE in ("D", "D1"):
        Prog.SERIAL = True
    fence = k.p.fence_toks()
    SEG = 256
    out_toks = []
    x2use = [None, None]
    bankR2 = [None, None]
    lnstat_prev = [None, None]
    outprev = [None] * NCH
    for th in range(4):
        t0c = th * SEG
        pre_toks = []
        for blk in range(8):
            buf = blk % 2
            wlf = k.p.fence_toks()
            wt[buf] = wload(Wo, blk, buf, extra=[o_done, gR] + wlf)
            for j in range(2):
                fch = blk * 2 + j
                xb2 = j % 2
                xt = k.dma(x0ch[xb2][:, 0:SEG],
                           x0fm[fch * 128:(fch + 1) * 128, t0c:t0c + SEG],
                           deps=[x2use[xb2]])
                bank = st4[:, (j % 2) * 512:(j % 2) * 512 + SEG]
                pairs = [(wblk[buf][:, kc * 256 + j * 128: kc * 256 + j * 128 + 128],
                          onrm[:, kc * TPC + t0c: kc * TPC + t0c + SEG])
                         for kc in range(NCH)]
                mmf = k.p.fence_toks()
                mt = k.mm_group(bank, pairs,
                                deps=[wt[buf], o_done, bankR2[j % 2]] + mmf)
                et = k.tt(pre[:, fch * SEG:(fch + 1) * SEG],
                          x0ch[xb2][:, 0:SEG], bank, ALU.add,
                          deps=[mt, xt, outprev[fch]])
                bankR2[j % 2] = et
                x2use[xb2] = et
                wuse[buf] = mt
                pre_toks.append(et)
        if SERIAL_PHASE == "D1":
            Prog.SERIAL = False
        if SERIAL_PHASE == "D2":
            Prog.SERIAL = True
        s1 = st4[0:1, 1024:1280]
        s2 = st4[0:1, 1536:1792]
        t_s1 = lnstat_prev[0]
        t_s2 = lnstat_prev[1]
        sq_tok = None
        pb_tok = None
        for kc in range(NCH):
            seg = pre[:, kc * SEG:(kc + 1) * SEG]
            pb_tok = k.copy(preb[:, 0:SEG], seg, deps=[pre_toks[kc], t_s1])
            sq_tok = k.act(sqs[:, 0:SEG], seg, AF.Square,
                           deps=[pre_toks[kc], t_s2])
            t_s1 = k.mm(s1, onescol[:], preb[:, 0:SEG],
                        deps=[pb_tok, tok_onesc], start=(kc == 0),
                        stop=(kc == NCH - 1), inc=True)
            t_s2 = k.mm(s2, onescol[:], sqs[:, 0:SEG],
                        deps=[sq_tok], start=(kc == 0),
                        stop=(kc == NCH - 1), inc=True)
        lnstat_prev = [t_s1, t_s2]
        t_m = k.ts(rowm[0:1, 0:SEG], s1, 1.0 / D, ALU.mult, deps=[t_s1])
        t_v = k.ts(rowv[0:1, 0:SEG], s2, 1.0 / D, ALU.mult, deps=[t_s2])
        t_m2 = k.tt(tmp[0:1, 0:SEG], rowm[0:1, 0:SEG], rowm[0:1, 0:SEG],
                    ALU.mult, deps=[t_m])
        t_v2 = k.tt(rowv[0:1, 0:SEG], rowv[0:1, 0:SEG], tmp[0:1, 0:SEG],
                    ALU.subtract, deps=[t_v, t_m2])
        t_sd = k.act(tmp[0:1, 0:SEG], rowv[0:1, 0:SEG], AF.Sqrt,
                     deps=[t_v2, tok_eps], bias=epst[0:1, 0:1], scale=1.0)
        t_r = k.recip(rowa[0:1, 0:SEG], tmp[0:1, 0:SEG], deps=[t_sd])
        t_ab = k.copy(rowab[0:1, 0:SEG], rowa[0:1, 0:SEG], deps=[t_r])
        t_bb = k.tt(tmp[0:1, 0:SEG], rowm[0:1, 0:SEG], rowa[0:1, 0:SEG],
                    ALU.mult, deps=[t_r, t_ab])
        t_bn = k.ts(tmp[0:1, 0:SEG], tmp[0:1, 0:SEG], -1.0, ALU.mult,
                    deps=[t_bb])
        t_bbb = k.copy(rowbb[0:1, 0:SEG], tmp[0:1, 0:SEG], deps=[t_bn])
        tba = k.mm(st4[:, 1024:1024 + SEG], ones1[:, :], rowab[0:1, 0:SEG],
                   deps=[t_ab, tok_ones, t_s1, t_s2])
        ebA = k.copy(lbc[:, 0:SEG], st4[:, 1024:1024 + SEG], deps=[tba])
        tbb = k.mm(st4[:, 1024:1024 + SEG], ones1[:, :], rowbb[0:1, 0:SEG],
                   deps=[t_bbb, ebA])
        tm2 = None
        tm2s = []
        for kc in range(NCH):
            seg = pre[:, kc * SEG:(kc + 1) * SEG]
            tm1 = k.tt(tmp[:, 0:SEG], seg, lbc[:, 0:SEG], ALU.mult,
                       deps=[tba, ebA, sq_tok, pb_tok, tm2])
            tm2 = k.tt(seg, tmp[:, 0:SEG], st4[:, 1024:1024 + SEG], ALU.add,
                       deps=[tm1, tbb])
            tm2s.append(tm2)
        vf = k.vflush([tm2s[-1]])
        for kc in range(NCH):
            seg = pre[:, kc * SEG:(kc + 1) * SEG]
            ot = k.dma(x1fm[kc * 128:(kc + 1) * 128, t0c:t0c + SEG],
                       seg, deps=[tm2s[kc], vf])
            outprev[kc] = ot
            out_toks.append(ot)
        if SERIAL_PHASE in ("D1", "D2"):
            Prog.SERIAL = (SERIAL_PHASE == "D1")
    Prog.SERIAL = False
    return k.finish(out_toks + dbg_toks)


def stage_a1(core, x0_fm_shard, inputs):
    b, r = divmod(core, 4)
    ld = np.asarray(inputs["L0_attn_log_decay"], np.float32)
    ld_rep = np.broadcast_to(np.repeat(ld, 128), (128, 2048))
    ld_rep4 = np.broadcast_to(np.repeat(ld, 4), (128, 64))
    ld_bc = np.broadcast_to(ld, (128, 16))
    cpos = np.arange(1, 129, dtype=np.float32)
    cposrep = np.broadcast_to(np.tile(cpos, 16), (128, 2048))
    s_ = np.arange(128)
    c_ = np.arange(128)
    diffmT = (c_[None, :] - s_[:, None]).astype(np.float32)
    diffmT[diffmT < 0] = 1e9
    diffrep = np.tile(diffmT, (1, 16))
    chain = np.full(4, 1e9, np.float32)
    for j in range(4):
        if j < r:
            chain[j] = 1024.0 * (r - 1 - j)
    chainrep = np.broadcast_to(np.tile(chain, 16), (128, 64))
    kposv = (127 - np.arange(128)).astype(np.float32).reshape(128, 1)
    return {
        "x0fm": x0_fm_shard,
        "Wq": inputs["L0_attn_Wq"], "Wk": inputs["L0_attn_Wk"],
        "Wv": inputs["L0_attn_Wv"], "Wo": inputs["L0_attn_Wo"],
        "Wg1": inputs["L0_attn_Wg1"], "Wg2": inputs["L0_attn_Wg2"],
        "normw_pc": _chunk_cols(inputs["L0_attn_norm_w"]),
        "ld_rep": np.ascontiguousarray(ld_rep, np.float32),
        "ld_rep4": np.ascontiguousarray(ld_rep4, np.float32),
        "ld_bc": np.ascontiguousarray(ld_bc, np.float32),
        "cposrep": np.ascontiguousarray(cposrep, np.float32),
        "diffrep": np.ascontiguousarray(diffrep, np.float32),
        "chainrep": np.ascontiguousarray(chainrep, np.float32),
        "kpos": kposv,
        "identb": _ident_bf16(),
    }


def build_a2():
    k = KB("a2")
    x1fm = k.dram("x1fm", [D, TPC], F32)
    ln1_pc = k.dram("ln1_pc", [128, NCH], F32)
    kv_d_w = k.dram("kv_d_w", [D, 512], F32)
    q_d_w = k.dram("q_d_w", [D, 512], F32)
    k_u_w = k.dram("k_u_w", [512, 1024], F32)
    v_u_w = k.dram("v_u_w", [512, D], F32)
    q_u_w = k.dram("q_u_w", [512, 1024], F32)
    rope_k_w = k.dram("rope_k_w", [D, 1024], F32)
    rope_q_w = k.dram("rope_q_w", [512, 1024], F32)
    cost = k.dram("cost", [128, 8 * 64], F32)
    sint = k.dram("sint", [128, 8 * 64], F32)
    k_tm = k.dram("k_tm", [TPC, D], BF16, kind="ExternalOutput")
    q_tm = k.dram("q_tm", [TPC, D], BF16, kind="ExternalOutput")
    v_tm = k.dram("v_tm", [TPC, D], BF16, kind="ExternalOutput")

    x1f = k.sb("x1f", [128, NCH * TPC], F32)
    kall = _BCView(x1f, 0, 8 * D)
    qall = _BCView(x1f, 8 * D, 8 * D)
    vall = k.sb("vall", [128, 8 * D], BF16)
    xn = k.sb("xn", [128, NCH * TPC], BF16)
    kvf = k.sb("kvf", [128, 4 * TPC], BF16)
    qdf = k.sb("qdf", [128, 4 * TPC], BF16)
    wblk = [k.sb("wblk%d" % i, [128, NCH * 512], BF16) for i in range(2)]
    costs = k.sb("costs", [128, 8 * 64], F32)
    sints = k.sb("sints", [128, 8 * 64], F32)
    lnw = k.sb("lnw", [128, NCH], F32)
    ones1 = k.sb("ones1", [1, 128], BF16)
    onescol = k.sb("onescol", [128, 1], BF16)
    epst = k.sb("epst", [1, 1], F32)
    sqs = k.sb("sqs", [128, 512], BF16)
    rowv = k.sb("rowv", [1, 512], F32)
    rowab = k.sb("rowab", [1, 512], BF16)
    lbc = k.sb("lbc", [128, 512], F32)
    rsc1 = k.sb("rsc1", [128, 64], F32)
    rsc2 = k.sb("rsc2", [128, 64], F32)

    st4 = k.ps("st4", [128, 2048], F32)
    bc = k.ps("bc", [128, 512], F32)
    srow = k.ps("srow", [1, 1024], F32)

    tok_ones = k.memset(ones1[:], 1.0)
    tok_onesc = k.memset(onescol[:], 1.0)
    tok_eps = k.memset(epst[:], 1e-5)
    tok_ln = k.dma(lnw[:], ln1_pc[:, :])
    tok_cos = k.dma(costs[:], cost[:, :])
    tok_sin = k.dma(sints[:], sint[:, :])
    tok_x1 = k.dma(_s3(x1f[:], TPC), _r3(x1fm))

    xn_toks = []
    sq_prev = None
    s2_tok = [None, None]
    for th in range(2):
        s2 = srow[0:1, th * 512:(th + 1) * 512]
        for kc in range(NCH):
            sq = k.act(sqs[:], x1f[:, kc * TPC + th * 512: kc * TPC + th * 512 + 512],
                       AF.Square, deps=[tok_x1, s2_tok[th] if kc else sq_prev])
            s2_tok[th] = k.mm(s2, onescol[:], sqs[:], deps=[sq, tok_onesc],
                              start=(kc == 0), stop=(kc == NCH - 1), inc=True)
            sq_prev = s2_tok[th]
    xn_last = None
    for th in range(2):
        s2 = srow[0:1, th * 512:(th + 1) * 512]
        t_v = k.ts(rowv[:], s2, 1.0 / D, ALU.mult, deps=[s2_tok[th]])
        t_sd = k.act(rowv[:], rowv[:], AF.Sqrt, deps=[t_v, tok_eps],
                     bias=epst[0:1, 0:1], scale=1.0)
        t_r = k.recip(rowv[:], rowv[:], deps=[t_sd])
        t_ab = k.copy(rowab[:], rowv[:], deps=[t_r])
        tb = k.mm(bc[:, :], ones1[:, :], rowab[:], deps=[t_ab, tok_ones,
                                                         xn_last])
        eb = k.copy(lbc[:, :], bc[:, :], deps=[tb])
        for kc in range(NCH):
            xt = k.stt(xn[:, kc * TPC + th * 512: kc * TPC + th * 512 + 512],
                       x1f[:, kc * TPC + th * 512: kc * TPC + th * 512 + 512],
                       lnw[:, kc:kc + 1], lbc[:, :], ALU.mult, ALU.mult,
                       deps=[eb, tok_ln])
            xn_toks.append(xt)
            xn_last = xt

    wuse = [None, None]
    wt = [None, None]

    def wload(W, c0, c1, buf, inner):
        w = c1 - c0
        src = _r3(W[:, c0:c1])
        return k.loadw(wblk[buf][:, 0:NCH * w].rearrange(
            "p (c t) -> p c t", t=w), src, deps=[wuse[buf]])

    bankR = [None, None, None, None]
    wt[0] = wload(kv_d_w, 0, 512, 0, None)
    wt[1] = wload(q_d_w, 0, 512, 1, None)
    for which, (dst, buf) in enumerate(((kvf, 0), (qdf, 1))):
        for lc in range(4):
            for th in range(2):
                bsl = (lc * 2 + th) % 4
                bank = st4[:, bsl * 512:bsl * 512 + 512]
                pairs = [(wblk[buf][:, kc * 512 + lc * 128: kc * 512 + lc * 128 + 128],
                          xn[:, kc * TPC + th * 512: kc * TPC + th * 512 + 512])
                         for kc in range(NCH)]
                mt = k.mm_group(bank, pairs, deps=[wt[buf], bankR[bsl]] + xn_toks)
                et = k.copy(dst[:, lc * TPC + th * 512: lc * TPC + th * 512 + 512],
                            bank, deps=[mt])
                bankR[bsl] = et
                wuse[buf] = mt

    kv_toks = []

    def tm_proj(bank, rhs_w, jt, nk, lat, deps):
        pairs = [(lat[:, kc * TPC + jt * 128: kc * TPC + jt * 128 + 128],
                  rhs_w[:, kc * 512:(kc + 1) * 512] if nk == 16 else
                  rhs_w[:, kc * 512:(kc + 1) * 512])
                 for kc in range(nk)]
        return k.mm_group(bank, pairs, deps=deps)

    seqs = [
        ("kp", k_u_w, 1024, kvf, 4, kall, 0),
        ("qp", q_u_w, 1024, qdf, 4, qall, 0),
        ("v",  v_u_w, 2048, kvf, 4, vall, None),
        ("kr", rope_k_w, 1024, xn, NCH, kall, 64),
        ("qr", rope_q_w, 1024, qdf, 4, qall, 64),
    ]
    buf = 0
    rope_R = None
    for name, W, wcols, lat, nk, dstall, coff in seqs:
        for blk in range(wcols // 512):
            buf = (buf + 1) % 2
            src = _r3(W[:, blk * 512:(blk + 1) * 512])
            wtk = k.loadw(wblk[buf][:, 0:nk * 512].rearrange(
                "p (c t) -> p c t", t=512), src, deps=[wuse[buf]])
            for jt in range(8):
                bsl = jt % 4
                bank = st4[:, bsl * 512:bsl * 512 + 512]
                pairs = [(lat[:, kc * TPC + jt * 128: kc * TPC + jt * 128 + 128],
                          wblk[buf][:, kc * 512:(kc + 1) * 512])
                         for kc in range(nk)]
                mt = k.mm_group(bank, pairs, deps=[wtk, bankR[bsl]] + xn_toks)
                wuse[buf] = mt
                if name == "v":
                    et = k.copy(vall[:, jt * D + blk * 512: jt * D + blk * 512 + 512],
                                bank, deps=[mt])
                elif name in ("kp", "qp"):
                    dst3 = dstall[:].rearrange("p (j f) -> p j f", f=D)[
                        :, jt, :].rearrange("p (h c) -> p h c", c=128)
                    et = k.copy(dst3[:, blk * 8:(blk + 1) * 8, 0:64],
                                bank[:].rearrange("p (h c) -> p h c", c=64),
                                deps=[mt])
                else:
                    ets = []
                    for hh in range(8):
                        h = blk * 8 + hh
                        pseg = bank[:, hh * 64:hh * 64 + 64]
                        pperm = bank[:, hh * 64:hh * 64 + 64].rearrange(
                            "p (a b) -> p a b", b=32)[:, ::-1, :] \
                            if False else None
                        t1 = k.tt(rsc1[:, 0:64], pseg,
                                  costs[:, jt * 64:(jt + 1) * 64], ALU.mult,
                                  deps=[mt, tok_cos, rope_R])
                        t2a = k.tt(rsc2[:, 0:32], bank[:, hh * 64 + 32:hh * 64 + 64],
                                   sints[:, jt * 64:jt * 64 + 32], ALU.mult,
                                   deps=[mt, tok_sin, rope_R])
                        t2b = k.tt(rsc2[:, 32:64], bank[:, hh * 64:hh * 64 + 32],
                                   sints[:, jt * 64 + 32:jt * 64 + 64], ALU.mult,
                                   deps=[mt, tok_sin, rope_R])
                        dst3 = dstall[:].rearrange("p (j f) -> p j f", f=D)[
                            :, jt, :]
                        t3 = k.tt(dst3[:, h * 128 + 64:h * 128 + 128],
                                  rsc1[:, 0:64], rsc2[:, 0:64], ALU.add,
                                  deps=[t1, t2a, t2b])
                        ets.append(t3)
                        rope_R = t3
                    et = ets[-1]
                bankR[bsl] = et
                kv_toks.append(et)

    vf = k.vflush([kv_toks[-1]])
    deps_all = kv_toks[-8:] + [vf]
    o1 = k.dma(_r3(k_tm), _s3(kall[:], D), deps=deps_all)
    o2 = k.dma(_r3(q_tm), _s3(qall[:], D), deps=deps_all)
    o3 = k.dma(_r3(v_tm), _s3(vall[:], D), deps=deps_all)
    return k.finish([o1, o2, o3])


def stage_a2(core, x1_fm_shard, inputs):
    b, r = divmod(core, 4)
    RD = 64
    inv = 1.0 / 10000.0 ** (np.arange(0, RD, 2, dtype=np.float32) / RD)
    t = np.arange(N, dtype=np.float32)
    ang = np.outer(t, inv)
    emb = np.concatenate([ang, ang], axis=1)
    cosf = np.cos(emb)
    sinf = np.sin(emb)
    sgn = np.ones(64, np.float32)
    sgn[:32] = -1.0
    sinf = sinf * sgn[None, :]
    sl = slice(r * TPC, (r + 1) * TPC)
    cos_c = cosf[sl].reshape(8, 128, 64).transpose(1, 0, 2).reshape(128, 8 * 64)
    sin_c = sinf[sl].reshape(8, 128, 64).transpose(1, 0, 2).reshape(128, 8 * 64)
    return {
        "x1fm": x1_fm_shard,
        "ln1_pc": _chunk_cols(inputs["L0_ffn_ln1_w"]),
        "kv_d_w": inputs["L0_ffn_kv_d_w"], "q_d_w": inputs["L0_ffn_q_d_w"],
        "k_u_w": inputs["L0_ffn_k_u_w"], "v_u_w": inputs["L0_ffn_v_u_w"],
        "q_u_w": inputs["L0_ffn_q_u_w"],
        "rope_k_w": inputs["L0_ffn_rope_k_w"],
        "rope_q_w": inputs["L0_ffn_rope_q_w"],
        "cost": np.ascontiguousarray(cos_c),
        "sint": np.ascontiguousarray(sin_c),
    }


def build_a3():
    k = KB("a3")
    ktm = k.dram("ktm", [N, 512], BF16)
    qtm = k.dram("qtm", [N, 512], BF16)
    vtm = k.dram("vtm", [N, 512], BF16)
    amask = k.dram("amask", [128, 4 * 512], F32)
    mmask = k.dram("mmask", [128, 4 * 512], BF16)
    o_fm = k.dram("o_fm", [512, N], BF16, kind="ExternalOutput")
    mscr = k.dram_internal("mscr", [32, 128], F32)

    kf = [k.sb("kf%d" % i, [128, N], BF16) for i in range(2)]
    qf = [k.sb("qf%d" % i, [128, N], BF16) for i in range(2)]
    vt = [k.sb("vt%d" % i, [128, 32 * 128], BF16) for i in range(2)]
    of = k.sb("of", [128, N], BF16)
    am = k.sb("am", [128, 4 * 512], F32)
    mm_ = k.sb("mm_", [128, 4 * 512], BF16)
    pt = k.sb("pt", [128, 2048], BF16)
    smsk = k.sb("smsk", [128, 512], F32)
    runm = k.sb("runm", [128, 1], F32)
    tmpc = k.sb("tmpc", [128, 8], F32)
    mneg = k.sb("mneg", [128, 32], F32)
    mnegT = k.sb("mnegT", [32, 128], F32)
    negrow = k.sb("negrow", [1, 512], F32)
    negrowb = k.sb("negrowb", [1, 512], BF16)
    linv = k.sb("linv", [1, 512], F32)
    linvb = k.sb("linvb", [1, 512], BF16)
    lbc = k.sb("lbc", [128, 512], F32)
    idb = k.sb("idb", [128, 128], BF16)
    idf = k.sb("idf", [128, 128], F32)
    ones1 = k.sb("ones1", [1, 128], BF16)
    onescol = k.sb("onescol", [128, 1], BF16)
    identb = k.dram("identb", [128, 128], BF16)
    identf = k.dram("identf", [128, 128], F32)

    st4 = k.ps("st4", [128, 2048], F32)
    pO = k.ps("pO", [128, 512], F32)
    pL = k.ps("pL", [128, 512], F32)
    pB = k.ps("pB", [128, 512], F32)

    tok_ones = k.memset(ones1[:], 1.0)
    tok_onesc = k.memset(onescol[:], 1.0)
    tok_id = k.dma(idb[:], identb[:, :])
    tok_idf = k.dma(idf[:], identf[:, :])
    tok_am = k.dma(am[:], amask[:, :])
    tok_mm = k.loadw(mm_[:], mmask[:, :].bitcast(F32)) if False else \
        k.dma(mm_[:], mmask[:, :])

    ld_tok = [None, None]
    use_tok = [None, None]

    def load_head(i, buf):
        c0 = i * 128
        t1 = k.dma_t(kf[buf][:], ktm[:, c0:c0 + 128], deps=[use_tok[buf]])
        t2 = k.dma_t(qf[buf][:], qtm[:, c0:c0 + 128], deps=[use_tok[buf]])
        t3 = k.dma(vt[buf][:].rearrange("p (j c) -> p j c", c=128),
                   vtm[:, c0:c0 + 128].rearrange("(j p) f -> p j f", p=128),
                   deps=[use_tok[buf]], q="gpsimd")
        return (t1, t2, t3)

    ld_tok[0] = load_head(0, 0)
    out_toks = []
    bankR = [None, None, None, None]
    oR = None
    lR = None
    bR = None
    ptR = [None, None]
    mskR = None
    redR = None
    of_use = None

    for i in range(4):
        buf = i % 2
        if i + 1 < 4:
            ld_tok[(i + 1) % 2] = load_head(i + 1, (i + 1) % 2)
        lt = ld_tok[buf]
        mn_toks = []
        fin_prev = None
        for qt in range(32):
            nkm = qt // 4 + 1
            ini = k.memset(tmpc[:], -3.0e38, deps=[fin_prev])
            rds = []
            for km in range(nkm):
                bsl = km % 4
                smt = k.mm(st4[:, bsl * 512:bsl * 512 + 512],
                           qf[buf][:, qt * 128:(qt + 1) * 128],
                           kf[buf][:, km * 512:(km + 1) * 512],
                           deps=[lt[0], lt[1], bankR[bsl]])
                if km == qt // 4:
                    pat = qt % 4
                    ms = k.tt(smsk[:], st4[:, bsl * 512:bsl * 512 + 512],
                              am[:, pat * 512:(pat + 1) * 512], ALU.add,
                              deps=[smt, tok_am, mskR])
                    rd = k.reduce(tmpc[:, km:km + 1], smsk[:], ALU.max,
                                  deps=[ms, ini])
                    mskR = rd
                else:
                    rd = k.reduce(tmpc[:, km:km + 1],
                                  st4[:, bsl * 512:bsl * 512 + 512],
                                  ALU.max, deps=[smt, ini])
                bankR[bsl] = rd
                rds.append(rd)
            mt_ = k.reduce(mneg[:, qt:qt + 1], tmpc[:], ALU.max, negate=True,
                           deps=rds)
            mn_toks.append(mt_)
            fin_prev = mt_
        ttm = k.transpose(pB[0:32, 0:128], mneg[:], idf[:],
                          deps=[tok_idf] + mn_toks[-4:])
        evm = k.copy(mnegT[:, :], pB[0:32, 0:128], deps=[ttm])
        vfm = k.vflush([evm])
        dmm = k.dma(mscr.ap(), mnegT[:, :], deps=[evm, vfm])
        ptW = [None] * 4
        for qm in range(8):
            nrow = k.dma(negrow[:].rearrange("o (a b) -> o a b", b=128),
                         mscr.ap()[qm * 4:(qm + 1) * 4, :].unsqueeze(0),
                         deps=[dmm, bR])
            nrb = k.copy(negrowb[:], negrow[:], deps=[nrow])
            pv = None
            lmm = None
            nkt = qm * 4 + 4
            equ = []
            for kt in range(nkt + 3):
                if kt < nkt:
                    half = kt % 2
                    ps_ = kt % 4
                    scol = half * 1024
                    mtA = k.mm(st4[:, scol:scol + 512],
                               kf[buf][:, kt * 128:(kt + 1) * 128],
                               qf[buf][:, qm * 512:(qm + 1) * 512],
                               deps=[lt[0], lt[1], ptR[half], bankR[half]],
                               inc=False, start=True, stop=False)
                    mtB = k.mm(st4[:, scol:scol + 512], ones1[:, :],
                               negrowb[:],
                               deps=[nrb], start=False, stop=True)
                    e = k.act(pt[:, ps_ * 512:ps_ * 512 + 512],
                              st4[:, scol:scol + 512], AF.Exp,
                              deps=[mtB, ptW[ps_]],
                              scale=float(1.0 / np.sqrt(128.0)))
                    ptR[half] = e
                    pdep = e
                    if kt >= qm * 4:
                        pat = kt - qm * 4
                        e2 = k.tt(pt[:, ps_ * 512:ps_ * 512 + 512],
                                  pt[:, ps_ * 512:ps_ * 512 + 512],
                                  mm_[:, pat * 512:(pat + 1) * 512], ALU.mult,
                                  deps=[e, tok_mm])
                        ptR[half] = e2
                        pdep = e2
                    equ.append((kt, ps_, pdep))
                if equ and (kt >= nkt or len(equ) > 2):
                    pk, pps, pd = equ.pop(0)
                    pgate = equ[0][2] if equ else k.sflush([pd])
                    lmm = k.mm(pL[0:1, :], onescol[:],
                               pt[:, pps * 512:pps * 512 + 512],
                               deps=[pd, pgate, tok_onesc,
                                     lR if pk == 0 else None],
                               start=(pk == 0), stop=(pk == nkt - 1),
                               inc=True)
                    pv = k.mm(pO[:, :], vt[buf][:, pk * 128:(pk + 1) * 128],
                              pt[:, pps * 512:pps * 512 + 512],
                              deps=[oR if pk == 0 else None],
                              start=(pk == 0), stop=(pk == nkt - 1),
                              inc=True)
                    ptW[pps] = pv
            t1 = k.recip(linv[:], pL[0:1, :], deps=[lmm, lR])
            t2 = k.copy(linvb[:], linv[:], deps=[t1])
            t3 = k.mm(pB[:, :], ones1[:, :], linvb[:], deps=[t2, tok_ones, bR])
            t4 = k.copy(lbc[:, :], pB[:, :], deps=[t3])
            t5 = k.tt(of[:, qm * 512:(qm + 1) * 512], pO[:, :], lbc[:, :],
                      ALU.mult, deps=[t4, pv, of_use])
            oR = t5
            lR = t5
            bR = t4
        vfo = k.vflush([oR])
        ot = k.dma(o_fm[i * 128:(i + 1) * 128, :], of[:], deps=[oR, vfo])
        out_toks.append(ot)
        of_use = ot
        use_tok[buf] = oR
        bankR[0] = ptR[0]
        bankR[1] = ptR[1]
    return k.finish(out_toks)


def stage_a3(core, ktm_my, qtm_my, vtm_my):
    am = np.zeros((128, 4, 512), np.float32)
    mmk = np.zeros((128, 4, 512), np.float32)
    row = np.arange(128)
    col = np.arange(512)
    for p in range(4):
        qg = p * 128 + row
        vis = (col[None, :] <= qg[:, None])
        am[:, p, :] = np.where(vis, 0.0, -1e30)
    for p in range(4):
        kg = p * 128 + row
        vis = (kg[:, None] <= col[None, :])
        mmk[:, p, :] = np.where(vis, 1.0, 0.0)
    import ml_dtypes
    return {
        "ktm": ktm_my, "qtm": qtm_my, "vtm": vtm_my,
        "amask": np.ascontiguousarray(am.reshape(128, 4 * 512)),
        "mmask": np.ascontiguousarray(mmk.reshape(128, 4 * 512)).astype(
            ml_dtypes.bfloat16),
        "identb": _ident_bf16(),
        "identf": np.eye(128, dtype=np.float32),
    }


def build_a4():
    k = KB("a4")
    o_in = k.dram("o_in", [D, TPC], BF16)
    x1fm = k.dram("x1fm", [D, TPC], F32)
    o_w = k.dram("o_w", [D, D], F32)
    ln2_pc = k.dram("ln2_pc", [128, NCH], F32)
    router_w = k.dram("router_w", [D, 7], F32)
    router_b = k.dram("router_b", [1, 7], F32)
    sh_gate = k.dram("sh_gate", [D, 1024], F32)
    sh_up = k.dram("sh_up", [D, 1024], F32)
    sh_down = k.dram("sh_down", [1024, D], F32)
    r_gate = k.dram("r_gate", [7, D, 1024], F32)
    r_up = k.dram("r_up", [7, D, 1024], F32)
    r_down = k.dram("r_down", [7, 1024, D], F32)
    identf = k.dram("identf", [128, 128], F32)
    x2fm = k.dram("x2fm", [D, TPC], F32, kind="ExternalOutput")
    wscr = k.dram_internal("wscr", [7, TPC], F32)

    h_f = k.sb("h_f", [128, NCH * TPC], F32)
    oin = k.sb("oin", [128, NCH * TPC], BF16)
    xn2 = oin
    xnf = [k.sb("xnf%d" % i, [128, 512], F32) for i in range(2)]
    wpool = [k.sb("wpool%d" % i, [128, 16 * 512], BF16) for i in range(3)]
    rws = k.sb("rws", [128, NCH * 8], F32)
    rbs = k.sb("rbs", [1, 7], F32)
    idf = k.sb("idf", [128, 128], F32)
    pr = k.sb("pr", [128, 8], F32)
    m1n = k.sb("m1n", [128, 1], F32)
    spm = k.sb("spm", [128, 8], F32)
    mask1 = k.sb("mask1", [128, 8], F32)
    m2 = k.sb("m2", [128, 1], F32)
    selm = k.sb("selm", [128, 8], F32)
    dn = k.sb("dn", [128, 1], F32)
    wtt = k.sb("wtt", [128, 8], F32)
    wrow = k.sb("wrow", [1, 512], F32)
    wrowb = k.sb("wrowb", [1, 512], BF16)
    wbc = k.sb("wbc", [128, 512], BF16)
    gs = k.sb("gs", [128, 8 * 512], BF16)
    lfs = gs[0:8, 0:2048].bitcast(F32)
    hb = k.sb("hb", [128, 8 * 512], BF16)
    us = k.sb("us", [128, 512], BF16)
    acc = k.sb("acc", [128, NCH * 512], F32)
    x1ch = k.sb("x1ch", [128, 512], F32)
    ones1 = k.sb("ones1", [1, 128], BF16)
    onescol = k.sb("onescol", [128, 1], BF16)
    epst = k.sb("epst", [1, 1], F32)
    sqs = k.sb("sqs", [128, 512], BF16)
    preb = k.sb("preb", [128, 512], BF16)
    rowbb = k.sb("rowbb", [1, 512], BF16)
    tmp = hb[:, 0:1024].bitcast(F32)
    tmpr0 = hb[0:1, 0:1024].bitcast(F32)
    lbc = hb[:, 1024:2048].bitcast(F32)
    rowa = hb[0:1, 2048:3072].bitcast(F32)
    rowv = hb[0:1, 3072:4096].bitcast(F32)
    rowm = wrow
    rowab = wrowb

    st4 = k.ps("st4", [128, 2048], F32)
    bc = k.ps("bc", [128, 512], F32)
    trp = k.ps("trp", [128, 128], F32)
    srow = k.ps("srow", [1, 1024], F32)

    tok_ones = k.memset(ones1[:], 1.0)
    tok_onesc = k.memset(onescol[:], 1.0)
    tok_eps = k.memset(epst[:], 1e-5)
    tok_idf = k.dma(idf[:], identf[:, :])
    tok_rw = k.dma(rws[:].rearrange("p (c e) -> p c e", e=8)[:, :, 0:7],
                   _r3(router_w))
    tok_rb = k.dma(rbs[:], router_b[:, :])
    tok_oin = k.dma(_s3(oin[:], TPC), _r3(o_in))
    tok_x1 = None

    wuse = [None, None, None]
    slot = [0]

    def wload(src_ap, inner, extra=()):
        s = slot[0]
        slot[0] = (s + 1) % 3
        t = k.loadw(wpool[s][:, 0:(src_ap.shape[0] // 128) * inner].rearrange(
            "p (c t) -> p c t", t=inner), _r3(src_ap),
            deps=[wuse[s]] + list(extra))
        return s, t

    bankR = [None, None, None, None]
    h_toks = []
    x1use = None
    for blk in range(4):
        s1_, lt = wload(o_w[:, blk * 512:(blk + 1) * 512], 512)
        for j in range(4):
            fch = blk * 4 + j
            for th in range(2):
                bsl = (j * 2 + th) % 4
                bank = st4[:, bsl * 512:bsl * 512 + 512]
                pairs = [(wpool[s1_][:, kc * 512 + j * 128: kc * 512 + j * 128 + 128],
                          oin[:, kc * TPC + th * 512: kc * TPC + th * 512 + 512])
                         for kc in range(NCH)]
                xt = k.dma(x1ch[:], x1fm[fch * 128:(fch + 1) * 128,
                                          th * 512:th * 512 + 512],
                           deps=[x1use])
                mt = k.mm_group(bank, pairs, deps=[lt, tok_oin, bankR[bsl]] + k.p.fence_toks())
                et = k.tt(h_f[:, fch * TPC + th * 512: fch * TPC + th * 512 + 512],
                          x1ch[:, :], bank, ALU.add,
                          deps=[mt, xt])
                bankR[bsl] = et
                wuse[s1_] = mt
                x1use = et
                h_toks.append(et)

    lnw = k.sb("lnw4", [128, NCH], F32)
    tok_ln = k.dma(lnw[:], ln2_pc[:, :])
    s2_tok = [None, None]
    sq_prev = None
    for th in range(2):
        s2 = srow[0:1, th * 512:(th + 1) * 512]
        for kc in range(NCH):
            sq = k.act(sqs[:], h_f[:, kc * TPC + th * 512: kc * TPC + th * 512 + 512],
                       AF.Square, deps=[h_toks[-1], s2_tok[th] if kc else sq_prev])
            s2_tok[th] = k.mm(s2, onescol[:], sqs[:], deps=[sq, tok_onesc],
                              start=(kc == 0), stop=(kc == NCH - 1), inc=True)
            sq_prev = s2_tok[th]
    xn_toks = []
    mt_l = [None, None]
    xn_last = None
    xnf_use = [None, None]
    for th in range(2):
        s2 = srow[0:1, th * 512:(th + 1) * 512]
        t_v = k.ts(rowv, s2, 1.0 / D, ALU.mult, deps=[s2_tok[th]])
        t_sd = k.act(rowv, rowv, AF.Sqrt, deps=[t_v, tok_eps],
                     bias=epst[0:1, 0:1], scale=1.0)
        t_r = k.recip(rowv, rowv, deps=[t_sd])
        t_ab = k.copy(rowab[0:1, 0:512], rowv, deps=[t_r])
        tb = k.mm(bc[:, :], ones1[:, :], rowab[0:1, 0:512], deps=[t_ab, tok_ones,
                                                         xn_last])
        eb = k.copy(lbc, bc[:, :], deps=[tb])
        for kc in range(NCH):
            xb2 = kc % 2
            xf = k.stt(xnf[xb2][:, :],
                       h_f[:, kc * TPC + th * 512: kc * TPC + th * 512 + 512],
                       lnw[:, kc:kc + 1], lbc, ALU.mult, ALU.mult,
                       deps=[eb, tok_ln, xnf_use[xb2]])
            xc = k.copy(xn2[:, kc * TPC + th * 512: kc * TPC + th * 512 + 512],
                        xnf[xb2][:, :], deps=[xf])
            mt_l[th] = k.mm(st4[0:7, 1024 + th * 512:1024 + th * 512 + 512],
                            rws[:, kc * 8:kc * 8 + 7],
                            xnf[xb2][:, :],
                            deps=[xf, tok_rw, bankR[2 + th] if kc == 0 else None],
                            start=(kc == 0), stop=(kc == NCH - 1), inc=True)
            xnf_use[xb2] = mt_l[th]
            xn_toks.append(xc)
            xn_last = xc
    bankR[2] = mt_l[0]
    bankR[3] = mt_l[1]

    ev_l = k.copy(gs[0:7, 0:2048].bitcast(F32), st4[0:7, 1024:2048], deps=[mt_l[0], mt_l[1]])
    rbb = k.sb("rbb", [8, 1], F32)
    tok_rbT = k.dma(rbb[0:7, 0:1], router_b.rearrange("a b -> b a"), deps=[])
    badd = k.ts(gs[0:7, 0:2048].bitcast(F32), gs[0:7, 0:2048].bitcast(F32),
                rbb[0:7, 0:1], ALU.add, deps=[ev_l, tok_rbT])
    wt_toks = []
    tr_last = None
    t8_prev = None
    for jt in range(8):
        tt_l = k.transpose(trp[:, 0:7],
                           gs[0:7, jt * 256:jt * 256 + 256].bitcast(F32),
                           idf[0:7, 0:7], deps=[badd, tok_idf, tr_last])
        t0 = k.act(pr[:, 0:7], trp[:, 0:7], AF.Sigmoid, deps=[tt_l, t8_prev])
        t2 = k.reduce(m1n[:], pr[:, 0:7], ALU.max, negate=True, deps=[t0])
        t3 = k.ts(spm[:, 0:7], pr[:, 0:7], m1n[:, 0:1], ALU.add, deps=[t2, t0])
        t4 = k.ts(mask1[:, 0:7], spm[:, 0:7], 0.0, ALU.is_ge, deps=[t3])
        t5 = k.stt(spm[:, 0:7], mask1[:, 0:7], 1e30, pr[:, 0:7], ALU.mult,
                   ALU.subtract, deps=[t4])
        t6 = k.reduce(m2[:], spm[:, 0:7], ALU.min, negate=True, deps=[t5])
        t7 = k.ts(selm[:, 0:7], pr[:, 0:7], m2[:, 0:1], ALU.is_ge,
                  deps=[t6, t5])
        td = k.tt(dn[:], m2[:], m1n[:], ALU.subtract, deps=[t6, t2])
        tdr = k.recip(dn[:], dn[:], deps=[td])
        tds = k.ts(dn[:], dn[:], 0.125, ALU.mult, deps=[tdr])
        t8a = k.tt(wtt[:, 0:7], pr[:, 0:7], selm[:, 0:7], ALU.mult,
                   deps=[t7, t0])
        t8 = k.ts(wtt[:, 0:7], wtt[:, 0:7], dn[:, 0:1], ALU.mult,
                  deps=[t8a, tds])
        t8_prev = t8
        tt_ = k.transpose(trp[0:7, :], wtt[:, 0:7] if False else wtt[:, 0:8],
                          idf[:], deps=[t8, tok_idf, tr_last]) if False else \
            k.transpose(trp[0:8, :], wtt[:, 0:8], idf[:],
                        deps=[t8, tok_idf, tr_last])
        ev = k.copy(hb[0:8, 0:256].bitcast(F32), trp[0:8, 0:128], deps=[tt_])
        tr_last = ev
        vfw = k.vflush([ev])
        wt_toks.append(k.dma(wscr.ap()[:, jt * 128:(jt + 1) * 128],
                             tmp[0:7, 0:128], deps=[ev, vfw]))

    out_toks = []
    lnstat_prev = [None, None]
    outprev = [None] * NCH
    wrow_prev = [None]
    bankR4 = [bankR[0], bankR[1], None, None]
    wbcR = None
    for th in range(2):
        tof = th * 512
        acc_tok = [None] * NCH
        for kc in range(NCH):
            xck = k.dma(x1ch[:], x1fm[kc * 128:(kc + 1) * 128,
                                       tof:tof + 512],
                        deps=[acc_tok[kc - 1] if kc else
                              (out_toks[-1] if out_toks else x1use)])
            at = k.tt(acc[:, kc * 512:(kc + 1) * 512],
                      x1ch[:, :],
                      h_f[:, kc * TPC + tof: kc * TPC + tof + 512], ALU.add,
                      deps=[xck, outprev[kc]])
            acc_tok[kc] = at
        rinR = None
        h_prev = [None] * 8
        for e in range(8):
            shared = (e == 0)
            if not shared:
                t_wr = k.dma(wrow[:], wscr.ap()[e - 1:e, tof:tof + 512],
                             deps=wt_toks + [wrow_prev[0]])
                t_wrb = k.copy(wrowb[:], wrow[:], deps=[t_wr, wrow_prev[0]])
                tb = k.mm(bc[:, :], ones1[:, :], wrowb[0:1, 0:512],
                          deps=[t_wrb, wbcR])
                eb2 = k.copy(wbc[:, 0:512], bc[:, :], deps=[tb, rinR])
                wrow_prev[0] = tb
                wbcR = eb2
            g_toks = [None] * 8
            for hh in range(2):
                wa = sh_gate if shared else r_gate[e - 1]
                s1_, lt = wload(wa[:, hh * 512:(hh + 1) * 512], 512)
                for hc in range(4):
                    hix = hh * 4 + hc
                    bsl = hc % 2
                    bank = st4[:, bsl * 512: bsl * 512 + 512]
                    pairs = [(wpool[s1_][:, kc * 512 + hc * 128:
                                       kc * 512 + hc * 128 + 128],
                              xn2[:, kc * TPC + tof: kc * TPC + tof + 512])
                             for kc in range(NCH)]
                    mt = k.mm_group(bank, pairs,
                                    deps=[lt, bankR4[bsl]] + k.p.fence_toks())
                    gt = k.act(gs[:, hix * 512: hix * 512 + 512],
                               bank, AF.Silu, deps=[mt, h_prev[hix],
                                                    wt_toks[-1]])
                    g_toks[hix] = gt
                    bankR4[bsl] = gt
                    wuse[s1_] = mt
            h_toks2 = [None] * 8
            for hh in range(2):
                wa = sh_up if shared else r_up[e - 1]
                s1_, lt = wload(wa[:, hh * 512:(hh + 1) * 512], 512)
                for hc in range(4):
                    hix = hh * 4 + hc
                    bsl = 2 + hc % 2
                    bank = st4[:, bsl * 512: bsl * 512 + 512]
                    pairs = [(wpool[s1_][:, kc * 512 + hc * 128:
                                       kc * 512 + hc * 128 + 128],
                              xn2[:, kc * TPC + tof: kc * TPC + tof + 512])
                             for kc in range(NCH)]
                    mt = k.mm_group(bank, pairs,
                                    deps=[lt, bankR4[bsl]] + k.p.fence_toks())
                    if shared:
                        ut = k.ts(us[:], bank, 0.125, ALU.mult, deps=[mt])
                    else:
                        ut = k.tt(us[:], bank, wbc[:, 0:512], ALU.mult,
                                  deps=[mt, wbcR])
                    ht = k.tt(hb[:, hix * 512: hix * 512 + 512],
                              us[:], gs[:, hix * 512: hix * 512 + 512],
                              ALU.mult, deps=[ut, g_toks[hix],
                                              wt_toks[-1]])
                    h_toks2[hix] = ht
                    bankR4[bsl] = ht
                    wuse[s1_] = mt
                    rinR = mt
            h_prev = h_toks2
            for fh in range(2):
                wa = sh_down if shared else r_down[e - 1]
                s1_, lt = wload(wa[:, fh * 1024:(fh + 1) * 1024], 1024)
                for fc in range(8):
                    fch = fh * 8 + fc
                    bsl = fc % 2
                    bank = st4[:, bsl * 512: bsl * 512 + 512]
                    pairs = [(wpool[s1_][:, hc * 1024 + fc * 128:
                                       hc * 1024 + fc * 128 + 128],
                              hb[:, hc * 512:(hc + 1) * 512])
                             for hc in range(8)]
                    mt = k.mm_group(bank, pairs,
                                    deps=[lt, bankR4[bsl]] + k.p.fence_toks())
                    at = k.tt(acc[:, fch * 512: fch * 512 + 512],
                              acc[:, fch * 512: fch * 512 + 512],
                              bank, ALU.add, deps=[mt, acc_tok[fch]])
                    acc_tok[fch] = at
                    bankR4[bsl] = at
                    wuse[s1_] = mt
        s1 = st4[0:1, 1024:1536]
        s2 = st4[0:1, 1536:2048]
        t_s1 = lnstat_prev[0]
        t_s2 = lnstat_prev[1]
        sq_tok = None
        pb_tok = None
        for kc in range(NCH):
            seg = acc[:, kc * 512:(kc + 1) * 512]
            pb_tok = k.copy(preb[:], seg, deps=[acc_tok[kc], t_s1, bankR[2]])
            sq_tok = k.act(sqs[:], seg, AF.Square, deps=[acc_tok[kc], t_s2,
                                                         bankR[3]])
            t_s1 = k.mm(s1, onescol[:], preb[:],
                        deps=[pb_tok, tok_onesc], start=(kc == 0),
                        stop=(kc == NCH - 1), inc=True)
            t_s2 = k.mm(s2, onescol[:], sqs[:],
                        deps=[sq_tok], start=(kc == 0),
                        stop=(kc == NCH - 1), inc=True)
        lnstat_prev = [t_s1, t_s2]
        bankR[2] = t_s1
        bankR[3] = t_s2
        t_m = k.ts(rowm[0:1, 0:512], s1, 1.0 / D, ALU.mult,
                   deps=[t_s1, wrow_prev[0]])
        t_v = k.ts(rowv, s2, 1.0 / D, ALU.mult, deps=[t_s2])
        t_m2 = k.tt(tmpr0, rowm[0:1, 0:512], rowm[0:1, 0:512], ALU.mult, deps=[t_m])
        t_v2 = k.tt(rowv, rowv, tmpr0, ALU.subtract,
                    deps=[t_v, t_m2])
        t_sd = k.act(tmpr0, rowv, AF.Sqrt, deps=[t_v2, tok_eps],
                     bias=epst[0:1, 0:1], scale=1.0)
        t_r = k.recip(rowa, tmpr0, deps=[t_sd])
        t_ab = k.copy(rowab[0:1, 0:512], rowa, deps=[t_r])
        t_bb = k.tt(tmpr0, rowm[0:1, 0:512], rowa, ALU.mult, deps=[t_r, t_ab])
        t_bn = k.ts(tmpr0, tmpr0, -1.0, ALU.mult, deps=[t_bb])
        t_bbb = k.copy(rowbb[:], tmpr0, deps=[t_bn])
        tba = k.mm(bc[:, :], ones1[:, :], rowab[0:1, 0:512], deps=[t_ab, tok_ones,
                                                          wbcR])
        ebA = k.copy(lbc, bc[:, :], deps=[tba])
        tbb = k.mm(bc[:, :], ones1[:, :], rowbb[:], deps=[t_bbb, ebA])
        tm2 = None
        tm2s = []
        for kc in range(NCH):
            seg = acc[:, kc * 512:(kc + 1) * 512]
            tm1 = k.tt(tmp, seg, lbc, ALU.mult,
                       deps=[tba, ebA, sq_tok, pb_tok, tm2])
            tm2 = k.tt(seg, tmp, bc[:, :], ALU.add, deps=[tm1, tbb])
            tm2s.append(tm2)
        vf = k.vflush([tm2s[-1]])
        for kc in range(NCH):
            ot = k.dma(x2fm[kc * 128:(kc + 1) * 128, tof:tof + 512],
                       acc[:, kc * 512:(kc + 1) * 512], deps=[tm2s[kc], vf])
            outprev[kc] = ot
            out_toks.append(ot)
        wbcR = tbb
        wrow_prev[0] = t_bbb
    return k.finish(out_toks)


def stage_a4(core, o_fm_slice, x1_fm_shard, inputs):
    return {
        "o_in": o_fm_slice, "x1fm": x1_fm_shard,
        "o_w": inputs["L0_ffn_o_w"],
        "ln2_pc": _chunk_cols(inputs["L0_ffn_ln2_w"]),
        "router_w": inputs["L0_ffn_router_w"],
        "router_b": _row(inputs["L0_ffn_router_b"]),
        "sh_gate": inputs["L0_ffn_sh_gate"], "sh_up": inputs["L0_ffn_sh_up"],
        "sh_down": inputs["L0_ffn_sh_down"],
        "r_gate": inputs["L0_ffn_r_gate"], "r_up": inputs["L0_ffn_r_up"],
        "r_down": inputs["L0_ffn_r_down"],
        "identf": np.eye(128, dtype=np.float32),
    }


_NC_CACHE = {}


def _get_nc(name):
    if name not in _NC_CACHE:
        _NC_CACHE[name] = globals()["build_" + name]()
    return _NC_CACHE[name]


def kernel(**inputs):
    cores = list(range(NCORES))
    x0 = np.asarray(inputs["L0_attn_x"], dtype=np.float32)

    def run(name, maps):
        return run_bass_kernel_spmd(_get_nc(name), maps, core_ids=cores)

    def shard_fm(full):
        out = []
        for c in range(NCORES):
            b, r = divmod(c, 4)
            out.append(_fm(full[b, r * TPC:(r + 1) * TPC]))
        return out

    r1 = run("a1", [stage_a1(c, shard_fm(x0)[c], inputs) for c in cores])
    x1fm = [r1.results[c]["x1fm"] for c in cores]

    r2 = run("a2", [stage_a2(c, x1fm[c], inputs) for c in cores])

    maps3 = []
    kfull = {}
    qfull = {}
    vfull = {}
    for b in range(2):
        kfull[b] = np.concatenate(
            [r2.results[4 * b + j]["k_tm"] for j in range(4)], axis=0)
        qfull[b] = np.concatenate(
            [r2.results[4 * b + j]["q_tm"] for j in range(4)], axis=0)
        vfull[b] = np.concatenate(
            [r2.results[4 * b + j]["v_tm"] for j in range(4)], axis=0)
    for c in cores:
        b, r = divmod(c, 4)
        hs = slice(r * 512, (r + 1) * 512)
        maps3.append(stage_a3(
            c, np.ascontiguousarray(kfull[b][:, hs]),
            np.ascontiguousarray(qfull[b][:, hs]),
            np.ascontiguousarray(vfull[b][:, hs])))
    r3 = run("a3", maps3)

    maps4 = []
    for c in cores:
        b, r = divmod(c, 4)
        o_full = np.concatenate(
            [r3.results[4 * b + j]["o_fm"] for j in range(4)], axis=0)
        o_slice = np.ascontiguousarray(o_full[:, r * TPC:(r + 1) * TPC])
        maps4.append(stage_a4(c, o_slice, x1fm[c], inputs))
    r4 = run("a4", maps4)
    x2fm = [r4.results[c]["x2fm"] for c in cores]

    r5 = run("a5", [stage_a5(c, x2fm[c], inputs) for c in cores])

    maps6 = []
    for c in cores:
        b, r = divmod(c, 4)
        ktm_full = np.concatenate(
            [r5.results[4 * b + j]["k_tm"] for j in range(4)], axis=0)
        vtm_full = np.concatenate(
            [r5.results[4 * b + j]["v_tm"] for j in range(4)], axis=0)
        maps6.append(stage_a6(c, r5.results[c]["q_fm"], ktm_full, vtm_full,
                              x2fm[c], inputs))
    r6 = run("a6", maps6)
    x3fm = [r6.results[c]["x3fm"] for c in cores]

    r7 = run("a7", [stage_a7(c, x3fm[c], inputs) for c in cores])

    out = np.empty((B, N, D), dtype=np.float32)
    for c in cores:
        b, r = divmod(c, 4)
        out[b, r * TPC:(r + 1) * TPC] = r7.results[c]["yout"]
    return out
